# version 13 (final)
# baseline (speedup 1.0000x reference)
import math
import numpy as np
from contextlib import ExitStack

import concourse.bass as bass
import concourse.mybir as mybir
from concourse.bass_utils import run_bass_kernel_spmd

F32 = mybir.dt.float32
BF16 = mybir.dt.bfloat16
I32 = mybir.dt.int32
AF = mybir.ActivationFunctionType
ALU = mybir.AluOpType
AX = mybir.AxisListType

P = 128
T = 1024
D = 2048
C = D // P
NH = 16
HD = 128
S = 2048
SOFT_SHIFT = -12.0
RG_PAIRS = [[0, 1], [2, 3], [4, 5], [6, 7]]


def _blk(a):
    K, N = a.shape
    kc, nb = K // P, N // P
    out = a.reshape(kc, P, nb, P).transpose(2, 1, 0, 3)
    return np.ascontiguousarray(out.reshape(nb * P, K))


def _pm(a):
    K, N = a.shape
    kc = K // P
    return np.ascontiguousarray(a.reshape(kc, P, N).transpose(1, 0, 2).reshape(P, kc * N))


def _l0_tables(h):
    pos = np.arange(h * T, (h + 1) * T, dtype=np.float64)
    d = np.arange(0, D, 2, dtype=np.float64) / D
    inv = 1.0 / (10000.0 ** d)
    ang = pos[None, :] * inv[:, None]
    return np.cos(ang).astype(np.float32), np.sin(ang).astype(np.float32)



ENGS = ("pe", "sv", "ss", "gp", "sy")
SEM_NAMES = ("ld", "lp", "st", "pe", "ss", "sv", "sg", "cc0", "cc1", "cc2")


class Sched:

    def __init__(self, nc, ctx):
        self.nc = nc
        self.ctx = ctx
        self.q = {e: [] for e in ENGS}
        self.sems = {}
        self.cnt = {}
        for name in SEM_NAMES:
            self.sems[name] = ctx.enter_context(nc.semaphore(name))
            self.cnt[name] = 0
        self._auto = True
        self.PS = [ctx.enter_context(nc.psum_tensor(f"psb{i}", [P, 512], F32))
                   for i in range(7)]
        self.PT = ctx.enter_context(nc.psum_tensor("pst", [P, 1024], BF16))
        self.last_pt_evac = None

    def sb(self, name, shape, dt, scope=None):
        tgt = scope or getattr(self, "default_scope", None) or self.ctx
        return tgt.enter_context(self.nc.sbuf_tensor(name, shape, dt))

    def barrier(self):
        snap = {k: self.cnt[k] for k in ("pe", "ss", "sv", "sg")}
        dsnap = [(nm, self.cnt[nm]) for nm in self.cnt
                 if nm not in ("pe", "ss", "sv", "sg") and self.cnt[nm] > 0]
        for eng, own in (("pe", "pe"), ("sv", "sv"), ("ss", "ss"), ("gp", "sg"), ("sy", None)):
            waits = [(k, v) for k, v in snap.items() if k != own and v > 0] + dsnap
            self.step(eng, lambda e: e.nop(), waits)

    def sem(self, name):
        if name not in self.sems:
            self.sems[name] = self.ctx.enter_context(self.nc.semaphore(name))
            self.cnt[name] = 0
        return name

    def bump(self, sem, n=1):
        self.cnt[sem] += n
        return self.cnt[sem]

    def step(self, eng, f, waits=(), inc=None):
        waits = [(sm, v) for (sm, v) in waits if v is not None and v > 0]
        for sm, val in waits:
            assert val <= self.cnt[sm], f"wait on future token {sm}:{val}>{self.cnt[sm]}"
        tok = None
        if inc is not None:
            tok = self.bump(inc[0], inc[1])
        sems = self.sems

        def run(e, f=f, waits=waits, inc=inc):
            for sm, val in waits:
                e.wait_ge(sems[sm], val)
            last = f(e)
            if inc is not None:
                assert last is not None
                last.then_inc(sems[inc[0]], inc[1])
        self.q[eng].append(run)
        return tok

    def dma(self, eng, out, in_, waits=(), sem="ld", **kw):
        self.sem(sem)
        return self.step(eng, lambda e: e.dma_start(out, in_, **kw), waits, (sem, 16))

    def emit(self, block):
        def mk(e):
            def body(eng):
                for f in self.q[e]:
                    f(eng)
            return body
        block.tensor(mk("pe"))
        block.vector(mk("sv"))
        block.scalar(mk("ss"))
        block.gpsimd(mk("gp"))
        block.sync(mk("sy"))


def fm_proj_chunk(S_, wbuf, rhs, rhs_off, pa, pb, waits, kc=C):
    def f(t):
        mm = None
        for c in range(kc):
            for s2, pp in ((0, pa), (1, pb)):
                mm = t.matmul(pp[:, 0:512], wbuf[:, c * P:(c + 1) * P],
                              rhs[:, rhs_off + c * T + s2 * 512: rhs_off + c * T + (s2 + 1) * 512],
                              start=(c == 0), stop=(c == kc - 1))
        return mm
    return S_.step("pe", f, waits, ("pe", 1))


def transpose16(S_, src, src_off, dst_view, waits0, ident, nt=16):
    PT = S_.PT
    tokv = S_.last_pt_evac
    nr = (nt + 7) // 8
    for r in range(nr):
        lo, hi = r * 8, min(nt, r * 8 + 8)
        def ft(t, lo=lo, hi=hi):
            mm = None
            for i in range(lo, hi):
                mm = t.transpose(PT[:, (i - lo) * P:(i - lo + 1) * P],
                                 src[:, src_off + i * P: src_off + (i + 1) * P], ident[:])
            return mm
        w = list(waits0) + ([("sv", tokv)] if tokv else [])
        waits0 = []
        tokp = S_.step("pe", ft, w, ("pe", 1))
        tokv = S_.step("sv", lambda v, r=r, n=hi - lo: v.tensor_copy(
            dst_view(r), PT[:, 0:n * P]), [("pe", tokp)], ("sv", 1))
    S_.last_pt_evac = tokv
    return tokv



def emit_l0(S_, io, dbg):
    PS = S_.PS
    sb = S_.sb

    XN = sb("XN", [P, C * T], BF16)
    ident = sb("ident", [P, P], BF16)
    biasT = sb("biasT", [P, C], F32)
    gateT = sb("gateT", [1, NH], F32)
    gsig = sb("gsig", [P, NH], F32)
    ones_b = sb("ones_b", [P, 1], BF16)
    ones_f = sb("ones_f", [1, P], F32)
    cosb = sb("cosb", [P, 2 * T], F32)
    sinb = sb("sinb", [P, 2 * T], F32)
    WB = [sb("wbA", [P, C * P], BF16), sb("wbB", [P, C * P], BF16)]
    scrA = sb("scrA", [P, T], F32)
    scrB = sb("scrB", [P, T], F32)
    scrC = sb("scrC", [P, T], F32)
    rowA = sb("rowA", [1, T], F32)
    rowB = sb("rowB", [1, T], F32)
    bcM = sb("bcM", [P, T], F32)
    bcR = sb("bcR", [P, T], F32)
    VCH = [sb("VCHa", [P, T], BF16), sb("VCHb", [P, T], BF16)]
    thr = sb("thr", [P, 1], F32)
    epsT = sb("epsT", [1, 1], F32)
    shiftT = sb("shiftT", [P, 1], F32)
    nthr = sb("nthr", [P, 1], F32)
    NTB = sb("NTB", [P, 2], BF16)
    KR = [sb("KR0", [P, T], BF16), sb("KR1", [P, T], BF16)]

    xd = io["x_in"].rearrange("(c p) t -> p c t", p=P)
    wq3 = io["wq"].rearrange("(j p) k -> j p k", p=P)
    wk3 = io["wk"].rearrange("(j p) k -> j p k", p=P)
    wv3 = io["wv"].rearrange("(j p) k -> j p k", p=P)
    wo3 = io["wo"].rearrange("(j p) k -> j p k", p=P)

    t_ones = S_.step("gp", lambda g: g.memset(ones_b[:], 1.0), (), ("sg", 1))
    t_eps = S_.step("gp", lambda g: g.memset(epsT[:], 1e-5), (), ("sg", 1))
    t_shift = S_.step("gp", lambda g: g.memset(shiftT[:], SOFT_SHIFT), (), ("sg", 1))
    t_onef = S_.step("gp", lambda g: g.memset(ones_f[:], 1.0), (), ("sg", 1))
    S_.dma("gp", ident[:], io["ident_in"], sem="once")
    S_.dma("gp", XN[:].rearrange("p (c t) -> p c t", t=T), xd, sem="once")
    S_.dma("sy", biasT[:], io["bias_sw"], sem="once")
    S_.dma("sy", gateT[:], io["gate_in"], sem="once")

    scope1 = ExitStack()
    X = sb("X", [P, C * T], F32, scope1)
    SQ2 = sb("SQ2", [P, 2 * T], BF16, scope1)
    WFB = [sb("WF32a", [P, D], F32, scope1), sb("WF32b", [P, D], F32, scope1)]
    TERN = sb("TERN", [P, D], BF16, scope1)
    FLT = sb("FLT", [P, D], BF16, scope1)
    t1b = SQ2
    FSC = sb("FSC", [P, 2 * T], F32, scope1)

    t_once = S_.dma("sy", X[:].rearrange("p (c t) -> p c t", t=T), xd, sem="once")
    t_x = t_once; t_id = t_once; t_xb = t_once; t_bias = t_once; t_gate = t_once

    sq_tok = {}
    mm_tok = {}
    for c in range(C):
        wsq = [("once", t_once)] if c == 0 else []
        if c >= 2:
            wsq.append(("pe", mm_tok[c - 2]))
        sq_tok[c] = S_.step("ss", lambda s, c=c: s.activation(
            SQ2[:, (c % 2) * T:(c % 2 + 1) * T], X[:, c * T:(c + 1) * T], AF.Square),
            wsq, ("ss", 1))
        def f_mm(t, c=c):
            mm = None
            for s2, pp in ((0, PS[0]), (1, PS[1])):
                t.matmul(pp[0:1, 0:512], ones_b[:],
                         XN[:, c * T + s2 * 512: c * T + (s2 + 1) * 512],
                         start=(c == 0), stop=(c == C - 1))
            for s2, pp in ((0, PS[2]), (1, PS[3])):
                mm = t.matmul(pp[0:1, 0:512], ones_b[:],
                              SQ2[:, (c % 2) * T + s2 * 512: (c % 2) * T + (s2 + 1) * 512],
                              start=(c == 0), stop=(c == C - 1))
            return mm
        wmm = [("ss", sq_tok[c])]
        if c == 0:
            wmm += [("once", t_once), ("sg", t_ones)]
        mm_tok[c] = S_.step("pe", f_mm, wmm, ("pe", 1))
    t_stats = mm_tok[C - 1]

    def f_rows(v):
        v.tensor_scalar(rowA[0:1, 0:512], PS[0][0:1, 0:512], 1.0 / D, None, ALU.mult)
        v.tensor_scalar(rowA[0:1, 512:1024], PS[1][0:1, 0:512], 1.0 / D, None, ALU.mult)
        v.tensor_scalar(rowB[0:1, 0:512], PS[2][0:1, 0:512], 1.0 / D, None, ALU.mult)
        v.tensor_scalar(rowB[0:1, 512:1024], PS[3][0:1, 0:512], 1.0 / D, None, ALU.mult)
        v.tensor_tensor(scrC[0:1, :], rowA[0:1, :], rowA[0:1, :], ALU.mult)
        v.tensor_tensor(rowB[0:1, :], rowB[0:1, :], scrC[0:1, :], ALU.subtract)
        return v.tensor_scalar(rowA[0:1, :], rowA[0:1, :], -1.0, None, ALU.mult)
    t_var = S_.step("sv", f_rows, [("pe", t_stats)], ("sv", 1))
    t_sd = S_.step("ss", lambda s: s.activation(rowB[0:1, :], rowB[0:1, :], AF.Sqrt, bias=epsT[:]),
                   [("sv", t_var), ("sg", t_eps)], ("ss", 1))
    t_rs = S_.step("sv", lambda v: v.reciprocal(rowB[0:1, :], rowB[0:1, :]),
                   [("ss", t_sd)], ("sv", 1))

    t_bc = None
    for s2 in range(2):
        def f_bcs(t, s2=s2):
            t.matmul(PS[4][:, 0:512], ones_f[:], rowA[0:1, s2 * 512:(s2 + 1) * 512],
                     start=True, stop=True)
            return t.matmul(PS[5][:, 0:512], ones_f[:], rowB[0:1, s2 * 512:(s2 + 1) * 512],
                            start=True, stop=True)
        w = [("sv", t_rs), ("sg", t_onef)] if s2 == 0 else [("sv", t_bc)]
        tk = S_.step("pe", f_bcs, w, ("pe", 1))

        def f_bce(v, s2=s2):
            v.tensor_copy(bcM[:, s2 * 512:(s2 + 1) * 512], PS[4][:, 0:512])
            return v.tensor_copy(bcR[:, s2 * 512:(s2 + 1) * 512], PS[5][:, 0:512])
        t_bc = S_.step("sv", f_bce, [("pe", tk)], ("sv", 1))

    def f_norm(v):
        last = None
        for c in range(C):
            v.tensor_tensor(scrA[:], X[:, c * T:(c + 1) * T], bcM[:], ALU.add)
            last = v.tensor_tensor(XN[:, c * T:(c + 1) * T], scrA[:], bcR[:], ALU.mult)
        return last
    t_xn = S_.step("sv", f_norm, [("sv", t_bc)], ("sv", 1))
    if dbg:
        S_.dma("gp", dbg["d_xn"].rearrange("(c p) t -> p c t", p=P),
               XN[:].rearrange("p (c t) -> p c t", t=T), [("sv", t_xn)], sem="dbg")

    korder = []
    for cp in range(8):
        korder += [cp, cp + 8]

    wsem = ["wb0", "wb1"]
    wld = {0: ("wb0", S_.dma("gp", WB[0][:], wk3[korder[0]], [("sv", t_xn)], sem="wb0"))}
    wld[1] = ("wb1", S_.dma("gp", WB[1][:], wk3[korder[1]], sem="wb1"))
    cs = {}
    for cp in range(2):
        sl = cp % 2
        S_.dma("sy", cosb[:, sl * T:(sl + 1) * T], io["cos0"][cp * P:(cp + 1) * P, :], sem=f"cs{sl}")
        cs[cp] = (f"cs{sl}", S_.dma("sy", sinb[:, sl * T:(sl + 1) * T], io["sin0"][cp * P:(cp + 1) * P, :], sem=f"cs{sl}"))

    evac = {}
    rope_done = {}
    kst = {}
    for ip in range(8):
        iA, iB = 2 * ip, 2 * ip + 1
        wA = [wld[iA]] + ([("sv", evac[iA - 2])] if iA >= 2 else [])
        tA = fm_proj_chunk(S_, WB[iA % 2], XN, 0, PS[0], PS[1], wA)
        wB = [wld[iB]] + ([("sv", evac[iB - 2])] if iB >= 2 else [])
        tB = fm_proj_chunk(S_, WB[iB % 2], XN, 0, PS[2], PS[3], wB)
        for nx in (iA + 2, iB + 2):
            if nx < 16 and nx not in wld:
                sm = wsem[nx % 2]
                wld[nx] = (sm, S_.dma("gp", WB[nx % 2][:], wk3[korder[nx]],
                                      [("pe", tA if nx % 2 == 0 else tB)], sem=sm))

        def f_ev(v, tB=tB):
            v.tensor_copy(scrA[:, 0:512], PS[0][:, 0:512])
            v.tensor_copy(scrA[:, 512:1024], PS[1][:, 0:512])
            v.wait_ge(S_.sems["pe"], tB)
            v.tensor_copy(scrB[:, 0:512], PS[2][:, 0:512])
            return v.tensor_copy(scrB[:, 512:1024], PS[3][:, 0:512])
        evac[iA] = evac[iB] = S_.step("sv", f_ev, [("pe", tA)], ("sv", 1))

        sl = ip % 2
        def f_rope(v, sl=sl):
            v.tensor_tensor(scrC[:], scrA[:], cosb[:, sl * T:(sl + 1) * T], ALU.mult)
            v.tensor_tensor(KR[1][:], scrB[:], cosb[:, sl * T:(sl + 1) * T], ALU.mult)
            v.tensor_tensor(scrB[:], scrB[:], sinb[:, sl * T:(sl + 1) * T], ALU.mult)
            v.tensor_tensor(scrA[:], scrA[:], sinb[:, sl * T:(sl + 1) * T], ALU.mult)
            v.tensor_tensor(KR[0][:], scrC[:], scrB[:], ALU.subtract)
            return v.tensor_tensor(KR[1][:], KR[1][:], scrA[:], ALU.add)
        waits = [cs[ip]] + ([("kr", kst[ip - 1])] if ip >= 1 else [])
        rope_done[ip] = S_.step("sv", f_rope, waits, ("sv", 1))

        S_.dma("sy", io["ctb_k"].ap()[ip * P:(ip + 1) * P, :], KR[0][:],
               [("sv", rope_done[ip])], sem="kr")
        kst[ip] = S_.dma("sy", io["ctb_k"].ap()[(ip + 8) * P:(ip + 9) * P, :], KR[1][:], sem="kr")
        if dbg:
            S_.dma("gp", dbg["d_k"][ip * P:(ip + 1) * P, :], KR[0][:],
                   [("sv", rope_done[ip])], sem="kr")
            kst[ip] = S_.dma("gp", dbg["d_k"][(ip + 8) * P:(ip + 9) * P, :], KR[1][:], sem="kr")
        nxt = ip + 2
        if nxt < 8:
            sl2 = nxt % 2
            S_.dma("sy", cosb[:, sl2 * T:(sl2 + 1) * T],
                   io["cos0"][nxt * P:(nxt + 1) * P, :],
                   [("sv", rope_done[ip])], sem=f"cs{sl2}")
            cs[nxt] = (f"cs{sl2}", S_.dma("sy", sinb[:, sl2 * T:(sl2 + 1) * T],
                       io["sin0"][nxt * P:(nxt + 1) * P, :], sem=f"cs{sl2}"))

    t_agk = S_.step("gp", lambda g: g.collective_compute(
        "AllGather", ALU.bypass, replica_groups=RG_PAIRS,
        ins=[io["ctb_k"].ap().opt()], outs=[io["gth_k"].ap().opt()]),
        [("kr", kst[7])], ("cc0", 1))

    vld = {0: ("wb0", S_.dma("gp", WB[0][:], wv3[0], [("pe", S_.cnt["pe"])], sem="wb0"))}
    vld[1] = ("wb1", S_.dma("gp", WB[1][:], wv3[1], sem="wb1"))
    vev = {}
    vst = {}
    vtok = {}
    for j in range(C):
        pa, pb = (PS[0], PS[1]) if j % 2 == 0 else (PS[2], PS[3])
        w = [vld[j]] + ([("ss", vev[j - 2])] if j >= 2 else [("sv", evac[15])])
        vtok[j] = fm_proj_chunk(S_, WB[j % 2], XN, 0, pa, pb, w)
        nx = j + 2
        if nx < C:
            sm = wsem[nx % 2]
            vld[nx] = (sm, S_.dma("gp", WB[nx % 2][:], wv3[nx], [("pe", vtok[j])], sem=sm))

        def f_vc(s, j=j):
            pa, pb = (PS[0], PS[1]) if j % 2 == 0 else (PS[2], PS[3])
            vc = VCH[j % 2]
            s.activation(vc[:, 0:512], pa[:, 0:512], AF.Copy)
            return s.activation(vc[:, 512:1024], pb[:, 0:512], AF.Copy)
        wv_ = [("pe", vtok[j])] + ([("vch", vst[j - 2])] if j >= 2 else [])
        vev[j] = S_.step("ss", f_vc, wv_, ("ss", 1))
        vst[j] = S_.dma("sy", io["ctb_v"].ap()[j * P:(j + 1) * P, :], VCH[j % 2][:],
                        [("ss", vev[j])], sem="vch")
        if dbg:
            vst[j] = S_.dma("gp", dbg["d_v"][j * P:(j + 1) * P, :], VCH[j % 2][:],
                            [("ss", vev[j])], sem="vch")

    t_agv = S_.step("gp", lambda g: g.collective_compute(
        "AllGather", ALU.bypass, replica_groups=RG_PAIRS,
        ins=[io["ctb_v"].ap().opt()], outs=[io["gth_v"].ap().opt()]),
        [("vch", vst[C - 1])], ("cc1", 1))
    t_vproj_last = vtok[C - 1]

    fld = {0: ("wf0", S_.dma("sy", WFB[0][:], io["wf"][0:P, :], sem="wf0"))}
    fld[1] = ("wf1", S_.dma("sy", WFB[1][:], io["wf"][P:2 * P, :], sem="wf1"))
    fxadd = {}
    fbias_prev = None
    fbias_prev2 = None
    fmm_prev = None
    for j in range(C):
        wv_ = [fld[j]] + ([("pe", fmm_prev)] if fmm_prev else [("pe", t_stats)])
        t_rd = S_.step("sv", lambda v, j=j: v.tensor_reduce(
            thr[:], WFB[j % 2][:], AX.X, ALU.add, apply_absolute_value=True),
            wv_, ("sv", 1))

        def f_thr(g):
            g.tensor_scalar(nthr[:], thr[:], -0.7 / D, None, ALU.mult)
            return g.tensor_scalar(thr[:], thr[:], 0.7 / D, None, ALU.mult)
        t_th = S_.step("gp", f_thr, [("sv", t_rd)], ("sg", 1))

        def f_tern(v, j=j):
            W = WFB[j % 2]
            v.tensor_scalar(t1b[:], W[:], thr[:], None, ALU.is_gt)
            v.tensor_scalar(TERN[:], W[:], nthr[:], None, ALU.is_lt)
            return v.tensor_tensor(TERN[:], t1b[:], TERN[:], ALU.subtract)
        t_tn = S_.step("sv", f_tern, [("sg", t_th)], ("sv", 1))
        nx = j + 2
        if nx < C:
            sm = f"wf{nx % 2}"
            fld[nx] = (sm, S_.dma("sy", WFB[nx % 2][:], io["wf"][nx * P:(nx + 1) * P, :],
                                  [("sv", t_tn)], sem=sm))
        t_fl = transpose16(S_, TERN, 0,
                           lambda r: FLT[:, r * 8 * P:(r + 1) * 8 * P],
                           [("sv", t_tn), ("once", t_once)], ident)
        pa, pb = (PS[0], PS[1]) if j % 2 == 0 else (PS[2], PS[3])
        wmm = [("sv", t_fl)] + ([("sv", fbias_prev2)] if fbias_prev2 else [("ss", vev[C - 1])])
        t_mm = fm_proj_chunk(S_, FLT, XN, 0, pa, pb, wmm)
        fmm_prev = t_mm

        def f_bias(v, j=j, pa=pa, pb=pb):
            o = (j % 2) * T
            v.tensor_scalar(FSC[:, o:o + 512], pa[:, 0:512], biasT[:, j:j + 1], None, ALU.add)
            return v.tensor_scalar(FSC[:, o + 512:o + 1024], pb[:, 0:512], biasT[:, j:j + 1], None, ALU.add)
        fbias_prev2 = fbias_prev
        fbias_prev = S_.step("sv", f_bias, [("pe", t_mm), ("once", t_once)], ("sv", 1))
        fxadd[j] = S_.step("gp", lambda g, j=j: g.tensor_tensor(
            X[:, j * T:(j + 1) * T], X[:, j * T:(j + 1) * T],
            FSC[:, (j % 2) * T:(j % 2 + 1) * T], ALU.add),
            [("sv", fbias_prev)], ("sg", 1))
        if dbg:
            S_.dma("gp", dbg["d_ffn"][j * P:(j + 1) * P, :], FSC[:, (j % 2) * T:(j % 2 + 1) * T],
                   [("sg", fxadd[j])], sem="dbg")

    ffn_mm_last = fmm_prev
    ffn_bias_last = fbias_prev

    t_gs = S_.step("ss", lambda s: s.activation(gateT[0:1, :], gateT[0:1, :], AF.Sigmoid),
                   [("once", t_once)], ("ss", 1))

    def f_gbc(t):
        mm = None
        for hh2 in range(NH):
            mm = t.matmul(PS[6][:, hh2:hh2 + 1], ones_f[:], gateT[0:1, hh2:hh2 + 1],
                          start=(hh2 == 0), stop=(hh2 == NH - 1))
        return mm
    t_gb = S_.step("pe", f_gbc, [("ss", t_gs), ("sg", t_onef)], ("pe", 1))
    t_gsig = S_.step("ss", lambda s: s.activation(gsig[:], PS[6][:, 0:NH], AF.Copy),
                     [("pe", t_gb)], ("ss", 1))

    t_spill = S_.dma("sy", io["spill"].ap().rearrange("(c p) t -> p c t", p=P),
                     X[:].rearrange("p (c t) -> p c t", t=T),
                     [("sg", fxadd[C - 1])], sem="spl")

    scope1.close()

    scope2 = ExitStack()
    COMB = sb("COMB", [P, NH * T], BF16, scope2)
    scope2a = ExitStack()
    KHB = [sb("KH0", [P, S], BF16, scope2a), sb("KH1", [P, S], BF16, scope2a)]
    VFH = [sb("VF0", [P, S], BF16, scope2a), sb("VF1", [P, S], BF16, scope2a)]
    VTH = [sb("VT0", [P, S], BF16, scope2a), sb("VT1", [P, S], BF16, scope2a)]
    QR = sb("QR", [P, 2 * T], BF16, scope2a)
    QF = sb("QF", [P, 4 * T], BF16, scope2a)
    KFM = sb("KFM", [P, S], BF16, scope2a)
    KFT = sb("KFT", [P, S], BF16, scope2a)
    MEMS = sb("MEMS", [P, 2 * P], BF16, scope2a)
    PTB = sb("PTB", [P, 2 * C * 512], BF16, scope2a)
    rz_sb = sb("rz_sb", [1, 512], F32, scope2a)
    rd_sb = sb("rd_sb", [1, 512], F32, scope2a)
    bz_sb = sb("bz_sb", [P, 512], F32, scope2a)
    mem_sb = scrB[:, 0:512]
    cT1 = scrC[:, 0:512]
    cT2 = scrC[:, 512:1024]

    gk = io["gth_k"].ap()
    gv = io["gth_v"].ap()

    qld = {0: ("wb0", S_.dma("gp", WB[0][:], wq3[korder[0]], [("pe", ffn_mm_last)], sem="wb0"))}
    qld[1] = ("wb1", S_.dma("gp", WB[1][:], wq3[korder[1]], sem="wb1"))
    khld = {}
    vfld = {}
    for pos in range(2):
        hh = korder[pos]
        S_.dma("gp", KHB[pos][:, 0:T], gk[hh * P:(hh + 1) * P, :],
               ([("cc0", t_agk)] if pos == 0 else []) + [("spl", t_spill)], sem=f"kh{pos}")
        khld[pos] = (f"kh{pos}", S_.dma("gp", KHB[pos][:, T:S],
                     gk[D + hh * P:D + (hh + 1) * P, :], sem=f"kh{pos}"))
        S_.dma("gp", VFH[pos][:, 0:T], gv[hh * P:(hh + 1) * P, :],
               [("cc1", t_agv)] if pos == 0 else [], sem=f"vf{pos}")
        vfld[pos] = (f"vf{pos}", S_.dma("gp", VFH[pos][:, T:S],
                     gv[D + hh * P:D + (hh + 1) * P, :], sem=f"vf{pos}"))

    qevac = {}
    qcs = {}
    ropeq = {}
    kfree = {}
    zav_hist = {}
    ex_hist = {}
    comb_toks = []
    pend = []
    sidx = [0]
    kt_prev = [None]
    last_msb = [None]

    def l0_tail(d):
        ii_, sig_, hh_, qo_, par_ = d["ii"], d["sig"], d["hh"], d["qo"], d["par"]
        po = par_ * C * 512

        def f_zav(t, ii_=ii_, qo_=qo_, po=po):
            for jt in range(C):
                t.matmul(PS[5][0:1, 0:512], ones_b[:], PTB[:, po + jt * 512:po + (jt + 1) * 512],
                         start=(jt == 0), stop=(jt == C - 1))
            for jt in range(C):
                t.matmul(PS[4][:, 0:512], VTH[ii_ % 2][:, jt * P:(jt + 1) * P],
                         PTB[:, po + jt * 512:po + (jt + 1) * 512],
                         start=(jt == 0), stop=(jt == C - 1))
            t.matmul(PS[6][:, 0:512], MEMS[:, (ii_ % 2) * P:(ii_ % 2 + 1) * P],
                     QF[:, qo_:qo_ + 512], start=True, stop=True)
            return t.matmul(PS[5][32:33, 0:512], NTB[:, ii_ % 2:ii_ % 2 + 1],
                            QF[:, qo_:qo_ + 512], start=True, stop=True)
        wz = [("ss", d["ex"]), ("sv", last_msb[0])]
        wz.append(("sv", comb_toks[-1]) if comb_toks else ("sv", S_.cnt["sv"]))
        tz = S_.step("pe", f_zav, wz, ("pe", 1))
        zav_hist[(ii_, sig_)] = tz

        def f_rcp(v):
            v.reciprocal(rz_sb[0:1, :], PS[5][0:1, 0:512])
            v.reciprocal(rd_sb[0:1, :], PS[5][32:33, 0:512])
            return v.tensor_copy(mem_sb, PS[6][:, 0:512])
        t_rcp = S_.step("sv", f_rcp, [("pe", tz)], ("sv", 1))
        t_bz = S_.step("pe", lambda t: t.matmul(PS[6][:, 0:512], ones_f[:], rz_sb[0:1, :],
                                                start=True, stop=True),
                       [("sv", t_rcp)], ("pe", 1))
        t_bzc = S_.step("sv", lambda v: v.tensor_copy(bz_sb[:], PS[6][:, 0:512]),
                        [("pe", t_bz)], ("sv", 1))
        t_bd = S_.step("pe", lambda t: t.matmul(PS[6][:, 0:512], ones_f[:], rd_sb[0:1, :],
                                                start=True, stop=True),
                       [("sv", t_bzc)], ("pe", 1))

        def f_comb(v, hh_=hh_, sig_=sig_):
            v.tensor_tensor(cT1, PS[4][:, 0:512], bz_sb[:], ALU.mult)
            v.tensor_tensor(cT2, mem_sb, PS[6][:, 0:512], ALU.mult)
            v.tensor_tensor(cT2, cT2, cT1, ALU.subtract)
            return v.scalar_tensor_tensor(
                COMB[:, hh_ * T + sig_ * 512: hh_ * T + (sig_ + 1) * 512],
                cT2, gsig[:, hh_:hh_ + 1], cT1, ALU.mult, ALU.add)
        comb_toks.append(S_.step("sv", f_comb, [("pe", t_bd), ("ss", t_gsig)], ("sv", 1)))

    for ip in range(8):
        iA, iB = 2 * ip, 2 * ip + 1
        wA = [qld[iA]] + ([("sv", qevac[iA - 2])] if iA >= 2 else [("sv", ffn_bias_last)])
        if sidx[0] >= 2:
            wA.append(("ss", ex_hist[sidx[0] - 2]))
        tA = fm_proj_chunk(S_, WB[iA % 2], XN, 0, PS[0], PS[1], wA)
        wB = [qld[iB]] + ([("sv", qevac[iB - 2])] if iB >= 2 else [("sv", ffn_bias_last)])
        if sidx[0] >= 1:
            wB.append(("ss", ex_hist[sidx[0] - 1]))
        tB = fm_proj_chunk(S_, WB[iB % 2], XN, 0, PS[2], PS[3], wB)
        for nx in (iA + 2, iB + 2):
            if nx < 16 and nx not in qld:
                sm = wsem[nx % 2]
                qld[nx] = (sm, S_.dma("gp", WB[nx % 2][:], wq3[korder[nx]],
                                      [("pe", tA if nx % 2 == 0 else tB)], sem=sm))
        sl = ip % 2
        w2 = [("sv", ropeq[ip - 2])] if ip >= 2 else [("sv", rope_done[6 + ip])]
        S_.dma("sy", cosb[:, sl * T:(sl + 1) * T], io["cos0"][ip * P:(ip + 1) * P, :],
               w2, sem=f"cs{sl}")
        qcs[ip] = (f"cs{sl}", S_.dma("sy", sinb[:, sl * T:(sl + 1) * T],
                   io["sin0"][ip * P:(ip + 1) * P, :], sem=f"cs{sl}"))

        def f_qev(v, tB=tB):
            v.tensor_copy(scrA[:, 0:512], PS[0][:, 0:512])
            v.tensor_copy(scrA[:, 512:1024], PS[1][:, 0:512])
            v.wait_ge(S_.sems["pe"], tB)
            v.tensor_copy(scrB[:, 0:512], PS[2][:, 0:512])
            return v.tensor_copy(scrB[:, 512:1024], PS[3][:, 0:512])
        qevac[iA] = qevac[iB] = S_.step("sv", f_qev, [("pe", tA)], ("sv", 1))

        tvthA = transpose16(S_, VFH[iA % 2], 0,
                            lambda r, iA=iA: VTH[iA % 2][:, r * 8 * P:(r + 1) * 8 * P],
                            [vfld[iA]] + ([("pe", zav_hist[(iA - 2, 1)])] if iA >= 2 else []),
                            ident)

        def f_qrope(v, sl=sl):
            v.tensor_tensor(scrC[:], scrA[:], cosb[:, sl * T:(sl + 1) * T], ALU.mult)
            v.tensor_tensor(QR[:, T:2 * T], scrB[:], cosb[:, sl * T:(sl + 1) * T], ALU.mult)
            v.tensor_tensor(scrB[:], scrB[:], sinb[:, sl * T:(sl + 1) * T], ALU.mult)
            v.tensor_tensor(scrA[:], scrA[:], sinb[:, sl * T:(sl + 1) * T], ALU.mult)
            v.tensor_tensor(QR[:, 0:T], scrC[:], scrB[:], ALU.subtract)
            v.tensor_tensor(QR[:, T:2 * T], QR[:, T:2 * T], scrA[:], ALU.add)
            v.tensor_scalar(scrA[:], QR[:, 0:T], 0.0, None, ALU.min)
            return v.tensor_scalar(scrB[:], QR[:, T:2 * T], 0.0, None, ALU.min)
        wr = [qcs[ip]]
        if ip == 0:
            wr.append(("spl", t_spill))
        if comb_toks:
            wr.append(("sv", comb_toks[-1]))
        ropeq[ip] = S_.step("sv", f_qrope, wr, ("sv", 1))

        def f_qexp(s):
            s.activation(scrA[:], scrA[:], AF.Exp)
            return s.activation(scrB[:], scrB[:], AF.Exp)
        t_qe = S_.step("ss", f_qexp, [("sv", ropeq[ip])], ("ss", 1))

        qfo = (ip % 2) * 2 * T

        def f_qf(v, qfo=qfo):
            v.scalar_tensor_tensor(QF[:, qfo:qfo + T], QR[:, 0:T], 0.0, scrA[:], ALU.max, ALU.add)
            return v.scalar_tensor_tensor(QF[:, qfo + T:qfo + 2 * T], QR[:, T:2 * T], 0.0,
                                          scrB[:], ALU.max, ALU.add)
        wqf = [("ss", t_qe)]
        if ip >= 2:
            wqf.append(("pe", zav_hist[(2 * ip - 3, 1)]))
        t_qf = S_.step("sv", f_qf, wqf, ("sv", 1))

        for which in range(2):
            ii = 2 * ip + which
            hh = korder[ii]
            kbuf = KHB[ii % 2]

            if which == 0:
                t_vth = tvthA
            else:
                t_vth = transpose16(S_, VFH[ii % 2], 0,
                                    lambda r, ii=ii: VTH[ii % 2][:, r * 8 * P:(r + 1) * 8 * P],
                                    [vfld[ii]] + ([("pe", zav_hist[(ii - 2, 1)])] if ii >= 2 else []),
                                    ident)

            def f_kmin(v, kbuf=kbuf):
                v.tensor_scalar(scrA[:, 0:512], kbuf[:, 0:512], 0.0, None, ALU.min)
                v.tensor_scalar(scrA[:, 512:1024], kbuf[:, 512:1024], 0.0, None, ALU.min)
                v.tensor_scalar(scrB[:, 0:512], kbuf[:, 1024:1536], 0.0, None, ALU.min)
                return v.tensor_scalar(scrB[:, 512:1024], kbuf[:, 1536:2048], 0.0, None, ALU.min)
            wkm = [khld[ii], ("sv", t_qf)]
            if kt_prev[0] is not None:
                wkm.append(("sv", kt_prev[0]))
            t_km = S_.step("sv", f_kmin, wkm, ("sv", 1))

            def f_kexp(s):
                s.activation(scrA[:], scrA[:], AF.Exp)
                return s.activation(scrB[:], scrB[:], AF.Exp)
            t_ke = S_.step("ss", f_kexp, [("sv", t_km)], ("ss", 1))

            def f_kfm(v, kbuf=kbuf):
                v.scalar_tensor_tensor(KFM[:, 0:T], kbuf[:, 0:T], 0.0, scrA[:], ALU.max, ALU.add)
                v.scalar_tensor_tensor(KFM[:, T:S], kbuf[:, T:S], 0.0, scrB[:], ALU.max, ALU.add)
                return v.tensor_reduce(thr[:], KFM[:], AX.X, ALU.add)
            t_kf0 = S_.step("sv", f_kfm, [("ss", t_ke)], ("sv", 1))
            wkf = [("sv", t_kf0)]
            if ii >= 2:
                wkf.append(("pe", zav_hist[(ii - 2, 1)]))
            t_kf = S_.step("ss", lambda s, ii=ii: s.copy(NTB[:, ii % 2:ii % 2 + 1], thr[:]),
                           wkf, ("ss", 1))

            t_kt = transpose16(S_, KFM, 0,
                               lambda r: KFT[:, r * 8 * P:(r + 1) * 8 * P],
                               [("ss", t_kf)], ident)
            kt_prev[0] = t_kt

            def f_memmm(t, ii=ii):
                mm = None
                for jt in range(C):
                    mm = t.matmul(PS[6][:, 0:P], KFT[:, jt * P:(jt + 1) * P],
                                  VTH[ii % 2][:, jt * P:(jt + 1) * P],
                                  start=(jt == 0), stop=(jt == C - 1))
                return mm
            wmem = [("sv", t_kt), ("sv", t_vth)]
            if comb_toks:
                wmem.append(("sv", comb_toks[-1]))
            t_mem = S_.step("pe", f_memmm, wmem, ("pe", 1))
            wmsb = [("pe", t_mem)]
            if ii >= 2:
                wmsb.append(("pe", zav_hist[(ii - 2, 1)]))
            t_msb = S_.step("sv", lambda v, ii=ii: v.tensor_copy(
                MEMS[:, (ii % 2) * P:(ii % 2 + 1) * P], PS[6][:, 0:P]), wmsb, ("sv", 1))
            last_msb[0] = t_msb

            for sig in range(2):
                par = sidx[0] % 2
                po = par * C * 512
                qr = which * T + sig * 512
                qo = qfo + which * T + sig * 512
                ex_toks = []
                sc_toks = []
                for jt in range(C):
                    wsc = []
                    if jt >= 2:
                        wsc.append(("ss", ex_toks[jt - 2]))
                    elif sidx[0] >= 2:
                        wsc.append(("ss", ex_hist[sidx[0] - 2]))
                    else:
                        wsc.append(("ss", S_.cnt["ss"]))
                    if jt == 0:
                        wsc.append(("sv", t_qf))

                    def f_sc(t, jt=jt, kbuf=kbuf, qr=qr, par=par):
                        return t.matmul(PS[2 * par + jt % 2][:, 0:512],
                                        kbuf[:, jt * P:(jt + 1) * P],
                                        QR[:, qr:qr + 512], start=True, stop=True)
                    tsc = S_.step("pe", f_sc, wsc, ("pe", 1))
                    sc_toks.append(tsc)
                    wex = [("pe", tsc)]
                    if jt == 0:
                        wex.append(("sg", t_shift))
                    ex_toks.append(S_.step("ss", lambda s, jt=jt, po=po, par=par: s.activation(
                        PTB[:, po + jt * 512:po + (jt + 1) * 512],
                        PS[2 * par + jt % 2][:, 0:512], AF.Exp,
                        bias=shiftT[:], scale=1.0 / math.sqrt(HD)), wex, ("ss", 1)))
                ex_hist[sidx[0]] = ex_toks[-1]
                if pend:
                    l0_tail(pend.pop())
                pend.append(dict(ii=ii, sig=sig, hh=hh, qo=qo, par=par, ex=ex_toks[-1]))
                sidx[0] += 1
            kfree[ii] = sc_toks[-1]
            nx = ii + 2
            if nx < 16 and nx not in khld:
                hh2 = korder[nx]
                S_.dma("gp", KHB[nx % 2][:, 0:T], gk[hh2 * P:(hh2 + 1) * P, :],
                       [("pe", kfree[ii]), ("sv", t_kt)], sem=f"kh{nx % 2}")
                khld[nx] = (f"kh{nx % 2}", S_.dma("gp", KHB[nx % 2][:, T:S],
                            gk[D + hh2 * P:D + (hh2 + 1) * P, :], sem=f"kh{nx % 2}"))
                S_.dma("gp", VFH[nx % 2][:, 0:T], gv[hh2 * P:(hh2 + 1) * P, :],
                       [("sv", t_vth)], sem=f"vf{nx % 2}")
                vfld[nx] = (f"vf{nx % 2}", S_.dma("gp", VFH[nx % 2][:, T:S],
                            gv[D + hh2 * P:D + (hh2 + 1) * P, :], sem=f"vf{nx % 2}"))
    while pend:
        l0_tail(pend.pop())


    if dbg:
        S_.dma("gp", dbg["d_comb"].rearrange("(c p) t -> p c t", p=P),
               COMB[:].rearrange("p (c t) -> p c t", t=T), [("sv", comb_toks[-1])], sem="dbg")

    scope2a.close()
    XCH = [sb("XCHa", [P, T], F32, scope2), sb("XCHb", [P, T], F32, scope2)]
    old = {0: ("wb0", S_.dma("gp", WB[0][:], wo3[0], [("pe", S_.cnt["pe"])], sem="wb0"))}
    old[1] = ("wb1", S_.dma("gp", WB[1][:], wo3[1], sem="wb1"))
    w_xc = [("spl", t_spill), ("sv", comb_toks[-1])]
    xch = {0: S_.dma("sy", XCH[0][:], io["spill"].ap()[0:P, :], w_xc, sem="xch"),
           1: S_.dma("sy", XCH[1][:], io["spill"].ap()[P:2 * P, :], w_xc, sem="xch")}
    xadd = {}
    x1st = {}
    for j in range(C):
        pa, pb = (PS[0], PS[1]) if j % 2 == 0 else (PS[2], PS[3])
        w = [old[j], ("sv", comb_toks[-1])] + ([("sv", xadd[j - 2])] if j >= 2 else [("ss", S_.cnt["ss"])])
        tj = fm_proj_chunk(S_, WB[j % 2], COMB, 0, pa, pb, w)
        nx = j + 2
        if nx < C:
            sm = wsem[nx % 2]
            old[nx] = (sm, S_.dma("gp", WB[nx % 2][:], wo3[nx], [("pe", tj)], sem=sm))

        def f_xadd(v, j=j, pa=pa, pb=pb):
            xc = XCH[j % 2]
            v.tensor_tensor(xc[:, 0:512], xc[:, 0:512], pa[:, 0:512], ALU.add)
            return v.tensor_tensor(xc[:, 512:1024], xc[:, 512:1024], pb[:, 0:512], ALU.add)
        xadd[j] = S_.step("sv", f_xadd, [("pe", tj), ("xch", xch[j])], ("sv", 1))
        x1st[j] = S_.dma("sy", io["x1_out"][j * P:(j + 1) * P, :], XCH[j % 2][:],
                         [("sv", xadd[j])], sem="x1")
        nx = j + 2
        if nx < C:
            xch[nx] = S_.dma("sy", XCH[nx % 2][:], io["spill"].ap()[nx * P:(nx + 1) * P, :],
                             [("x1", x1st[j])], sem="xch")

    scope2.close()
    allw = [(nm, S_.cnt[nm]) for nm in S_.cnt
            if nm not in ("pe", "ss", "sv", "sg") and S_.cnt[nm] > 0]
    S_.step("sy", lambda sy: sy.nop(), allw)


def build_l0(dbg=False):
    nc = bass.Bass("TRN2", target_bir_lowering=False)
    io = {}
    io["x_in"] = nc.dram_tensor("x_fm", [D, T], F32, kind="ExternalInput").ap()
    io["wq"] = nc.dram_tensor("wq_blk", [C * P, D], F32, kind="ExternalInput").ap()
    io["wk"] = nc.dram_tensor("wk_blk", [C * P, D], F32, kind="ExternalInput").ap()
    io["wv"] = nc.dram_tensor("wv_blk", [C * P, D], F32, kind="ExternalInput").ap()
    io["wo"] = nc.dram_tensor("wo_blk", [C * P, D], F32, kind="ExternalInput").ap()
    io["wf"] = nc.dram_tensor("wf_rows", [D, D], F32, kind="ExternalInput").ap()
    io["bias_sw"] = nc.dram_tensor("bias_sw", [P, C], F32, kind="ExternalInput").ap()
    io["gate_in"] = nc.dram_tensor("gate_row", [1, NH], F32, kind="ExternalInput").ap()
    io["cos0"] = nc.dram_tensor("cos0", [8 * P, T], F32, kind="ExternalInput").ap()
    io["sin0"] = nc.dram_tensor("sin0", [8 * P, T], F32, kind="ExternalInput").ap()
    io["ident_in"] = nc.dram_tensor("ident_in", [P, P], F32, kind="ExternalInput").ap()
    io["x1_out"] = nc.dram_tensor("x1_fm", [D, T], F32, kind="ExternalOutput").ap()
    io["spill"] = nc.dram_tensor("spill_x", [D, T], F32, kind="Internal")
    io["ctb_k"] = nc.dram_tensor("ctb_k", [D, T], BF16, kind="Internal")
    io["ctb_v"] = nc.dram_tensor("ctb_v", [D, T], BF16, kind="Internal")
    io["gth_k"] = nc.dram_tensor("gth_k", [2 * D, T], BF16, kind="Internal")
    io["gth_v"] = nc.dram_tensor("gth_v", [2 * D, T], BF16, kind="Internal")
    dbgd = {}
    if dbg:
        for nm in ("d_xn", "d_k", "d_v", "d_ffn", "d_comb"):
            dbgd[nm] = nc.dram_tensor(nm, [D, T], F32, kind="ExternalOutput").ap()
        dbgd["d_ptb"] = nc.dram_tensor("d_ptb", [P, C * 512], F32, kind="ExternalOutput").ap()
        dbgd["d_zrow"] = nc.dram_tensor("d_zrow", [1, 512], F32, kind="ExternalOutput").ap()
        dbgd["d_den"] = nc.dram_tensor("d_den", [1, 512], F32, kind="ExternalOutput").ap()
        dbgd["d_av"] = nc.dram_tensor("d_av", [P, 512], F32, kind="ExternalOutput").ap()
        dbgd["d_mo"] = nc.dram_tensor("d_mo", [P, 512], F32, kind="ExternalOutput").ap()
        dbgd["d_qr"] = nc.dram_tensor("d_qr", [P, T], F32, kind="ExternalOutput").ap()
        dbgd["d_qf"] = nc.dram_tensor("d_qf", [P, T], F32, kind="ExternalOutput").ap()
        dbgd["d_kfm"] = nc.dram_tensor("d_kfm", [P, S], F32, kind="ExternalOutput").ap()
        dbgd["d_ntb"] = nc.dram_tensor("d_ntb", [P, 1], F32, kind="ExternalOutput").ap()
        dbgd["d_mems"] = nc.dram_tensor("d_mems", [P, P], F32, kind="ExternalOutput").ap()
        dbgd["d_vth"] = nc.dram_tensor("d_vth", [P, S], F32, kind="ExternalOutput").ap()
    with ExitStack() as ctx:
        S_ = Sched(nc, ctx)
        emit_l0(S_, io, dbgd)
        blk = ctx.enter_context(nc.Block())
        S_.emit(blk)
    return nc




NT_Q = 8
E64 = 64
G8 = 8
I14 = 1408
IC = I14 // P
KVR = 512
KC4 = KVR // P


def emit_l1(S_, io, dbg, stop_after=None):
    PS = S_.PS
    PT = S_.PT
    sb = S_.sb
    SL = S
    NSL = SL // 512

    XN = sb("XN1", [P, C * T], BF16)
    ident = sb("ident1", [P, P], BF16)
    ones_b = sb("ones1b", [P, 1], BF16)
    ones_f = sb("ones1f", [1, P], F32)
    ones8 = sb("ones8", [8, P], BF16)
    shiftT = sb("shiftT1", [P, 1], F32)
    epsT = sb("epsT1", [1, 1], F32)
    eps2T = sb("eps2T", [1, 1], F32)
    cos1 = sb("cos1t", [P, T], F32)
    sin1 = sb("sin1t", [P, T], F32)
    R128B = sb("R128B", [P, P], BF16)
    KPEF = sb("KPEF", [P, SL], BF16)
    WB = [sb("wb1A", [P, C * P], BF16), sb("wb1B", [P, C * P], BF16)]
    scrA = sb("scrA1", [P, T], F32)
    scrB = sb("scrB1", [P, T], F32)
    scrC = sb("scrC1", [P, T], F32)
    Pmat = sb("Pmat", [64, 64], BF16)
    iota64 = sb("iota64t", [64, 1], F32)
    invf = sb("invf", [1, 64], F32)
    invi = sb("invi", [1, 64], I32)
    seg1 = sb("seg1", [P, 64], F32)
    seg2 = sb("seg2", [P, 64], F32)
    sg8a = sb("sg8a", [P, 8], F32)
    sg8b = sb("sg8b", [P, 8], F32)
    mx8 = sb("mx8", [P, 8], F32)
    rz128 = sb("rz128", [P, 1], F32)
    kvnsw = sb("kvnswt", [P, KC4], F32)

    xd = io["x_in"].rearrange("(c p) t -> p c t", p=P)
    kb3 = io["wkvb_k"].rearrange("(j p) k -> j p k", p=P)
    vb3 = io["wkvb_v"].rearrange("(j p) k -> j p k", p=P)
    wq3 = io["wq1"].rearrange("(j p) k -> j p k", p=P)
    wo3 = io["wo1"].rearrange("(j p) k -> j p k", p=P)
    wkva3 = io["wkva"].rearrange("(j p) k -> j p k", p=P)
    gpw4 = io["gpw"].rearrange("(g j p) k -> g j p k", j=IC, p=P)
    upw4 = io["upw"].rearrange("(g j p) k -> g j p k", j=IC, p=P)
    dnw4 = io["dnw"].rearrange("(g j p) k -> g j p k", j=C, p=P)

    t_on1 = S_.step("gp", lambda g: g.memset(ones_b[:], 1.0), (), ("sg", 1))
    S_.step("gp", lambda g: g.memset(ones_f[:], 1.0), (), ("sg", 1))
    S_.step("gp", lambda g: g.memset(ones8[:], 1.0), (), ("sg", 1))
    S_.step("gp", lambda g: g.memset(shiftT[:], SOFT_SHIFT), (), ("sg", 1))
    S_.step("gp", lambda g: g.memset(epsT[:], 1e-5), (), ("sg", 1))
    t_ms = S_.step("gp", lambda g: g.memset(eps2T[:], 1e-6), (), ("sg", 1))
    S_.dma("gp", ident[:], io["ident_in"], sem="once")
    S_.dma("gp", XN[:].rearrange("p (c t) -> p c t", t=T), xd, sem="once")
    S_.dma("gp", R128B[:], io["r128"], sem="once")
    S_.dma("sy", cos1[:], io["cos1"], sem="once")
    S_.dma("sy", sin1[:], io["sin1"], sem="once")
    S_.dma("sy", iota64[:], io["iota64"], sem="once")
    S_.dma("sy", invi[:], io["inv_row"], sem="once")
    t_once = S_.dma("sy", kvnsw[:], io["kvnsw"], sem="once")

    sAB = ExitStack()
    rowA = sb("rowA1", [1, T], F32, sAB)
    rowB = sb("rowB1", [1, T], F32, sAB)
    bcM = sb("bcM1", [P, T], F32, sAB)
    bcR = sb("bcR1", [P, T], F32, sAB)
    gwpm = sb("gwpm", [P, C * E64], F32, sAB)
    t_gw = S_.dma("sy", gwpm[:], io["gatew"], sem="once")

    sA = ExitStack()
    X = sb("X1", [P, C * T], F32, sA)
    SQ2 = sb("SQ21", [P, 2 * T], BF16, sA)
    XF = [sb("XFa", [P, T], F32, sA), sb("XFb", [P, T], F32, sA)]
    t_x = S_.dma("sy", X[:].rearrange("p (c t) -> p c t", t=T), xd, sem="once")

    sq_tok = {}
    mm_tok = {}
    for c in range(C):
        wsq = [("once", t_x)] if c == 0 else []
        if c >= 2:
            wsq.append(("pe", mm_tok[c - 2]))
        sq_tok[c] = S_.step("ss", lambda s, c=c: s.activation(
            SQ2[:, (c % 2) * T:(c % 2 + 1) * T], X[:, c * T:(c + 1) * T], AF.Square),
            wsq, ("ss", 1))
        def f_mm(t, c=c):
            mm = None
            for s2, pp in ((0, PS[0]), (1, PS[1])):
                t.matmul(pp[0:1, 0:512], ones_b[:],
                         XN[:, c * T + s2 * 512: c * T + (s2 + 1) * 512],
                         start=(c == 0), stop=(c == C - 1))
            for s2, pp in ((0, PS[2]), (1, PS[3])):
                mm = t.matmul(pp[0:1, 0:512], ones_b[:],
                              SQ2[:, (c % 2) * T + s2 * 512: (c % 2) * T + (s2 + 1) * 512],
                              start=(c == 0), stop=(c == C - 1))
            return mm
        wmm = [("ss", sq_tok[c])]
        if c == 0:
            wmm += [("sg", t_ms)]
        mm_tok[c] = S_.step("pe", f_mm, wmm, ("pe", 1))
    t_stats = mm_tok[C - 1]

    def f_rows(v):
        v.tensor_scalar(rowA[0:1, 0:512], PS[0][0:1, 0:512], 1.0 / D, None, ALU.mult)
        v.tensor_scalar(rowA[0:1, 512:1024], PS[1][0:1, 0:512], 1.0 / D, None, ALU.mult)
        v.tensor_scalar(rowB[0:1, 0:512], PS[2][0:1, 0:512], 1.0 / D, None, ALU.mult)
        v.tensor_scalar(rowB[0:1, 512:1024], PS[3][0:1, 0:512], 1.0 / D, None, ALU.mult)
        v.tensor_tensor(scrC[0:1, :], rowA[0:1, :], rowA[0:1, :], ALU.mult)
        v.tensor_tensor(rowB[0:1, :], rowB[0:1, :], scrC[0:1, :], ALU.subtract)
        return v.tensor_scalar(rowA[0:1, :], rowA[0:1, :], -1.0, None, ALU.mult)
    t_var = S_.step("sv", f_rows, [("pe", t_stats)], ("sv", 1))
    t_sd = S_.step("ss", lambda s: s.activation(rowB[0:1, :], rowB[0:1, :], AF.Sqrt, bias=epsT[:]),
                   [("sv", t_var)], ("ss", 1))
    t_rs = S_.step("sv", lambda v: v.reciprocal(rowB[0:1, :], rowB[0:1, :]),
                   [("ss", t_sd)], ("sv", 1))
    t_bc = None
    for s2 in range(2):
        def f_bcs(t, s2=s2):
            t.matmul(PS[4][:, 0:512], ones_f[:], rowA[0:1, s2 * 512:(s2 + 1) * 512],
                     start=True, stop=True)
            return t.matmul(PS[5][:, 0:512], ones_f[:], rowB[0:1, s2 * 512:(s2 + 1) * 512],
                            start=True, stop=True)
        w = [("sv", t_rs)] if s2 == 0 else [("sv", t_bc)]
        tk = S_.step("pe", f_bcs, w, ("pe", 1))
        def f_bce(v, s2=s2):
            v.tensor_copy(bcM[:, s2 * 512:(s2 + 1) * 512], PS[4][:, 0:512])
            return v.tensor_copy(bcR[:, s2 * 512:(s2 + 1) * 512], PS[5][:, 0:512])
        t_bc = S_.step("sv", f_bce, [("pe", tk)], ("sv", 1))

    nrm_tok = {}
    gmm_tok = {}
    for c in range(C):
        def f_nrm(v, c=c):
            v.tensor_tensor(scrC[:], X[:, c * T:(c + 1) * T], bcM[:], ALU.add)
            v.tensor_tensor(XF[c % 2][:], scrC[:], bcR[:], ALU.mult)
            return v.tensor_tensor(XN[:, c * T:(c + 1) * T], scrC[:], bcR[:], ALU.mult)
        wn = [("sv", t_bc)] if c == 0 else []
        if c >= 2:
            wn.append(("pe", gmm_tok[c - 2]))
        nrm_tok[c] = S_.step("sv", f_nrm, wn, ("sv", 1))
        def f_gmm(t, c=c):
            mm = None
            for tt in range(NT_Q):
                mm = t.matmul(PS[6][:, tt * E64:(tt + 1) * E64],
                              XF[c % 2][:, tt * P:(tt + 1) * P],
                              gwpm[:, c * E64:(c + 1) * E64],
                              start=(c == 0 and tt == 0), stop=(c == C - 1))
            return mm
        gmm_tok[c] = S_.step("pe", f_gmm, [("sv", nrm_tok[c]), ("once", t_gw)] if c == 0
                             else [("sv", nrm_tok[c])], ("pe", 1))
    t_xn = nrm_tok[C - 1]
    t_gl = gmm_tok[C - 1]
    if dbg:
        S_.dma("gp", dbg["d_xn1"].rearrange("(c p) t -> p c t", p=P),
               XN[:].rearrange("p (c t) -> p c t", t=T), [("sv", t_xn)], sem="dbg")
    sA.close()

    sB = ExitStack()
    GSC = sb("GSC", [P, 512], F32, sB)
    W_E = sb("W_E", [P, 512], F32, sB)
    FLAT = sb("FLATB", [P, 512], F32, sB)
    FLATBF = sb("FLATBF", [P, 512], BF16, sB)
    SOFTM = sb("SOFTM", [P, 512], BF16, sB)
    SOFTG = [sb(f"SOFTG{g}", [8, T], BF16, sB) for g in range(G8)]
    FLATG = [sb(f"FLATG{g}", [8, T], BF16, sB) for g in range(G8)]
    WET = sb("WET", [64, T], BF16, sB)
    FLT_J = sb("FLT_J", [64, T], BF16, sB)
    CKF = sb("CKF", [P, KC4 * T], F32, sB)
    KPE32 = sb("KPE32", [64, T], F32, sB)
    WPE = sb("WPE", [P, C * 64], BF16, sB)
    SQK = sb("SQK", [P, 2 * T], BF16, sB)
    CKN = sb("CKN", [P, KC4 * T], BF16, sB)
    KPER = sb("KPER", [64, T], BF16, sB)
    CKVF = sb("CKVF", [P, KC4 * SL], BF16, sB)
    KCH = sb("KCH", [P, SL], BF16, sB)

    t_gsc = S_.step("sv", lambda v: v.tensor_copy(GSC[:], PS[6][:, 0:512]),
                    [("pe", t_gl), ("sv", t_xn)], ("sv", 1))
    we_toks = []
    for tt in range(NT_Q):
        sl = GSC[:, tt * E64:(tt + 1) * E64]
        t_mx = S_.step("sv", lambda v, sl=sl: v.reduce_max(rz128[:], sl, AX.X, negate=True),
                       [("sv", t_gsc)] if tt == 0 else [], ("sv", 1))
        t_ex = S_.step("ss", lambda s, sl=sl: s.activation(
            sl, sl, AF.Exp, bias=rz128[:], accum_out=sg8a[:, 0:1]),
            [("sv", t_mx)], ("ss", 1))
        t_rc = S_.step("sv", lambda v: v.reciprocal(rz128[:], sg8a[:, 0:1]),
                       [("ss", t_ex)], ("sv", 1))
        t_sm = S_.step("gp", lambda g, sl=sl: g.tensor_scalar(sl, sl, rz128[:], None, ALU.mult),
                       [("sv", t_rc)], ("sg", 1))
        t_m8 = S_.step("sv", lambda v, sl=sl: v.max(mx8[:], sl), [("sg", t_sm)], ("sv", 1))
        def f_we(g, sl=sl, tt=tt):
            g.tensor_scalar(seg1[:], sl, mx8[:, 7:8], None, ALU.is_ge)
            return g.tensor_tensor(W_E[:, tt * E64:(tt + 1) * E64], seg1[:], sl, ALU.mult)
        we_toks.append(S_.step("gp", f_we, [("sv", t_m8)], ("sg", 1)))

    t_if = S_.step("sv", lambda v: v.tensor_copy(invf[:], invi[:]), [("once", t_once)], ("sv", 1))
    t_ib = S_.step("pe", lambda t: t.matmul(PS[0][0:64, 0:64], ones_f[0:1, 0:64], invf[:],
                                            start=True, stop=True),
                   [("sv", t_if), ("sv", S_.cnt["sv"]), ("ss", S_.cnt["ss"])], ("pe", 1))
    t_pm = S_.step("sv", lambda v: v.tensor_scalar(Pmat[:], PS[0][0:64, 0:64], iota64[:],
                                                   None, ALU.is_equal),
                   [("pe", t_ib)], ("sv", 1))

    t_web = S_.step("ss", lambda s: s.activation(FLATBF[:], W_E[:], AF.Copy),
                    [("sg", we_toks[-1])], ("ss", 1))
    wet_tok = None
    for r in range(2):
        def f_tw(t, r=r):
            mm = None
            for i in range(4):
                tt = r * 4 + i
                mm = t.transpose(PT[0:64, i * P:(i + 1) * P],
                                 FLATBF[:, tt * E64:(tt + 1) * E64], ident[:])
            return mm
        w = [("ss", t_web)] + ([("sv", S_.last_pt_evac)] if S_.last_pt_evac else [])
        if wet_tok:
            w = [("sv", wet_tok)]
        tp = S_.step("pe", f_tw, w, ("pe", 1))
        wet_tok = S_.step("sv", lambda v, r=r: v.tensor_copy(
            WET[0:64, r * 4 * P:(r + 1) * 4 * P], PT[0:64, 0:4 * P]), [("pe", tp)], ("sv", 1))
    S_.last_pt_evac = wet_tok

    def f_flat(t):
        t.matmul(PS[1][0:64, 0:512], Pmat[:], WET[0:64, 0:512], start=True, stop=True)
        return t.matmul(PS[2][0:64, 0:512], Pmat[:], WET[0:64, 512:1024], start=True, stop=True)
    t_ft = S_.step("pe", f_flat, [("sv", wet_tok), ("sv", t_pm)], ("pe", 1))
    t_fj = S_.step("sv", lambda v: (v.tensor_copy(FLT_J[0:64, 0:512], PS[1][0:64, 0:512]),
                                    v.tensor_copy(FLT_J[0:64, 512:1024], PS[2][0:64, 0:512]))[-1],
                   [("pe", t_ft)], ("sv", 1))
    flat_tok = None
    for r in range(2):
        def f_tf(t, r=r):
            mm = None
            for i in range(4):
                tt = r * 4 + i
                mm = t.transpose(PT[:, i * E64:(i + 1) * E64],
                                 FLT_J[0:64, tt * P:(tt + 1) * P], ident[0:64, 0:64])
            return mm
        w = [("sv", t_fj), ("sv", S_.last_pt_evac)]
        if flat_tok:
            w = [("sv", flat_tok)]
        tp = S_.step("pe", f_tf, w, ("pe", 1))
        flat_tok = S_.step("sv", lambda v, r=r: v.tensor_copy(
            FLAT[:, r * 4 * E64:(r + 1) * 4 * E64].rearrange("p (i e) -> p i e", e=E64),
            PT[:, 0:4 * E64].rearrange("p (i e) -> p i e", e=E64)), [("pe", tp)], ("sv", 1))
    S_.last_pt_evac = flat_tok

    soft_toks = []
    for tt in range(NT_Q):
        fl = FLAT[:, tt * E64:(tt + 1) * E64]
        def f_msk(v, fl=fl):
            v.tensor_scalar(seg1[:], fl, 0.0, None, ALU.is_equal)
            v.scalar_tensor_tensor(seg1[:], seg1[:], -1e9, fl, ALU.mult, ALU.add)
            return v.tensor_reduce(sg8a[:], seg1[:].rearrange("p (g s) -> p g s", s=8),
                                   AX.X, ALU.max)
        t_mk = S_.step("sv", f_msk, [("sv", flat_tok)] if tt == 0 else [], ("sv", 1))
        def f_sub(g):
            return g.tensor_tensor(seg2[:].rearrange("p (g s) -> p g s", s=8),
                                   seg1[:].rearrange("p (g s) -> p g s", s=8),
                                   sg8a[:].unsqueeze(2).broadcast_to([P, 8, 8]),
                                   ALU.subtract)
        t_sb2 = S_.step("gp", f_sub, [("sv", t_mk)], ("sg", 1))
        t_ex2 = S_.step("ss", lambda s: s.activation(seg2[:], seg2[:], AF.Exp),
                        [("sg", t_sb2)], ("ss", 1))
        t_rd2 = S_.step("sv", lambda v: v.tensor_reduce(
            sg8a[:], seg2[:].rearrange("p (g s) -> p g s", s=8), AX.X, ALU.add),
            [("ss", t_ex2)], ("sv", 1))
        t_cp2 = S_.step("gp", lambda g: g.tensor_copy(mx8[:], sg8a[:]),
                        [("sv", t_rd2)], ("sg", 1))
        t_ss2 = S_.step("sv", lambda v: v.reciprocal(sg8b[:], mx8[:]),
                        [("sg", t_cp2)], ("sv", 1))
        def f_div(g, tt=tt):
            return g.tensor_tensor(SOFTM[:, tt * E64:(tt + 1) * E64].rearrange(
                "p (g s) -> p g s", s=8),
                seg2[:].rearrange("p (g s) -> p g s", s=8),
                sg8b[:].unsqueeze(2).broadcast_to([P, 8, 8]), ALU.mult)
        soft_toks.append(S_.step("gp", f_div, [("sv", t_ss2)], ("sg", 1)))

    tg_tok = None
    for tt in range(NT_Q):
        def f_tg(t, tt=tt):
            mm = None
            for g in range(G8):
                mm = t.transpose(PT[0:8, g * P:(g + 1) * P],
                                 SOFTM[:, tt * E64 + g * 8: tt * E64 + (g + 1) * 8], ident[:])
            return mm
        w = [("sg", soft_toks[tt]), ("sv", S_.last_pt_evac)]
        if tg_tok:
            w = [("sg", soft_toks[tt]), ("sv", tg_tok)]
        tp = S_.step("pe", f_tg, w, ("pe", 1))
        def f_eg(v, tt=tt):
            last = None
            for g in range(G8):
                last = v.tensor_copy(SOFTG[g][:, tt * P:(tt + 1) * P], PT[0:8, g * P:(g + 1) * P])
            return last
        tg_tok = S_.step("sv", f_eg, [("pe", tp)], ("sv", 1))
        t_fb = S_.step("ss", lambda s, tt=tt: s.activation(
            FLATBF[:, tt * E64:(tt + 1) * E64], FLAT[:, tt * E64:(tt + 1) * E64], AF.Copy),
            [("sv", tg_tok)] if tt == 0 else [], ("ss", 1))
        def f_tg2(t, tt=tt):
            mm = None
            for g in range(G8):
                mm = t.transpose(PT[0:8, g * P:(g + 1) * P],
                                 FLATBF[:, tt * E64 + g * 8: tt * E64 + (g + 1) * 8], ident[:])
            return mm
        tp2 = S_.step("pe", f_tg2, [("ss", t_fb), ("sv", tg_tok)], ("pe", 1))
        def f_eg2(v, tt=tt):
            last = None
            for g in range(G8):
                last = v.tensor_copy(FLATG[g][:, tt * P:(tt + 1) * P], PT[0:8, g * P:(g + 1) * P])
            return last
        tg_tok = S_.step("sv", f_eg2, [("pe", tp2)], ("sv", 1))
    S_.last_pt_evac = tg_tok

    if dbg:
        S_.dma("gp", dbg["d_we"], W_E[:], [("sg", we_toks[-1])], sem="dbg")
        S_.dma("gp", dbg["d_soft"], SOFTM[:], [("sv", tg_tok)], sem="dbg")
        S_.dma("gp", dbg["d_flat"], FLAT[:], [("sv", tg_tok)], sem="dbg")
        S_.step("sv", lambda v: v.engine_nop(), [("dbg", S_.cnt["dbg"])], ("sv", 1))
    sg_st = None
    for g in range(G8):
        S_.dma("sy", io["softg_d"].ap()[g * 8:(g + 1) * 8, :], SOFTG[g][:],
               [("sv", tg_tok)] if g == 0 else [], sem="sgst")
        sg_st = S_.dma("sy", io["flatg_d"].ap()[g * 8:(g + 1) * 8, :], FLATG[g][:], sem="sgst")

    if stop_after == "gate":
        sB.close()
        allw = [(nm, S_.cnt[nm]) for nm in S_.cnt
                if nm not in ("pe", "ss", "sv", "sg") and S_.cnt[nm] > 0]
        S_.step("sy", lambda sy: sy.nop(), allw)
        return
    kva_ld = {}
    for j in range(2):
        kva_ld[j] = S_.dma("gp", WB[j % 2][:], wkva3[j], [("pe", S_.cnt["pe"])] if j == 0 else [],
                           sem=f"wb{j % 2}")
    t_wpe = S_.dma("gp", WPE[:], io["wkva_pe"], sem="once")
    ck_ev = {}
    for j in range(KC4):
        pa, pb = (PS[0], PS[1]) if j % 2 == 0 else (PS[2], PS[3])
        w = [(f"wb{j % 2}", kva_ld[j])] + ([("sv", ck_ev[j - 2])] if j >= 2
                                           else [("sv", S_.cnt["sv"]), ("ss", S_.cnt["ss"])])
        tj = fm_proj_chunk(S_, WB[j % 2], XN, 0, pa, pb, w)
        if j + 2 < KC4:
            kva_ld[j + 2] = S_.dma("gp", WB[j % 2][:], wkva3[j + 2], [("pe", tj)], sem=f"wb{j % 2}")
        def f_ev(v, j=j):
            pa, pb = (PS[0], PS[1]) if j % 2 == 0 else (PS[2], PS[3])
            v.tensor_copy(CKF[:, j * T:j * T + 512], pa[:, 0:512])
            return v.tensor_copy(CKF[:, j * T + 512:(j + 1) * T], pb[:, 0:512])
        ck_ev[j] = S_.step("sv", f_ev, [("pe", tj)], ("sv", 1))
    def f_kpe(t):
        mm = None
        for c in range(C):
            for s2, pp in ((0, PS[2]), (1, PS[3])):
                mm = t.matmul(pp[0:64, 0:512], WPE[:, c * 64:(c + 1) * 64],
                              XN[:, c * T + s2 * 512: c * T + (s2 + 1) * 512],
                              start=(c == 0), stop=(c == C - 1))
        return mm
    t_kpm = S_.step("pe", f_kpe, [("once", t_wpe), ("sv", ck_ev[KC4 - 1])], ("pe", 1))
    t_kpev = S_.step("sv", lambda v: (v.tensor_copy(KPE32[0:64, 0:512], PS[2][0:64, 0:512]),
                                      v.tensor_copy(KPE32[0:64, 512:1024], PS[3][0:64, 0:512]))[-1],
                     [("pe", t_kpm)], ("sv", 1))

    rms_sq = {}
    rms_mm = {}
    for j in range(KC4):
        wsq = [("sv", ck_ev[j])]
        if j >= 2:
            wsq.append(("pe", rms_mm[j - 2]))
        rms_sq[j] = S_.step("ss", lambda s, j=j: s.activation(
            SQK[:, (j % 2) * T:(j % 2 + 1) * T], CKF[:, j * T:(j + 1) * T], AF.Square),
            wsq, ("ss", 1))
        def f_rmm(t, j=j):
            mm = None
            for s2, pp in ((0, PS[4]), (1, PS[5])):
                mm = t.matmul(pp[0:1, 0:512], ones_b[:],
                              SQK[:, (j % 2) * T + s2 * 512:(j % 2) * T + (s2 + 1) * 512],
                              start=(j == 0), stop=(j == KC4 - 1))
            return mm
        rms_mm[j] = S_.step("pe", f_rmm, [("ss", rms_sq[j])], ("pe", 1))
    def f_rvar(v):
        v.tensor_scalar(rowA[0:1, 0:512], PS[4][0:1, 0:512], 1.0 / KVR, None, ALU.mult)
        return v.tensor_scalar(rowA[0:1, 512:1024], PS[5][0:1, 0:512], 1.0 / KVR, None, ALU.mult)
    t_rv = S_.step("sv", f_rvar, [("pe", rms_mm[KC4 - 1])], ("sv", 1))
    t_rsd = S_.step("ss", lambda s: s.activation(rowA[0:1, :], rowA[0:1, :], AF.Sqrt, bias=eps2T[:]),
                    [("sv", t_rv)], ("ss", 1))
    t_rrc = S_.step("sv", lambda v: v.reciprocal(rowA[0:1, :], rowA[0:1, :]),
                    [("ss", t_rsd)], ("sv", 1))
    t_rbc = None
    for s2 in range(2):
        tk = S_.step("pe", lambda t, s2=s2: t.matmul(
            PS[6][:, 0:512], ones_f[:], rowA[0:1, s2 * 512:(s2 + 1) * 512], start=True, stop=True),
            [("sv", t_rrc)] if s2 == 0 else [("sv", t_rbc)], ("pe", 1))
        t_rbc = S_.step("sv", lambda v, s2=s2: v.tensor_copy(
            bcR[:, s2 * 512:(s2 + 1) * 512], PS[6][:, 0:512]), [("pe", tk)], ("sv", 1))
    ckn_tok = None
    for j in range(KC4):
        def f_ckn(v, j=j):
            v.tensor_tensor(scrA[:], CKF[:, j * T:(j + 1) * T], bcR[:], ALU.mult)
            return v.tensor_scalar(CKN[:, j * T:(j + 1) * T], scrA[:], kvnsw[:, j:j + 1],
                                   None, ALU.mult)
        ckn_tok = S_.step("sv", f_ckn, [("sv", t_rbc)] if j == 0 else [], ("sv", 1))
    t_kpb = S_.step("ss", lambda s: s.copy(KPER[0:64, :], KPE32[0:64, :]),
                    [("sv", t_kpev)], ("ss", 1))
    t_rotk = S_.step("pe", lambda t: (t.matmul(PS[2][0:64, 0:512], R128B[0:64, 0:64],
                                               KPER[0:64, 0:512], start=True, stop=True),
                                      t.matmul(PS[3][0:64, 0:512], R128B[0:64, 0:64],
                                               KPER[0:64, 512:1024], start=True, stop=True))[-1],
                     [("ss", t_kpb)], ("pe", 1))
    def f_kper(v):
        v.tensor_tensor(scrA[0:64, :], KPE32[0:64, :], cos1[0:64, :], ALU.mult)
        v.tensor_tensor(scrB[0:64, 0:512], PS[2][0:64, 0:512], sin1[0:64, 0:512], ALU.mult)
        v.tensor_tensor(scrB[0:64, 512:1024], PS[3][0:64, 0:512], sin1[0:64, 512:1024], ALU.mult)
        return v.tensor_tensor(KPER[0:64, :], scrA[0:64, :], scrB[0:64, :], ALU.add)
    t_kper = S_.step("sv", f_kper, [("pe", t_rotk)], ("sv", 1))

    S_.sem("ckl"); S_.sem("kpl"); S_.sem("ckst")
    t_ckst = S_.dma("sy", io["ctb_ckv"].ap().rearrange("(c p) t -> p c t", p=P),
                    CKN[:].rearrange("p (c t) -> p c t", t=T),
                    [("sv", ckn_tok)], sem="ckst")
    t_kpst = S_.dma("sy", io["ctb_kpe"].ap(), KPER[0:64, :], [("sv", t_kper)], sem="ckst")
    t_agc = S_.step("gp", lambda g: g.collective_compute(
        "AllGather", ALU.bypass, replica_groups=RG_PAIRS,
        ins=[io["ctb_ckv"].ap().opt()], outs=[io["gth_ckv"].ap().opt()]),
        [("ckst", t_kpst)], ("cc0", 1))
    t_agp = S_.step("gp", lambda g: g.collective_compute(
        "AllGather", ALU.bypass, replica_groups=RG_PAIRS,
        ins=[io["ctb_kpe"].ap().opt()], outs=[io["gth_kpe"].ap().opt()]),
        (), ("cc1", 1))
    gc = io["gth_ckv"].ap()
    gpe = io["gth_kpe"].ap()
    for c in range(KC4):
        S_.dma("gp", CKVF[:, c * SL:c * SL + T], gc[c * P:(c + 1) * P, :],
               [("cc0", t_agc)] if c == 0 else [], sem="ckl")
        ck_last = S_.dma("gp", CKVF[:, c * SL + T:(c + 1) * SL],
                         gc[KVR + c * P:KVR + (c + 1) * P, :], sem="ckl")
    S_.dma("gp", KPEF[0:64, 0:T], gpe[0:64, :], [("cc1", t_agp)], sem="kpl")
    S_.dma("gp", KPEF[0:64, T:S], gpe[64:128, :], sem="kpl")
    S_.dma("gp", KPEF[64:128, 0:T], gpe[0:64, :], sem="kpl")
    t_kpef = S_.dma("gp", KPEF[64:128, T:S], gpe[64:128, :], sem="kpl")

    def kvb_proj(blocks3, dram_out, name):
        mmrec = {}
        stp = None
        evp = None
        for j in range(C):
            sm = f"wb{j % 2}"
            tld = S_.dma("gp", WB[j % 2][:, 0:KC4 * P], blocks3[j],
                         [("pe", S_.cnt["pe"])] if j < 2 else [("pe", mmrec[j - 2])], sem=sm)
            w = [(sm, tld), ("ckl", ck_last)]
            if evp is not None:
                w.append(("ss", evp))
            else:
                w += [("sv", S_.cnt["sv"]), ("ss", S_.cnt["ss"])]
            def f_k(t, j=j):
                mm = None
                for c in range(KC4):
                    for s4 in range(NSL):
                        pp = PS[s4]
                        mm = t.matmul(pp[:, 0:512], WB[j % 2][:, c * P:(c + 1) * P],
                                      CKVF[:, c * SL + s4 * 512: c * SL + (s4 + 1) * 512],
                                      start=(c == 0), stop=(c == KC4 - 1))
                return mm
            tmm = S_.step("pe", f_k, w, ("pe", 1))
            mmrec[j] = tmm
            def f_ev(s):
                last = None
                for s4 in range(NSL):
                    last = s.activation(KCH[:, s4 * 512:(s4 + 1) * 512], PS[s4][:, 0:512], AF.Copy)
                return last
            wv_ = [("pe", tmm)]
            if stp is not None:
                wv_.append((name, stp))
            evp = S_.step("ss", f_ev, wv_, ("ss", 1))
            stp = S_.dma("sy", dram_out[j * P:(j + 1) * P, :], KCH[:], [("ss", evp)], sem=name)
        return stp

    if stop_after == "kva":
        sB.close()
        allw = [(nm, S_.cnt[nm]) for nm in S_.cnt
                if nm not in ("pe", "ss", "sv", "sg") and S_.cnt[nm] > 0]
        S_.step("sy", lambda sy: sy.nop(), allw)
        return
    t_knst = kvb_proj(kb3, io["kn_dram"].ap(), "knst")
    t_vnst = kvb_proj(vb3, io["vn_dram"].ap(), "vnst")
    sB.close()
    sAB.close()
    if stop_after == "kvb":
        allw = [(nm, S_.cnt[nm]) for nm in S_.cnt
                if nm not in ("pe", "ss", "sv", "sg") and S_.cnt[nm] > 0]
        S_.step("sy", lambda sy: sy.nop(), allw)
        return

    scope2 = ExitStack()
    COMB = sb("COMB1", [P, NH * T], BF16, scope2)
    KHB = [sb("KH10", [P, SL], BF16, scope2), sb("KH11", [P, SL], BF16, scope2)]
    VFH = [sb("VF10", [P, SL], BF16, scope2), sb("VF11", [P, SL], BF16, scope2)]
    VTH = [sb("VT10", [P, SL], BF16, scope2), sb("VT11", [P, SL], BF16, scope2)]
    QNB = [sb("QN0", [P, T], BF16, scope2), sb("QN1", [P, T], BF16, scope2)]
    QPE = sb("QPE", [P, T], BF16, scope2)
    QPEB = sb("QPEB", [P, T], BF16, scope2)
    QPW = sb("QPW", [P, C * P], BF16, scope2)
    PTB = sb("PTB1", [P, 28 * 512], BF16, scope2)
    MSKF = sb("MSKF", [P, 24 * 512], BF16, scope2)
    rz_sb = sb("rz1", [1, 512], F32, scope2)
    bz_sb = sb("bz1", [P, 512], F32, scope2)
    XCH = [sb("XCH1a", [P, T], F32, scope2), sb("XCH1b", [P, T], F32, scope2)]

    gk = io["kn_dram"].ap()
    gv = io["vn_dram"].ap()
    qld = {}
    khld = {}
    vfld = {}
    for pos in range(2):
        qld[pos] = (f"wb{pos}", S_.dma("gp", WB[pos][:], wq3[pos],
                    [("pe", S_.cnt["pe"]), ("knst", t_knst), ("vnst", t_vnst),
                     ("sgst", sg_st)] if pos == 0 else [], sem=f"wb{pos}"))
        khld[pos] = (f"kh{pos}", S_.dma("gp", KHB[pos][:], gk[pos * P:(pos + 1) * P, 0:SL],
                     sem=f"kh{pos}"))
        vfld[pos] = (f"vf{pos}", S_.dma("gp", VFH[pos][:], gv[pos * P:(pos + 1) * P, 0:SL],
                     sem=f"vf{pos}"))
    t_mskf = S_.dma("gp", MSKF[:, 0:12 * 512].rearrange("p (m n) -> p m n", n=512),
                    io["masks"].rearrange("(m p) n -> p m n", p=P)[:, 0:12, :], sem="once")
    t_mskf = S_.dma("gp", MSKF[:, 12 * 512:24 * 512].rearrange("p (m n) -> p m n", n=512),
                    io["masks"].rearrange("(m p) n -> p m n", p=P)[:, 20:32, :], sem="once")
    qpe_ld = {0: S_.dma("gp", QPW[:], wq3[16], sem="qpw")}

    qevac = {}
    kfree = {}
    comb_toks = []
    zav1 = {}
    ex1 = {}
    pend1 = []
    s1 = [0]
    last_qpe = [None]

    def l1_tail(d):
        ii_, sig_, par_, ntl_ = d["ii"], d["sig"], d["par"], d["ntl"]
        po = 0 if sig_ == 0 else 12 * 512

        def f_zav(t, ii_=ii_, po=po, ntl_=ntl_):
            for jt in range(ntl_):
                t.matmul(PS[5][0:1, 0:512], ones_b[:], PTB[:, po + jt * 512:po + (jt + 1) * 512],
                         start=(jt == 0), stop=(jt == ntl_ - 1))
            mm = None
            for jt in range(ntl_):
                mm = t.matmul(PS[4][:, 0:512], VTH[ii_ % 2][:, jt * P:(jt + 1) * P],
                              PTB[:, po + jt * 512:po + (jt + 1) * 512],
                              start=(jt == 0), stop=(jt == ntl_ - 1))
            return mm
        wz = [("sv", d["msk"]), ("sv", d["vth"])]
        if last_qpe[0] is not None:
            wz.append(("sv", last_qpe[0]))
        wz.append(("sv", comb_toks[-1]) if comb_toks else ("sv", S_.cnt["sv"]))
        tz = S_.step("pe", f_zav, wz, ("pe", 1))
        zav1[(ii_, sig_)] = tz
        t_rcp = S_.step("sv", lambda v: v.reciprocal(rz_sb[0:1, :], PS[5][0:1, 0:512]),
                        [("pe", tz)], ("sv", 1))
        t_bz = S_.step("pe", lambda t: t.matmul(PS[6][:, 0:512], ones_f[:], rz_sb[0:1, :],
                                                start=True, stop=True),
                       [("sv", t_rcp)], ("pe", 1))
        t_bzc = S_.step("sv", lambda v: v.tensor_copy(bz_sb[:], PS[6][:, 0:512]),
                        [("pe", t_bz)], ("sv", 1))

        def f_out(v, ii_=ii_, sig_=sig_):
            return v.tensor_tensor(COMB[:, ii_ * T + sig_ * 512: ii_ * T + (sig_ + 1) * 512],
                                   PS[4][:, 0:512], bz_sb[:], ALU.mult)
        comb_toks.append(S_.step("sv", f_out, [("sv", t_bzc)], ("sv", 1)))

    for ip in range(8):
        iA, iB = 2 * ip, 2 * ip + 1
        wA = [qld[iA]] + ([("ss", qevac[iA - 2])] if iA >= 2
                          else [("ss", S_.cnt["ss"]), ("sv", S_.cnt["sv"])])
        if s1[0] >= 2:
            wA.append(("ss", ex1[s1[0] - 2]))
        tA = fm_proj_chunk(S_, WB[iA % 2], XN, 0, PS[0], PS[1], wA)
        wB = [qld[iB]] + ([("ss", qevac[iB - 2])] if iB >= 2 else [])
        if s1[0] >= 1:
            wB.append(("ss", ex1[s1[0] - 1]))
        tB = fm_proj_chunk(S_, WB[iB % 2], XN, 0, PS[2], PS[3], wB)
        for nx in (iA + 2, iB + 2):
            if nx < 16 and nx not in qld:
                qld[nx] = (f"wb{nx % 2}", S_.dma("gp", WB[nx % 2][:], wq3[nx],
                           [("pe", tA if nx % 2 == 0 else tB)], sem=f"wb{nx % 2}"))

        def f_qn(s, tB=tB):
            s.activation(QNB[0][:, 0:512], PS[0][:, 0:512], AF.Copy)
            s.activation(QNB[0][:, 512:1024], PS[1][:, 0:512], AF.Copy)
            s.wait_ge(S_.sems["pe"], tB)
            s.activation(QNB[1][:, 0:512], PS[2][:, 0:512], AF.Copy)
            return s.activation(QNB[1][:, 512:1024], PS[3][:, 0:512], AF.Copy)
        wq_ = [("pe", tA)]
        if comb_toks:
            wq_.append(("sv", comb_toks[-1]))
        qevac[iA] = qevac[iB] = S_.step("ss", f_qn, wq_, ("ss", 1))

        tvthA = transpose16(S_, VFH[iA % 2], 0,
                            lambda r, iA=iA: VTH[iA % 2][:, r * 8 * P:(r + 1) * 8 * P],
                            [vfld[iA]] + ([("pe", zav1[(iA - 2, 1)])] if iA >= 2 else []),
                            ident, nt=SL // P)

        wqp = [("qpw", qpe_ld[ip])]
        if comb_toks:
            wqp.append(("sv", comb_toks[-1]))
        tqp = fm_proj_chunk(S_, QPW, XN, 0, PS[4], PS[5], wqp)
        if ip + 1 < 8:
            qpe_ld[ip + 1] = S_.dma("gp", QPW[:], wq3[17 + ip], [("pe", tqp)], sem="qpw")
        t_qev = S_.step("sv", lambda v: (v.tensor_copy(scrA[:, 0:512], PS[4][:, 0:512]),
                                         v.tensor_copy(scrA[:, 512:1024], PS[5][:, 0:512]))[-1],
                        [("pe", tqp)], ("sv", 1))
        t_qpb = S_.step("ss", lambda s: s.copy(QPEB[:], scrA[:]), [("sv", t_qev)], ("ss", 1))
        t_rotq = S_.step("pe", lambda t: (t.matmul(PS[4][:, 0:512], R128B[:], QPEB[:, 0:512],
                                                   start=True, stop=True),
                                          t.matmul(PS[5][:, 0:512], R128B[:], QPEB[:, 512:1024],
                                                   start=True, stop=True))[-1],
                         [("ss", t_qpb)], ("pe", 1))

        def f_qper(v):
            v.tensor_tensor(scrB[:], scrA[:], cos1[:], ALU.mult)
            v.tensor_tensor(scrC[:, 0:512], PS[4][:, 0:512], sin1[:, 0:512], ALU.mult)
            v.tensor_tensor(scrC[:, 512:1024], PS[5][:, 0:512], sin1[:, 512:1024], ALU.mult)
            return v.tensor_tensor(QPE[:], scrB[:], scrC[:], ALU.add)
        t_qpe = S_.step("sv", f_qper, [("pe", t_rotq)], ("sv", 1))
        last_qpe[0] = t_qpe

        for which in range(2):
            ii = 2 * ip + which
            kbuf = KHB[ii % 2]
            if which == 0:
                t_vth = tvthA
            else:
                t_vth = transpose16(S_, VFH[ii % 2], 0,
                                    lambda r, ii=ii: VTH[ii % 2][:, r * 8 * P:(r + 1) * 8 * P],
                                    [vfld[ii]] + ([("pe", zav1[(ii - 2, 1)])] if ii >= 2 else []),
                                    ident, nt=SL // P)
            for sig in range(2):
                par = s1[0] % 2
                ntl = 12 if sig == 0 else C
                po = 0 if sig == 0 else 12 * 512
                mtiles = range(0, 12) if sig == 0 else range(4, 16)
                ex_toks = []
                sc_toks = []
                for jt in range(ntl):
                    wsc = []
                    if len(sc_toks) >= 2:
                        wsc.append(("ss", ex_toks[-2]))
                    elif s1[0] >= 2:
                        wsc.append(("ss", ex1[s1[0] - 2]))
                    else:
                        wsc.append(("ss", S_.cnt["ss"]))
                    if jt == 0:
                        wsc += [("sv", t_qpe), ("kpl", t_kpef), khld[ii]]

                    def f_sc(t, jt=jt, kbuf=kbuf, which=which, sig=sig, par=par):
                        pp = PS[2 * par + jt % 2]
                        t.matmul(pp[:, 0:512], kbuf[:, jt * P:(jt + 1) * P],
                                 QNB[which][:, sig * 512:(sig + 1) * 512],
                                 start=True, stop=False)
                        return t.matmul(pp[:, 0:512],
                                        KPEF[64 * which:64 * which + 64, jt * P:(jt + 1) * P],
                                        QPE[64 * which:64 * which + 64, sig * 512:(sig + 1) * 512],
                                        start=False, stop=True)
                    tsc = S_.step("pe", f_sc, wsc, ("pe", 1))
                    sc_toks.append(tsc)
                    wex = [("pe", tsc)]
                    ex_toks.append(S_.step("ss", lambda s, jt=jt, po=po, par=par: s.activation(
                        PTB[:, po + jt * 512:po + (jt + 1) * 512],
                        PS[2 * par + jt % 2][:, 0:512], AF.Exp,
                        bias=shiftT[:], scale=1.0 / math.sqrt(192.0)), wex, ("ss", 1)))

                def f_msk(v, po=po, sig=sig, mtiles=mtiles):
                    r = None
                    for jt in mtiles:
                        mc = (jt if sig == 0 else 8 + jt) * 512
                        r = v.tensor_tensor(PTB[:, po + jt * 512:po + (jt + 1) * 512],
                                            PTB[:, po + jt * 512:po + (jt + 1) * 512],
                                            MSKF[:, mc:mc + 512], ALU.mult)
                    return r
                t_msk = S_.step("sv", f_msk, [("ss", ex_toks[-1]), ("once", t_mskf)], ("sv", 1))
                ex1[s1[0]] = ex_toks[-1]
                if pend1:
                    l1_tail(pend1.pop())
                pend1.append(dict(ii=ii, sig=sig, par=par, ntl=ntl, msk=t_msk, vth=t_vth))
                s1[0] += 1
            kfree[ii] = sc_toks[-1]
            nx = ii + 2
            if nx < 16 and nx not in khld:
                khld[nx] = (f"kh{nx % 2}", S_.dma("gp", KHB[nx % 2][:],
                            gk[nx * P:(nx + 1) * P, 0:SL],
                            [("pe", kfree[ii])], sem=f"kh{nx % 2}"))
                vfld[nx] = (f"vf{nx % 2}", S_.dma("gp", VFH[nx % 2][:],
                            gv[nx * P:(nx + 1) * P, 0:SL],
                            [("sv", t_vth)], sem=f"vf{nx % 2}"))
    while pend1:
        l1_tail(pend1.pop())

    if dbg:
        S_.dma("gp", dbg["d_comb1"].rearrange("(c p) t -> p c t", p=P),
               COMB[:].rearrange("p (c t) -> p c t", t=T), [("sv", comb_toks[-1])], sem="dbg")

    old = {0: ("wb0", S_.dma("gp", WB[0][:], wo3[0], [("pe", S_.cnt["pe"])], sem="wb0"))}
    old[1] = ("wb1", S_.dma("gp", WB[1][:], wo3[1], sem="wb1"))
    xa = io["xacc"].ap()
    xch = {0: S_.dma("sy", XCH[0][:], io["x_in"][0:P, :], sem="xch"),
           1: S_.dma("sy", XCH[1][:], io["x_in"][P:2 * P, :], sem="xch")}
    xadd = {}
    x1st = {}
    for j in range(C):
        pa, pb = (PS[0], PS[1]) if j % 2 == 0 else (PS[2], PS[3])
        w = [old[j], ("sv", comb_toks[-1])] + ([("sv", xadd[j - 2])] if j >= 2
                                               else [("ss", S_.cnt["ss"])])
        tj = fm_proj_chunk(S_, WB[j % 2], COMB, 0, pa, pb, w)
        nx = j + 2
        if nx < C:
            old[nx] = (f"wb{nx % 2}", S_.dma("gp", WB[nx % 2][:], wo3[nx], [("pe", tj)],
                       sem=f"wb{nx % 2}"))
        def f_xadd(v, j=j, pa=pa, pb=pb):
            xc = XCH[j % 2]
            v.tensor_tensor(xc[:, 0:512], xc[:, 0:512], pa[:, 0:512], ALU.add)
            return v.tensor_tensor(xc[:, 512:1024], xc[:, 512:1024], pb[:, 0:512], ALU.add)
        xadd[j] = S_.step("sv", f_xadd, [("pe", tj), ("xch", xch[j])], ("sv", 1))
        x1st[j] = S_.dma("sy", xa[j * P:(j + 1) * P, :], XCH[j % 2][:], [("sv", xadd[j])],
                         sem="x2st")
        nx = j + 2
        if nx < C:
            xch[nx] = S_.dma("sy", XCH[nx % 2][:], io["x_in"][nx * P:(nx + 1) * P, :],
                             [("x2st", x1st[j])], sem="xch")
    scope2.close()
    if stop_after == "wo":
        allw = [(nm, S_.cnt[nm]) for nm in S_.cnt
                if nm not in ("pe", "ss", "sv", "sg") and S_.cnt[nm] > 0]
        S_.step("sy", lambda sy: sy.nop(), allw)
        return

    scope3 = ExitStack()
    XG = sb("XG", [P, C * T], BF16, scope3)
    ABUF = sb("ABUF", [P, IC * T], BF16, scope3)
    SBC = sb("SBC", [P, T], F32, scope3)
    GW = [sb("GWa", [P, C * P], BF16, scope3), sb("GWb", [P, C * P], BF16, scope3)]
    UW = [sb("UWa", [P, C * P], BF16, scope3), sb("UWb", [P, C * P], BF16, scope3)]
    DW = [sb("DWa", [P, IC * P], BF16, scope3), sb("DWb", [P, IC * P], BF16, scope3)]
    XC2 = [sb("XC2a", [P, T], F32, scope3), sb("XC2b", [P, T], F32, scope3)]
    MSC = sb("MSC", [P, T], F32, scope3)
    MASKG = sb("MASKG", [8, D], BF16, scope3)
    SGB = sb("SGB", [8, T], BF16, scope3)
    FGB = sb("FGB", [8, T], BF16, scope3)

    x2st_total = x1st[C - 1]
    sgd = io["softg_d"].ap()
    fgd = io["flatg_d"].ap()
    mup = io["maskup"]
    gld = {}
    uld = {}
    dld = {}
    rmw_prev = None
    for g in range(G8):
        alias_w = [("x2st", x2st_total), ("sgst", sg_st)] if g == 0 else []
        t_sgl = S_.dma("gp", SGB[:], sgd[g * 8:(g + 1) * 8, :],
                       alias_w + ([("pe", S_.cnt["pe"])] if g > 0 else []), sem="sgl")
        t_fgl = S_.dma("gp", FGB[:], fgd[g * 8:(g + 1) * 8, :], sem="sgl")
        t_mgl = S_.dma("gp", MASKG[:], mup[g * 8:(g + 1) * 8, :], sem="sgl")
        def f_sbc(t, g=g):
            t.matmul(PS[6][:, 0:512], ones8[:], FGB[:, 0:512], start=True, stop=True)
            return t.matmul(PS[5][:, 0:512], ones8[:], FGB[:, 512:1024], start=True, stop=True)
        wsb = [("sgl", t_mgl), ("sv", S_.cnt["sv"]), ("ss", S_.cnt["ss"])]
        t_sbc = S_.step("pe", f_sbc, wsb, ("pe", 1))
        t_sbe = S_.step("sv", lambda v: (v.tensor_copy(SBC[:, 0:512], PS[6][:, 0:512]),
                                         v.tensor_copy(SBC[:, 512:1024], PS[5][:, 0:512]))[-1],
                        [("pe", t_sbc)], ("sv", 1))
        xg_toks = {}
        for j in range(C):
            pda = PS[0] if j % 2 == 0 else PS[2]
            pdb = PS[1] if j % 2 == 0 else PS[3]
            def f_dl(t, g=g, j=j, pda=pda, pdb=pdb):
                mm = None
                for s2, pp in ((0, pda), (1, pdb)):
                    mm = t.matmul(pp[:, 0:512], MASKG[:, j * P:(j + 1) * P],
                                  SGB[:, s2 * 512:(s2 + 1) * 512],
                                  start=True, stop=True)
                return mm
            wdl = [("sv", xg_toks[j - 2])] if j >= 2 else [("sv", t_sbe)]
            tdl = S_.step("pe", f_dl, wdl, ("pe", 1))
            def f_xg(v, j=j, pda=pda, pdb=pdb):
                v.tensor_tensor(XG[:, j * T:j * T + 512], pda[:, 0:512],
                                XN[:, j * T:j * T + 512], ALU.add)
                return v.tensor_tensor(XG[:, j * T + 512:(j + 1) * T], pdb[:, 0:512],
                                       XN[:, j * T + 512:(j + 1) * T], ALU.add)
            xg_toks[j] = S_.step("sv", f_xg, [("pe", tdl)], ("sv", 1))
        t_xg = xg_toks[C - 1]
        if stop_after == "moeD":
            break

        a_toks = {}
        for it in range(IC):
            key = (g, it)
            if key not in gld:
                gld[key] = S_.dma("gp", GW[it % 2][:], gpw4[g, it],
                                  [("pe", S_.cnt["pe"])], sem=f"gw{it % 2}")
                uld[key] = S_.dma("gp", UW[it % 2][:], upw4[g, it], sem=f"uw{it % 2}")
            def f_g(t, g=g, it=it):
                mm = None
                for c in range(C):
                    for s2 in range(2):
                        mm = t.matmul(PS[s2][:, 0:512], GW[it % 2][:, c * P:(c + 1) * P],
                                      XG[:, c * T + s2 * 512: c * T + (s2 + 1) * 512],
                                      start=(c == 0), stop=(c == C - 1))
                return mm
            wg = [(f"gw{it % 2}", gld[key]), ("sv", t_xg)]
            if it >= 1:
                wg.append(("ss", tsl))
            tg = S_.step("pe", f_g, wg, ("pe", 1))
            def f_u(t, g=g, it=it):
                mm = None
                for c in range(C):
                    for s2 in range(2):
                        mm = t.matmul(PS[2 + s2][:, 0:512], UW[it % 2][:, c * P:(c + 1) * P],
                                      XG[:, c * T + s2 * 512: c * T + (s2 + 1) * 512],
                                      start=(c == 0), stop=(c == C - 1))
                return mm
            wu = [(f"uw{it % 2}", uld[key])]
            wu.append(("sv", a_toks[it - 1]) if it >= 1 else ("sv", t_xg))
            tu = S_.step("pe", f_u, wu, ("pe", 1))
            nkey = (g, it + 2) if it + 2 < IC else ((g + 1, it + 2 - IC) if g + 1 < G8 else None)
            if nkey is not None and nkey not in gld:
                gld[nkey] = S_.dma("gp", GW[nkey[1] % 2][:], gpw4[nkey[0], nkey[1]],
                                   [("pe", tu)], sem=f"gw{nkey[1] % 2}")
                uld[nkey] = S_.dma("gp", UW[nkey[1] % 2][:], upw4[nkey[0], nkey[1]],
                                   sem=f"uw{nkey[1] % 2}")
            def f_silu(s):
                s.activation(MSC[:, 0:512], PS[0][:, 0:512], AF.Silu)
                return s.activation(MSC[:, 512:1024], PS[1][:, 0:512], AF.Silu)
            wsl = [("pe", tg)] + ([("sv", a_toks[it - 1])] if it >= 1 else [])
            tsl = S_.step("ss", f_silu, wsl, ("ss", 1))
            def f_amul(v, it=it):
                v.tensor_tensor(scrA[:, 0:512], MSC[:, 0:512], PS[2][:, 0:512], ALU.mult)
                v.tensor_tensor(scrA[:, 512:1024], MSC[:, 512:1024], PS[3][:, 0:512], ALU.mult)
                v.tensor_tensor(ABUF[:, it * T:it * T + 512], scrA[:, 0:512],
                                SBC[:, 0:512], ALU.mult)
                return v.tensor_tensor(ABUF[:, it * T + 512:(it + 1) * T], scrA[:, 512:1024],
                                       SBC[:, 512:1024], ALU.mult)
            a_toks[it] = S_.step("sv", f_amul, [("ss", tsl), ("pe", tu)], ("sv", 1))

        if stop_after == "moeA":
            break
        rmw_ev = {}
        rmw_st = {}
        for j in range(C):
            pda = PS[0] if j % 2 == 0 else PS[2]
            pdb = PS[1] if j % 2 == 0 else PS[3]
            psem = f"x2{'ab'[j % 2]}"
            key = (g, j)
            if key not in dld:
                dld[key] = S_.dma("gp", DW[j % 2][:], dnw4[g, j],
                                  [("pe", S_.cnt["pe"])], sem=f"dw{j % 2}")
            if j >= 2:
                wxl = [(psem, rmw_st[j - 2])]
            elif rmw_prev:
                wxl = [(psem, rmw_prev[C - 2 + j])]
            else:
                wxl = [("x2st", x2st_total)]
            t_xl = S_.dma("sy", XC2[j % 2][:], xa[j * P:(j + 1) * P, :], wxl, sem="xch")
            def f_dn(t, g=g, j=j, pda=pda, pdb=pdb):
                mm = None
                for ic in range(IC):
                    for s2, pp in ((0, pda), (1, pdb)):
                        mm = t.matmul(pp[:, 0:512], DW[j % 2][:, ic * P:(ic + 1) * P],
                                      ABUF[:, ic * T + s2 * 512: ic * T + (s2 + 1) * 512],
                                      start=(ic == 0), stop=(ic == IC - 1))
                return mm
            wdn = [(f"dw{j % 2}", dld[key]), ("sv", a_toks[IC - 1])]
            if j >= 2:
                wdn.append(("sv", rmw_ev[j - 2]))
            elif j == 0:
                wdn.append(("ss", tsl))
            tdn = S_.step("pe", f_dn, wdn, ("pe", 1))
            nkey = (g, j + 2) if j + 2 < C else ((g + 1, j + 2 - C) if g + 1 < G8 else None)
            if nkey is not None and nkey not in dld:
                dld[nkey] = S_.dma("gp", DW[nkey[1] % 2][:], dnw4[nkey[0], nkey[1]],
                                   [("pe", tdn)], sem=f"dw{nkey[1] % 2}")
            def f_rmw(v, j=j, pda=pda, pdb=pdb):
                xc = XC2[j % 2]
                v.tensor_tensor(xc[:, 0:512], xc[:, 0:512], pda[:, 0:512], ALU.add)
                return v.tensor_tensor(xc[:, 512:1024], xc[:, 512:1024], pdb[:, 0:512], ALU.add)
            rmw_ev[j] = S_.step("sv", f_rmw, [("pe", tdn), ("xch", t_xl)], ("sv", 1))
            target = io["x2_out"] if g == G8 - 1 else xa
            rmw_st[j] = S_.dma("sy", target[j * P:(j + 1) * P, :], XC2[j % 2][:],
                               [("sv", rmw_ev[j])], sem=psem)
        rmw_prev = rmw_st
        if stop_after == "moe1":
            break
    scope3.close()
    allw = [(nm, S_.cnt[nm]) for nm in S_.cnt
            if nm not in ("pe", "ss", "sv", "sg") and S_.cnt[nm] > 0]
    S_.step("sy", lambda sy: sy.nop(), allw)


def build_l1(dbg=False, stop_after=None):
    nc = bass.Bass("TRN2", target_bir_lowering=False)
    io = {}
    io["x_in"] = nc.dram_tensor("x1_fm", [D, T], F32, kind="ExternalInput").ap()
    io["wq1"] = nc.dram_tensor("wq1_blk", [24 * P, D], F32, kind="ExternalInput").ap()
    io["wkva"] = nc.dram_tensor("wkva_blk", [4 * P, D], F32, kind="ExternalInput").ap()
    io["wkva_pe"] = nc.dram_tensor("wkva_pe", [P, C * 64], F32, kind="ExternalInput").ap()
    io["kvnsw"] = nc.dram_tensor("kvnsw", [P, KC4], F32, kind="ExternalInput").ap()
    io["wkvb_k"] = nc.dram_tensor("wkvb_k", [C * P, KVR], F32, kind="ExternalInput").ap()
    io["wkvb_v"] = nc.dram_tensor("wkvb_v", [C * P, KVR], F32, kind="ExternalInput").ap()
    io["wo1"] = nc.dram_tensor("wo1_blk", [C * P, D], F32, kind="ExternalInput").ap()
    io["gatew"] = nc.dram_tensor("gatew_pm", [P, C * E64], F32, kind="ExternalInput").ap()
    io["maskup"] = nc.dram_tensor("maskup", [G8 * 8, D], F32, kind="ExternalInput").ap()
    io["gpw"] = nc.dram_tensor("gpw_blk", [G8 * IC * P, D], F32, kind="ExternalInput").ap()
    io["upw"] = nc.dram_tensor("upw_blk", [G8 * IC * P, D], F32, kind="ExternalInput").ap()
    io["dnw"] = nc.dram_tensor("dnw_blk", [G8 * C * P, I14], F32, kind="ExternalInput").ap()
    io["inv_row"] = nc.dram_tensor("inv_row", [1, 64], I32, kind="ExternalInput").ap()
    io["iota64"] = nc.dram_tensor("iota64", [64, 1], F32, kind="ExternalInput").ap()
    io["cos1"] = nc.dram_tensor("cos1", [P, T], F32, kind="ExternalInput").ap()
    io["sin1"] = nc.dram_tensor("sin1", [P, T], F32, kind="ExternalInput").ap()
    io["r128"] = nc.dram_tensor("r128", [P, P], F32, kind="ExternalInput").ap()
    io["masks"] = nc.dram_tensor("masks", [2 * C * P, 512], F32, kind="ExternalInput").ap()
    io["ident_in"] = nc.dram_tensor("ident_in", [P, P], F32, kind="ExternalInput").ap()
    io["x2_out"] = nc.dram_tensor("x2_fm", [D, T], F32, kind="ExternalOutput").ap()
    io["ctb_ckv"] = nc.dram_tensor("ctb_ckv", [KVR, T], BF16, kind="Internal")
    io["ctb_kpe"] = nc.dram_tensor("ctb_kpe", [64, T], BF16, kind="Internal")
    io["gth_ckv"] = nc.dram_tensor("gth_ckv", [2 * KVR, T], BF16, kind="Internal")
    io["gth_kpe"] = nc.dram_tensor("gth_kpe", [128, T], BF16, kind="Internal")
    io["kn_dram"] = nc.dram_tensor("kn_dram", [D, S], BF16, kind="Internal")
    io["vn_dram"] = nc.dram_tensor("vn_dram", [D, S], BF16, kind="Internal")
    io["xacc"] = nc.dram_tensor("xacc", [D, T], F32, kind="Internal")
    io["softg_d"] = nc.dram_tensor("softg_d", [64, T], BF16, kind="Internal")
    io["flatg_d"] = nc.dram_tensor("flatg_d", [64, T], BF16, kind="Internal")
    dbgd = {}
    if dbg:
        for nm in ("d_xn1", "d_comb1", "d_soft", "d_we", "d_flat"):
            shp = [D, T] if nm in ("d_xn1", "d_comb1") else [P, 512]
            dbgd[nm] = nc.dram_tensor(nm, shp, F32, kind="ExternalOutput").ap()
    with ExitStack() as ctx:
        S_ = Sched(nc, ctx)
        emit_l1(S_, io, dbgd, stop_after=stop_after)
        blk = ctx.enter_context(nc.Block())
        S_.emit(blk)
    return nc


def build_fused(dbg=False):
    nc = bass.Bass("TRN2", target_bir_lowering=False)
    io0 = {}
    io0["x_in"] = nc.dram_tensor("x_fm", [D, T], F32, kind="ExternalInput").ap()
    io0["wq"] = nc.dram_tensor("wq_blk", [C * P, D], F32, kind="ExternalInput").ap()
    io0["wk"] = nc.dram_tensor("wk_blk", [C * P, D], F32, kind="ExternalInput").ap()
    io0["wv"] = nc.dram_tensor("wv_blk", [C * P, D], F32, kind="ExternalInput").ap()
    io0["wo"] = nc.dram_tensor("wo_blk", [C * P, D], F32, kind="ExternalInput").ap()
    io0["wf"] = nc.dram_tensor("wf_rows", [D, D], F32, kind="ExternalInput").ap()
    io0["bias_sw"] = nc.dram_tensor("bias_sw", [P, C], F32, kind="ExternalInput").ap()
    io0["gate_in"] = nc.dram_tensor("gate_row", [1, NH], F32, kind="ExternalInput").ap()
    io0["cos0"] = nc.dram_tensor("cos0", [8 * P, T], F32, kind="ExternalInput").ap()
    io0["sin0"] = nc.dram_tensor("sin0", [8 * P, T], F32, kind="ExternalInput").ap()
    io0["ident_in"] = nc.dram_tensor("ident_in", [P, P], F32, kind="ExternalInput").ap()
    x1d = nc.dram_tensor("x1_mid", [D, T], F32, kind="Internal")
    io0["x1_out"] = x1d.ap()
    io0["spill"] = nc.dram_tensor("spill_x", [D, T], F32, kind="Internal")
    io0["ctb_k"] = nc.dram_tensor("ctb_k", [D, T], BF16, kind="Internal")
    io0["ctb_v"] = nc.dram_tensor("ctb_v", [D, T], BF16, kind="Internal")
    io0["gth_k"] = nc.dram_tensor("gth_k", [2 * D, T], BF16, kind="Internal")
    io0["gth_v"] = nc.dram_tensor("gth_v", [2 * D, T], BF16, kind="Internal")

    io1 = {}
    io1["x_in"] = x1d.ap()
    io1["wq1"] = nc.dram_tensor("wq1_blk", [24 * P, D], F32, kind="ExternalInput").ap()
    io1["wkva"] = nc.dram_tensor("wkva_blk", [4 * P, D], F32, kind="ExternalInput").ap()
    io1["wkva_pe"] = nc.dram_tensor("wkva_pe", [P, C * 64], F32, kind="ExternalInput").ap()
    io1["kvnsw"] = nc.dram_tensor("kvnsw", [P, KC4], F32, kind="ExternalInput").ap()
    io1["wkvb_k"] = nc.dram_tensor("wkvb_k", [C * P, KVR], F32, kind="ExternalInput").ap()
    io1["wkvb_v"] = nc.dram_tensor("wkvb_v", [C * P, KVR], F32, kind="ExternalInput").ap()
    io1["wo1"] = nc.dram_tensor("wo1_blk", [C * P, D], F32, kind="ExternalInput").ap()
    io1["gatew"] = nc.dram_tensor("gatew_pm", [P, C * E64], F32, kind="ExternalInput").ap()
    io1["maskup"] = nc.dram_tensor("maskup", [G8 * 8, D], F32, kind="ExternalInput").ap()
    io1["gpw"] = nc.dram_tensor("gpw_blk", [G8 * IC * P, D], F32, kind="ExternalInput").ap()
    io1["upw"] = nc.dram_tensor("upw_blk", [G8 * IC * P, D], F32, kind="ExternalInput").ap()
    io1["dnw"] = nc.dram_tensor("dnw_blk", [G8 * C * P, I14], F32, kind="ExternalInput").ap()
    io1["inv_row"] = nc.dram_tensor("inv_row", [1, 64], I32, kind="ExternalInput").ap()
    io1["iota64"] = nc.dram_tensor("iota64", [64, 1], F32, kind="ExternalInput").ap()
    io1["cos1"] = nc.dram_tensor("cos1", [P, T], F32, kind="ExternalInput").ap()
    io1["sin1"] = nc.dram_tensor("sin1", [P, T], F32, kind="ExternalInput").ap()
    io1["r128"] = nc.dram_tensor("r128", [P, P], F32, kind="ExternalInput").ap()
    io1["masks"] = nc.dram_tensor("masks", [2 * C * P, 512], F32, kind="ExternalInput").ap()
    io1["ident_in"] = io0["ident_in"]
    io1["x2_out"] = nc.dram_tensor("x2_fm", [D, T], F32, kind="ExternalOutput").ap()
    io1["ctb_ckv"] = nc.dram_tensor("ctb_ckv", [KVR, T], BF16, kind="Internal")
    io1["ctb_kpe"] = nc.dram_tensor("ctb_kpe", [64, T], BF16, kind="Internal")
    io1["gth_ckv"] = nc.dram_tensor("gth_ckv", [2 * KVR, T], BF16, kind="Internal")
    io1["gth_kpe"] = nc.dram_tensor("gth_kpe", [128, T], BF16, kind="Internal")
    io1["kn_dram"] = nc.dram_tensor("kn_dram", [D, S], BF16, kind="Internal")
    io1["vn_dram"] = nc.dram_tensor("vn_dram", [D, S], BF16, kind="Internal")
    io1["xacc"] = nc.dram_tensor("xacc", [D, T], F32, kind="Internal")
    io1["softg_d"] = nc.dram_tensor("softg_d", [64, T], BF16, kind="Internal")
    io1["flatg_d"] = nc.dram_tensor("flatg_d", [64, T], BF16, kind="Internal")

    with ExitStack() as ctx:
        S_ = Sched(nc, ctx)
        l0scope = ExitStack()
        S_.default_scope = l0scope
        emit_l0(S_, io0, {})
        l0scope.close()
        S_.default_scope = None
        S_.barrier()
        emit_l1(S_, io1, {})
        blk = ctx.enter_context(nc.Block())
        S_.emit(blk)
    return nc


def kernel(**inputs):
    inputs = {k: np.asarray(v) for k, v in inputs.items()}
    x0 = np.asarray(inputs["L0_attn_hidden_states"])
    com0 = _prep_l0(inputs)
    com1 = _prep_l1(inputs)
    t0 = [_l0_tables(0), _l0_tables(1)]
    t1 = _l1_tables()
    masks = [_full_masks(0), _full_masks(1)]
    nc = build_fused()
    in_maps = []
    for core in range(8):
        b, h = core // 2, core % 2
        m = dict(com0)
        m.update(com1)
        m["x_fm"] = np.ascontiguousarray(x0[b, h * T:(h + 1) * T, :].T)
        m["cos0"], m["sin0"] = t0[h]
        m["cos1"], m["sin1"] = t1[h]
        m["masks"] = masks[h]
        in_maps.append(m)
    r = run_bass_kernel_spmd(nc, in_maps, core_ids=list(range(8)))
    out = np.zeros((4, S, D), np.float32)
    for core in range(8):
        b, h = core // 2, core % 2
        out[b, h * T:(h + 1) * T, :] = r.results[core]["x2_fm"].T
    return out


def kernel_unfused(**inputs):
    inputs = {k: np.asarray(v) for k, v in inputs.items()}
    res0 = run_l0(inputs)
    x1 = [res0.results[c]["x1_fm"] for c in range(8)]
    res1 = run_l1(inputs, x1)
    out = np.zeros((4, S, D), np.float32)
    for core in range(8):
        b, h = core // 2, core % 2
        out[b, h * T:(h + 1) * T, :] = res1[core]["x2_fm"].T
    return out


def _prep_l0(ins):
    out = {}
    out["wq_blk"] = _blk(np.ascontiguousarray(ins["L0_attn_Wq"].T))
    out["wk_blk"] = _blk(np.ascontiguousarray(ins["L0_attn_Wk"].T))
    out["wv_blk"] = _blk(np.ascontiguousarray(ins["L0_attn_Wv"].T))
    out["wo_blk"] = _blk(np.ascontiguousarray(ins["L0_attn_Wo"].T))
    out["wf_rows"] = np.ascontiguousarray(ins["L0_ffn_weight"])
    out["bias_sw"] = np.ascontiguousarray(ins["L0_ffn_bias"].reshape(C, P).T)
    out["gate_row"] = np.ascontiguousarray(ins["L0_attn_gate"].reshape(1, NH))
    out["ident_in"] = np.eye(P, dtype=np.float32)
    return out


def _l1_tables():
    tabs = []
    for h in range(2):
        pos = np.arange(h * T, (h + 1) * T, dtype=np.float64)
        fr = np.arange(0, 32, dtype=np.float64) / 32.0
        inv = 1.0 / (10000.0 ** fr)
        ang = pos[None, :] * inv[:, None]
        cos = np.cos(ang)
        sin = np.sin(ang)
        c64 = np.concatenate([cos, cos], 0)
        s64 = np.concatenate([sin, sin], 0)
        tabs.append((np.concatenate([c64, c64], 0).astype(np.float32),
                     np.concatenate([s64, s64], 0).astype(np.float32)))
    return tabs


def _rot_mat64():
    R = np.zeros((P, P), dtype=np.float32)
    for p in range(P):
        blk = (p // 64) * 64
        off = p - blk
        if off < 32:
            R[blk + off + 32, p] = -1.0
        else:
            R[blk + off - 32, p] = 1.0
    return R


def _full_masks(h):
    out = []
    for sig in range(2):
        qbase = h * T + sig * 512
        for jt in range(C):
            pp = 128 * jt + np.arange(P)[:, None]
            jj = qbase + np.arange(512)[None, :]
            out.append((pp <= jj).astype(np.float32))
    return np.concatenate(out, axis=0)


def _prep_l1(ins):
    out = {}
    wq = np.asarray(ins["L1_attn_wq"])
    qc = wq.reshape(D, NH, 192)
    nope = qc[:, :, 0:128].reshape(D, NH * 128)
    pe = qc[:, :, 128:192].reshape(D, NH * 64)
    out["wq1_blk"] = _blk(np.concatenate([nope, pe], axis=1))
    wkva = np.asarray(ins["L1_attn_w_kv_a"])
    out["wkva_blk"] = _blk(np.ascontiguousarray(wkva[:, 0:512]))
    out["wkva_pe"] = _pm(np.ascontiguousarray(wkva[:, 512:576]))
    out["kvnsw"] = np.ascontiguousarray(ins["L1_attn_kv_norm_scale"].reshape(KC4, P).T)
    wkvb = np.asarray(ins["L1_attn_w_kv_b"])
    kvc = wkvb.reshape(KVR, NH, 256)
    kn = np.ascontiguousarray(kvc[:, :, 0:128].reshape(KVR, NH * 128))
    vv = np.ascontiguousarray(kvc[:, :, 128:256].reshape(KVR, NH * 128))
    out["wkvb_k"] = _blk(kn)
    out["wkvb_v"] = _blk(vv)
    out["wo1_blk"] = _blk(np.asarray(ins["L1_attn_wo"]))
    out["gatew_pm"] = _pm(np.ascontiguousarray(ins["L1_ffn_gate_w"].T))
    out["maskup"] = np.ascontiguousarray(
        np.asarray(ins["L1_ffn_mask_up_w"]).transpose(0, 2, 1).reshape(G8 * 8, D))
    gpw = np.asarray(ins["L1_ffn_gate_proj_w"])
    upw = np.asarray(ins["L1_ffn_up_proj_w"])
    dnw = np.asarray(ins["L1_ffn_down_proj_w"])

    def blk_gu(w):
        wt = w.transpose(0, 2, 1)
        b = wt.reshape(G8, C, P, IC, P).transpose(0, 3, 2, 1, 4)
        return np.ascontiguousarray(b.reshape(G8 * IC * P, D))

    def blk_dn(w):
        wt = w.transpose(0, 2, 1)
        b = wt.reshape(G8, IC, P, C, P).transpose(0, 3, 2, 1, 4)
        return np.ascontiguousarray(b.reshape(G8 * C * P, I14))

    out["gpw_blk"] = blk_gu(gpw)
    out["upw_blk"] = blk_gu(upw)
    out["dnw_blk"] = blk_dn(dnw)
    out["inv_row"] = np.ascontiguousarray(
        np.asarray(ins["L1_ffn_inv_map"]).astype(np.int32).reshape(1, 64))
    out["iota64"] = np.arange(64, dtype=np.float32).reshape(64, 1)
    out["r128"] = _rot_mat64()
    out["ident_in"] = np.eye(P, dtype=np.float32)
    return out


def run_l0(inputs, dbg=False):
    x0 = np.asarray(inputs["L0_attn_hidden_states"])
    com = _prep_l0(inputs)
    tabs = [_l0_tables(0), _l0_tables(1)]
    nc0 = build_l0(dbg=dbg)
    in_maps = []
    for core in range(8):
        b, h = core // 2, core % 2
        m = dict(com)
        m["x_fm"] = np.ascontiguousarray(x0[b, h * T:(h + 1) * T, :].T)
        m["cos0"], m["sin0"] = tabs[h]
        in_maps.append(m)
    return run_bass_kernel_spmd(nc0, in_maps, core_ids=list(range(8)))


def run_l1(inputs, x1_percore, dbg=False):
    com = _prep_l1(inputs)
    tabs = _l1_tables()
    masks = [_full_masks(0), _full_masks(1)]
    nc1 = build_l1(dbg=dbg)
    in_maps = []
    for core in range(8):
        b, h = core // 2, core % 2
        m = dict(com)
        m["x1_fm"] = np.ascontiguousarray(x1_percore[core])
        m["cos1"], m["sin1"] = tabs[h]
        m["masks"] = masks[h]
        in_maps.append(m)
    r = run_bass_kernel_spmd(nc1, in_maps, core_ids=list(range(8)))
    return list(r.results)




# version 14
# speedup vs baseline: 1.0560x; 1.0560x over previous
import math
import numpy as np
from contextlib import ExitStack

import concourse.bass as bass
import concourse.mybir as mybir
from concourse.bass_utils import run_bass_kernel_spmd

F32 = mybir.dt.float32
BF16 = mybir.dt.bfloat16
I32 = mybir.dt.int32
AF = mybir.ActivationFunctionType
ALU = mybir.AluOpType
AX = mybir.AxisListType

P = 128
T = 1024
D = 2048
C = D // P
NH = 16
HD = 128
S = 2048
SOFT_SHIFT = -12.0
RG_PAIRS = [[0, 1], [2, 3], [4, 5], [6, 7]]


def _blk(a):
    K, N = a.shape
    kc, nb = K // P, N // P
    out = a.reshape(kc, P, nb, P).transpose(2, 1, 0, 3)
    return np.ascontiguousarray(out.reshape(nb * P, K))


def _pm(a):
    K, N = a.shape
    kc = K // P
    return np.ascontiguousarray(a.reshape(kc, P, N).transpose(1, 0, 2).reshape(P, kc * N))


def _l0_tables(h):
    pos = np.arange(h * T, (h + 1) * T, dtype=np.float64)
    d = np.arange(0, D, 2, dtype=np.float64) / D
    inv = 1.0 / (10000.0 ** d)
    ang = pos[None, :] * inv[:, None]
    return np.cos(ang).astype(np.float32), np.sin(ang).astype(np.float32)



ENGS = ("pe", "sv", "ss", "gp", "sy")
SEM_NAMES = ("ld", "lp", "st", "pe", "ss", "sv", "sg", "cc0", "cc1", "cc2")


class Sched:

    def __init__(self, nc, ctx):
        self.nc = nc
        self.ctx = ctx
        self.q = {e: [] for e in ENGS}
        self.sems = {}
        self.cnt = {}
        for name in SEM_NAMES:
            self.sems[name] = ctx.enter_context(nc.semaphore(name))
            self.cnt[name] = 0
        self._auto = True
        self.PS = [ctx.enter_context(nc.psum_tensor(f"psb{i}", [P, 512], F32))
                   for i in range(7)]
        self.PT = ctx.enter_context(nc.psum_tensor("pst", [P, 1024], BF16))
        self.last_pt_evac = None

    def sb(self, name, shape, dt, scope=None):
        tgt = scope or getattr(self, "default_scope", None) or self.ctx
        return tgt.enter_context(self.nc.sbuf_tensor(name, shape, dt))

    def barrier(self):
        snap = {k: self.cnt[k] for k in ("pe", "ss", "sv", "sg")}
        dsnap = [(nm, self.cnt[nm]) for nm in self.cnt
                 if nm not in ("pe", "ss", "sv", "sg") and self.cnt[nm] > 0]
        for eng, own in (("pe", "pe"), ("sv", "sv"), ("ss", "ss"), ("gp", "sg"), ("sy", None)):
            waits = [(k, v) for k, v in snap.items() if k != own and v > 0] + dsnap
            self.step(eng, lambda e: e.nop(), waits)

    def sem(self, name):
        if name not in self.sems:
            self.sems[name] = self.ctx.enter_context(self.nc.semaphore(name))
            self.cnt[name] = 0
        return name

    def bump(self, sem, n=1):
        self.cnt[sem] += n
        return self.cnt[sem]

    def step(self, eng, f, waits=(), inc=None):
        waits = [(sm, v) for (sm, v) in waits if v is not None and v > 0]
        for sm, val in waits:
            assert val <= self.cnt[sm], f"wait on future token {sm}:{val}>{self.cnt[sm]}"
        tok = None
        if inc is not None:
            tok = self.bump(inc[0], inc[1])
        sems = self.sems

        def run(e, f=f, waits=waits, inc=inc):
            for sm, val in waits:
                e.wait_ge(sems[sm], val)
            last = f(e)
            if inc is not None:
                assert last is not None
                last.then_inc(sems[inc[0]], inc[1])
        self.q[eng].append(run)
        return tok

    def dma(self, eng, out, in_, waits=(), sem="ld", **kw):
        self.sem(sem)
        return self.step(eng, lambda e: e.dma_start(out, in_, **kw), waits, (sem, 16))

    def emit(self, block):
        def mk(e):
            def body(eng):
                for f in self.q[e]:
                    f(eng)
            return body
        block.tensor(mk("pe"))
        block.vector(mk("sv"))
        block.scalar(mk("ss"))
        block.gpsimd(mk("gp"))
        block.sync(mk("sy"))


def fm_proj_chunk(S_, wbuf, rhs, rhs_off, pa, pb, waits, kc=C):
    def f(t):
        mm = None
        for c in range(kc):
            for s2, pp in ((0, pa), (1, pb)):
                mm = t.matmul(pp[:, 0:512], wbuf[:, c * P:(c + 1) * P],
                              rhs[:, rhs_off + c * T + s2 * 512: rhs_off + c * T + (s2 + 1) * 512],
                              start=(c == 0), stop=(c == kc - 1))
        return mm
    return S_.step("pe", f, waits, ("pe", 1))


def transpose16(S_, src, src_off, dst_view, waits0, ident, nt=16):
    PT = S_.PT
    tokv = S_.last_pt_evac
    nr = (nt + 7) // 8
    for r in range(nr):
        lo, hi = r * 8, min(nt, r * 8 + 8)
        def ft(t, lo=lo, hi=hi):
            mm = None
            for i in range(lo, hi):
                mm = t.transpose(PT[:, (i - lo) * P:(i - lo + 1) * P],
                                 src[:, src_off + i * P: src_off + (i + 1) * P], ident[:])
            return mm
        w = list(waits0) + ([("sv", tokv)] if tokv else [])
        waits0 = []
        tokp = S_.step("pe", ft, w, ("pe", 1))
        tokv = S_.step("sv", lambda v, r=r, n=hi - lo: v.tensor_copy(
            dst_view(r), PT[:, 0:n * P]), [("pe", tokp)], ("sv", 1))
    S_.last_pt_evac = tokv
    return tokv



def emit_l0(S_, io, dbg):
    PS = S_.PS
    sb = S_.sb

    XN = sb("XN", [P, C * T], BF16)
    ident = sb("ident", [P, P], BF16)
    biasT = sb("biasT", [P, C], F32)
    gateT = sb("gateT", [1, NH], F32)
    gsig = sb("gsig", [P, NH], F32)
    ones_b = sb("ones_b", [P, 1], BF16)
    ones_f = sb("ones_f", [1, P], F32)
    cosb = sb("cosb", [P, 2 * T], F32)
    sinb = sb("sinb", [P, 2 * T], F32)
    WB = [sb("wbA", [P, C * P], BF16), sb("wbB", [P, C * P], BF16)]
    scrA = sb("scrA", [P, T], F32)
    scrB = sb("scrB", [P, T], F32)
    scrC = sb("scrC", [P, T], F32)
    rowA = sb("rowA", [1, T], F32)
    rowB = sb("rowB", [1, T], F32)
    bcM = sb("bcM", [P, T], F32)
    bcR = sb("bcR", [P, T], F32)
    VCH = [sb("VCHa", [P, T], BF16), sb("VCHb", [P, T], BF16)]
    thr = sb("thr", [P, 1], F32)
    epsT = sb("epsT", [1, 1], F32)
    shiftT = sb("shiftT", [P, 1], F32)
    nthr = sb("nthr", [P, 1], F32)
    NTB = sb("NTB", [P, 2], BF16)
    KR = [sb("KR0", [P, T], BF16), sb("KR1", [P, T], BF16)]

    xd = io["x_in"].rearrange("(c p) t -> p c t", p=P)
    wq3 = io["wq"].rearrange("(j p) k -> j p k", p=P)
    wk3 = io["wk"].rearrange("(j p) k -> j p k", p=P)
    wv3 = io["wv"].rearrange("(j p) k -> j p k", p=P)
    wo3 = io["wo"].rearrange("(j p) k -> j p k", p=P)

    t_ones = S_.step("gp", lambda g: g.memset(ones_b[:], 1.0), (), ("sg", 1))
    t_eps = S_.step("gp", lambda g: g.memset(epsT[:], 1e-5), (), ("sg", 1))
    t_shift = S_.step("gp", lambda g: g.memset(shiftT[:], SOFT_SHIFT), (), ("sg", 1))
    t_onef = S_.step("gp", lambda g: g.memset(ones_f[:], 1.0), (), ("sg", 1))
    S_.dma("gp", ident[:], io["ident_in"], sem="once")
    S_.dma("gp", XN[:].rearrange("p (c t) -> p c t", t=T), xd, sem="once")
    S_.dma("sy", biasT[:], io["bias_sw"], sem="once")
    S_.dma("sy", gateT[:], io["gate_in"], sem="once")

    scope1 = ExitStack()
    X = sb("X", [P, C * T], F32, scope1)
    SQ2 = sb("SQ2", [P, 2 * T], BF16, scope1)
    WFB = [sb("WF32a", [P, D], F32, scope1), sb("WF32b", [P, D], F32, scope1)]
    TERN = sb("TERN", [P, D], BF16, scope1)
    FLT = sb("FLT", [P, D], BF16, scope1)
    t1b = SQ2
    FSC = sb("FSC", [P, 2 * T], F32, scope1)

    t_once = S_.dma("sy", X[:].rearrange("p (c t) -> p c t", t=T), xd, sem="once")
    t_x = t_once; t_id = t_once; t_xb = t_once; t_bias = t_once; t_gate = t_once

    sq_tok = {}
    mm_tok = {}
    for c in range(C):
        wsq = [("once", t_once)] if c == 0 else []
        if c >= 2:
            wsq.append(("pe", mm_tok[c - 2]))
        sq_tok[c] = S_.step("ss", lambda s, c=c: s.activation(
            SQ2[:, (c % 2) * T:(c % 2 + 1) * T], X[:, c * T:(c + 1) * T], AF.Square),
            wsq, ("ss", 1))
        def f_mm(t, c=c):
            mm = None
            for s2, pp in ((0, PS[0]), (1, PS[1])):
                t.matmul(pp[0:1, 0:512], ones_b[:],
                         XN[:, c * T + s2 * 512: c * T + (s2 + 1) * 512],
                         start=(c == 0), stop=(c == C - 1))
            for s2, pp in ((0, PS[2]), (1, PS[3])):
                mm = t.matmul(pp[0:1, 0:512], ones_b[:],
                              SQ2[:, (c % 2) * T + s2 * 512: (c % 2) * T + (s2 + 1) * 512],
                              start=(c == 0), stop=(c == C - 1))
            return mm
        wmm = [("ss", sq_tok[c])]
        if c == 0:
            wmm += [("once", t_once), ("sg", t_ones)]
        mm_tok[c] = S_.step("pe", f_mm, wmm, ("pe", 1))
    t_stats = mm_tok[C - 1]

    def f_rows(v):
        v.tensor_scalar(rowA[0:1, 0:512], PS[0][0:1, 0:512], 1.0 / D, None, ALU.mult)
        v.tensor_scalar(rowA[0:1, 512:1024], PS[1][0:1, 0:512], 1.0 / D, None, ALU.mult)
        v.tensor_scalar(rowB[0:1, 0:512], PS[2][0:1, 0:512], 1.0 / D, None, ALU.mult)
        v.tensor_scalar(rowB[0:1, 512:1024], PS[3][0:1, 0:512], 1.0 / D, None, ALU.mult)
        v.tensor_tensor(scrC[0:1, :], rowA[0:1, :], rowA[0:1, :], ALU.mult)
        v.tensor_tensor(rowB[0:1, :], rowB[0:1, :], scrC[0:1, :], ALU.subtract)
        return v.tensor_scalar(rowA[0:1, :], rowA[0:1, :], -1.0, None, ALU.mult)
    t_var = S_.step("sv", f_rows, [("pe", t_stats)], ("sv", 1))
    t_sd = S_.step("ss", lambda s: s.activation(rowB[0:1, :], rowB[0:1, :], AF.Sqrt, bias=epsT[:]),
                   [("sv", t_var), ("sg", t_eps)], ("ss", 1))
    t_rs = S_.step("sv", lambda v: v.reciprocal(rowB[0:1, :], rowB[0:1, :]),
                   [("ss", t_sd)], ("sv", 1))

    t_bc = None
    for s2 in range(2):
        def f_bcs(t, s2=s2):
            t.matmul(PS[4][:, 0:512], ones_f[:], rowA[0:1, s2 * 512:(s2 + 1) * 512],
                     start=True, stop=True)
            return t.matmul(PS[5][:, 0:512], ones_f[:], rowB[0:1, s2 * 512:(s2 + 1) * 512],
                            start=True, stop=True)
        w = [("sv", t_rs), ("sg", t_onef)] if s2 == 0 else [("sv", t_bc)]
        tk = S_.step("pe", f_bcs, w, ("pe", 1))

        def f_bce(v, s2=s2):
            v.tensor_copy(bcM[:, s2 * 512:(s2 + 1) * 512], PS[4][:, 0:512])
            return v.tensor_copy(bcR[:, s2 * 512:(s2 + 1) * 512], PS[5][:, 0:512])
        t_bc = S_.step("sv", f_bce, [("pe", tk)], ("sv", 1))

    def f_norm(v):
        last = None
        for c in range(C):
            v.tensor_tensor(scrA[:], X[:, c * T:(c + 1) * T], bcM[:], ALU.add)
            last = v.tensor_tensor(XN[:, c * T:(c + 1) * T], scrA[:], bcR[:], ALU.mult)
        return last
    t_xn = S_.step("sv", f_norm, [("sv", t_bc)], ("sv", 1))
    if dbg:
        S_.dma("gp", dbg["d_xn"].rearrange("(c p) t -> p c t", p=P),
               XN[:].rearrange("p (c t) -> p c t", t=T), [("sv", t_xn)], sem="dbg")

    korder = []
    for cp in range(8):
        korder += [cp, cp + 8]

    wsem = ["wb0", "wb1"]
    wld = {0: ("wb0", S_.dma("gp", WB[0][:], wk3[korder[0]], [("sv", t_xn)], sem="wb0"))}
    wld[1] = ("wb1", S_.dma("gp", WB[1][:], wk3[korder[1]], sem="wb1"))
    cs = {}
    for cp in range(2):
        sl = cp % 2
        S_.dma("sy", cosb[:, sl * T:(sl + 1) * T], io["cos0"][cp * P:(cp + 1) * P, :], sem=f"cs{sl}")
        cs[cp] = (f"cs{sl}", S_.dma("sy", sinb[:, sl * T:(sl + 1) * T], io["sin0"][cp * P:(cp + 1) * P, :], sem=f"cs{sl}"))

    evac = {}
    rope_done = {}
    kst = {}
    for ip in range(8):
        iA, iB = 2 * ip, 2 * ip + 1
        wA = [wld[iA]] + ([("sv", evac[iA - 2])] if iA >= 2 else [])
        tA = fm_proj_chunk(S_, WB[iA % 2], XN, 0, PS[0], PS[1], wA)
        wB = [wld[iB]] + ([("sv", evac[iB - 2])] if iB >= 2 else [])
        tB = fm_proj_chunk(S_, WB[iB % 2], XN, 0, PS[2], PS[3], wB)
        for nx in (iA + 2, iB + 2):
            if nx < 16 and nx not in wld:
                sm = wsem[nx % 2]
                wld[nx] = (sm, S_.dma("gp", WB[nx % 2][:], wk3[korder[nx]],
                                      [("pe", tA if nx % 2 == 0 else tB)], sem=sm))

        def f_ev(v, tB=tB):
            v.tensor_copy(scrA[:, 0:512], PS[0][:, 0:512])
            v.tensor_copy(scrA[:, 512:1024], PS[1][:, 0:512])
            v.wait_ge(S_.sems["pe"], tB)
            v.tensor_copy(scrB[:, 0:512], PS[2][:, 0:512])
            return v.tensor_copy(scrB[:, 512:1024], PS[3][:, 0:512])
        evac[iA] = evac[iB] = S_.step("sv", f_ev, [("pe", tA)], ("sv", 1))

        sl = ip % 2
        def f_rope(v, sl=sl):
            v.tensor_tensor(scrC[:], scrA[:], cosb[:, sl * T:(sl + 1) * T], ALU.mult)
            v.tensor_tensor(KR[1][:], scrB[:], cosb[:, sl * T:(sl + 1) * T], ALU.mult)
            v.tensor_tensor(scrB[:], scrB[:], sinb[:, sl * T:(sl + 1) * T], ALU.mult)
            v.tensor_tensor(scrA[:], scrA[:], sinb[:, sl * T:(sl + 1) * T], ALU.mult)
            v.tensor_tensor(KR[0][:], scrC[:], scrB[:], ALU.subtract)
            return v.tensor_tensor(KR[1][:], KR[1][:], scrA[:], ALU.add)
        waits = [cs[ip]] + ([("kr", kst[ip - 1])] if ip >= 1 else [])
        rope_done[ip] = S_.step("sv", f_rope, waits, ("sv", 1))

        S_.dma("sy", io["ctb_k"].ap()[ip * P:(ip + 1) * P, :], KR[0][:],
               [("sv", rope_done[ip])], sem="kr")
        kst[ip] = S_.dma("sy", io["ctb_k"].ap()[(ip + 8) * P:(ip + 9) * P, :], KR[1][:], sem="kr")
        if dbg:
            S_.dma("gp", dbg["d_k"][ip * P:(ip + 1) * P, :], KR[0][:],
                   [("sv", rope_done[ip])], sem="kr")
            kst[ip] = S_.dma("gp", dbg["d_k"][(ip + 8) * P:(ip + 9) * P, :], KR[1][:], sem="kr")
        nxt = ip + 2
        if nxt < 8:
            sl2 = nxt % 2
            S_.dma("sy", cosb[:, sl2 * T:(sl2 + 1) * T],
                   io["cos0"][nxt * P:(nxt + 1) * P, :],
                   [("sv", rope_done[ip])], sem=f"cs{sl2}")
            cs[nxt] = (f"cs{sl2}", S_.dma("sy", sinb[:, sl2 * T:(sl2 + 1) * T],
                       io["sin0"][nxt * P:(nxt + 1) * P, :], sem=f"cs{sl2}"))

    t_agk = S_.step("gp", lambda g: g.collective_compute(
        "AllGather", ALU.bypass, replica_groups=RG_PAIRS,
        ins=[io["ctb_k"].ap().opt()], outs=[io["gth_k"].ap().opt()]),
        [("kr", kst[7])], ("cc0", 1))

    vld = {0: ("wb0", S_.dma("gp", WB[0][:], wv3[0], [("pe", S_.cnt["pe"])], sem="wb0"))}
    vld[1] = ("wb1", S_.dma("gp", WB[1][:], wv3[1], sem="wb1"))
    vev = {}
    vst = {}
    vtok = {}
    for j in range(C):
        pa, pb = (PS[0], PS[1]) if j % 2 == 0 else (PS[2], PS[3])
        w = [vld[j]] + ([("ss", vev[j - 2])] if j >= 2 else [("sv", evac[15])])
        vtok[j] = fm_proj_chunk(S_, WB[j % 2], XN, 0, pa, pb, w)
        nx = j + 2
        if nx < C:
            sm = wsem[nx % 2]
            vld[nx] = (sm, S_.dma("gp", WB[nx % 2][:], wv3[nx], [("pe", vtok[j])], sem=sm))

        def f_vc(s, j=j):
            pa, pb = (PS[0], PS[1]) if j % 2 == 0 else (PS[2], PS[3])
            vc = VCH[j % 2]
            s.activation(vc[:, 0:512], pa[:, 0:512], AF.Copy)
            return s.activation(vc[:, 512:1024], pb[:, 0:512], AF.Copy)
        wv_ = [("pe", vtok[j])] + ([("vch", vst[j - 2])] if j >= 2 else [])
        vev[j] = S_.step("ss", f_vc, wv_, ("ss", 1))
        vst[j] = S_.dma("sy", io["ctb_v"].ap()[j * P:(j + 1) * P, :], VCH[j % 2][:],
                        [("ss", vev[j])], sem="vch")
        if dbg:
            vst[j] = S_.dma("gp", dbg["d_v"][j * P:(j + 1) * P, :], VCH[j % 2][:],
                            [("ss", vev[j])], sem="vch")

    t_agv = S_.step("gp", lambda g: g.collective_compute(
        "AllGather", ALU.bypass, replica_groups=RG_PAIRS,
        ins=[io["ctb_v"].ap().opt()], outs=[io["gth_v"].ap().opt()]),
        [("vch", vst[C - 1])], ("cc1", 1))
    t_vproj_last = vtok[C - 1]

    fld = {0: ("wf0", S_.dma("sy", WFB[0][:], io["wf"][0:P, :], sem="wf0"))}
    fld[1] = ("wf1", S_.dma("sy", WFB[1][:], io["wf"][P:2 * P, :], sem="wf1"))
    fxadd = {}
    fbias_prev = None
    fbias_prev2 = None
    fmm_prev = None
    for j in range(C):
        wv_ = [fld[j]] + ([("pe", fmm_prev)] if fmm_prev else [("pe", t_stats)])
        t_rd = S_.step("sv", lambda v, j=j: v.tensor_reduce(
            thr[:], WFB[j % 2][:], AX.X, ALU.add, apply_absolute_value=True),
            wv_, ("sv", 1))

        def f_thr(g):
            g.tensor_scalar(nthr[:], thr[:], -0.7 / D, None, ALU.mult)
            return g.tensor_scalar(thr[:], thr[:], 0.7 / D, None, ALU.mult)
        t_th = S_.step("gp", f_thr, [("sv", t_rd)], ("sg", 1))

        def f_tern(v, j=j):
            W = WFB[j % 2]
            v.tensor_scalar(t1b[:], W[:], thr[:], None, ALU.is_gt)
            v.tensor_scalar(TERN[:], W[:], nthr[:], None, ALU.is_lt)
            return v.tensor_tensor(TERN[:], t1b[:], TERN[:], ALU.subtract)
        t_tn = S_.step("sv", f_tern, [("sg", t_th)], ("sv", 1))
        nx = j + 2
        if nx < C:
            sm = f"wf{nx % 2}"
            fld[nx] = (sm, S_.dma("sy", WFB[nx % 2][:], io["wf"][nx * P:(nx + 1) * P, :],
                                  [("sv", t_tn)], sem=sm))
        t_fl = transpose16(S_, TERN, 0,
                           lambda r: FLT[:, r * 8 * P:(r + 1) * 8 * P],
                           [("sv", t_tn), ("once", t_once)], ident)
        pa, pb = (PS[0], PS[1]) if j % 2 == 0 else (PS[2], PS[3])
        wmm = [("sv", t_fl)] + ([("sv", fbias_prev2)] if fbias_prev2 else [("ss", vev[C - 1])])
        t_mm = fm_proj_chunk(S_, FLT, XN, 0, pa, pb, wmm)
        fmm_prev = t_mm

        def f_bias(v, j=j, pa=pa, pb=pb):
            o = (j % 2) * T
            v.tensor_scalar(FSC[:, o:o + 512], pa[:, 0:512], biasT[:, j:j + 1], None, ALU.add)
            return v.tensor_scalar(FSC[:, o + 512:o + 1024], pb[:, 0:512], biasT[:, j:j + 1], None, ALU.add)
        fbias_prev2 = fbias_prev
        fbias_prev = S_.step("sv", f_bias, [("pe", t_mm), ("once", t_once)], ("sv", 1))
        fxadd[j] = S_.step("gp", lambda g, j=j: g.tensor_tensor(
            X[:, j * T:(j + 1) * T], X[:, j * T:(j + 1) * T],
            FSC[:, (j % 2) * T:(j % 2 + 1) * T], ALU.add),
            [("sv", fbias_prev)], ("sg", 1))
        if dbg:
            S_.dma("gp", dbg["d_ffn"][j * P:(j + 1) * P, :], FSC[:, (j % 2) * T:(j % 2 + 1) * T],
                   [("sg", fxadd[j])], sem="dbg")

    ffn_mm_last = fmm_prev
    ffn_bias_last = fbias_prev

    t_gs = S_.step("ss", lambda s: s.activation(gateT[0:1, :], gateT[0:1, :], AF.Sigmoid),
                   [("once", t_once)], ("ss", 1))

    def f_gbc(t):
        mm = None
        for hh2 in range(NH):
            mm = t.matmul(PS[6][:, hh2:hh2 + 1], ones_f[:], gateT[0:1, hh2:hh2 + 1],
                          start=(hh2 == 0), stop=(hh2 == NH - 1))
        return mm
    t_gb = S_.step("pe", f_gbc, [("ss", t_gs), ("sg", t_onef)], ("pe", 1))
    t_gsig = S_.step("ss", lambda s: s.activation(gsig[:], PS[6][:, 0:NH], AF.Copy),
                     [("pe", t_gb)], ("ss", 1))

    t_spill = S_.dma("sy", io["spill"].ap().rearrange("(c p) t -> p c t", p=P),
                     X[:].rearrange("p (c t) -> p c t", t=T),
                     [("sg", fxadd[C - 1])], sem="spl")

    scope1.close()

    scope2 = ExitStack()
    COMB = sb("COMB", [P, NH * T], BF16, scope2)
    scope2a = ExitStack()
    KHB = [sb("KH0", [P, S], BF16, scope2a), sb("KH1", [P, S], BF16, scope2a)]
    VFH = [sb("VF0", [P, S], BF16, scope2a), sb("VF1", [P, S], BF16, scope2a)]
    VTH = [sb("VT0", [P, S], BF16, scope2a), sb("VT1", [P, S], BF16, scope2a)]
    QR = sb("QR", [P, 2 * T], BF16, scope2a)
    QF = sb("QF", [P, 4 * T], BF16, scope2a)
    KFM = sb("KFM", [P, S], BF16, scope2a)
    KFT = sb("KFT", [P, S], BF16, scope2a)
    MEMS = sb("MEMS", [P, 2 * P], BF16, scope2a)
    PTB = sb("PTB", [P, 2 * C * 512], BF16, scope2a)
    rz_sb = sb("rz_sb", [1, 512], F32, scope2a)
    rd_sb = sb("rd_sb", [1, 512], F32, scope2a)
    bz_sb = sb("bz_sb", [P, 512], F32, scope2a)
    mem_sb = scrB[:, 0:512]
    cT1 = scrC[:, 0:512]
    cT2 = scrC[:, 512:1024]

    gk = io["gth_k"].ap()
    gv = io["gth_v"].ap()

    qld = {0: ("wb0", S_.dma("gp", WB[0][:], wq3[korder[0]], [("pe", ffn_mm_last)], sem="wb0"))}
    qld[1] = ("wb1", S_.dma("gp", WB[1][:], wq3[korder[1]], sem="wb1"))
    khld = {}
    vfld = {}
    for pos in range(2):
        hh = korder[pos]
        S_.dma("gp", KHB[pos][:, 0:T], gk[hh * P:(hh + 1) * P, :],
               ([("cc0", t_agk)] if pos == 0 else []) + [("spl", t_spill)], sem=f"kh{pos}")
        khld[pos] = (f"kh{pos}", S_.dma("gp", KHB[pos][:, T:S],
                     gk[D + hh * P:D + (hh + 1) * P, :], sem=f"kh{pos}"))
        S_.dma("gp", VFH[pos][:, 0:T], gv[hh * P:(hh + 1) * P, :],
               [("cc1", t_agv)] if pos == 0 else [], sem=f"vf{pos}")
        vfld[pos] = (f"vf{pos}", S_.dma("gp", VFH[pos][:, T:S],
                     gv[D + hh * P:D + (hh + 1) * P, :], sem=f"vf{pos}"))

    qevac = {}
    qcs = {}
    ropeq = {}
    kfree = {}
    zav_hist = {}
    ex_hist = {}
    comb_toks = []
    pend = []
    sidx = [0]
    kt_prev = [None]
    last_msb = [None]

    def l0_tail(d):
        ii_, sig_, hh_, qo_, par_ = d["ii"], d["sig"], d["hh"], d["qo"], d["par"]
        po = par_ * C * 512

        def f_zav(t, ii_=ii_, qo_=qo_, po=po):
            for jt in range(C):
                t.matmul(PS[5][0:1, 0:512], ones_b[:], PTB[:, po + jt * 512:po + (jt + 1) * 512],
                         start=(jt == 0), stop=(jt == C - 1))
            for jt in range(C):
                t.matmul(PS[4][:, 0:512], VTH[ii_ % 2][:, jt * P:(jt + 1) * P],
                         PTB[:, po + jt * 512:po + (jt + 1) * 512],
                         start=(jt == 0), stop=(jt == C - 1))
            t.matmul(PS[6][:, 0:512], MEMS[:, (ii_ % 2) * P:(ii_ % 2 + 1) * P],
                     QF[:, qo_:qo_ + 512], start=True, stop=True)
            return t.matmul(PS[5][32:33, 0:512], NTB[:, ii_ % 2:ii_ % 2 + 1],
                            QF[:, qo_:qo_ + 512], start=True, stop=True)
        wz = [("ss", d["ex"]), ("sv", last_msb[0])]
        wz.append(("sv", comb_toks[-1]) if comb_toks else ("sv", S_.cnt["sv"]))
        tz = S_.step("pe", f_zav, wz, ("pe", 1))
        zav_hist[(ii_, sig_)] = tz

        def f_rcp(v):
            v.reciprocal(rz_sb[0:1, :], PS[5][0:1, 0:512])
            v.reciprocal(rd_sb[0:1, :], PS[5][32:33, 0:512])
            return v.tensor_copy(mem_sb, PS[6][:, 0:512])
        t_rcp = S_.step("sv", f_rcp, [("pe", tz)], ("sv", 1))
        t_bz = S_.step("pe", lambda t: t.matmul(PS[6][:, 0:512], ones_f[:], rz_sb[0:1, :],
                                                start=True, stop=True),
                       [("sv", t_rcp)], ("pe", 1))
        t_bzc = S_.step("sv", lambda v: v.tensor_copy(bz_sb[:], PS[6][:, 0:512]),
                        [("pe", t_bz)], ("sv", 1))
        t_bd = S_.step("pe", lambda t: t.matmul(PS[6][:, 0:512], ones_f[:], rd_sb[0:1, :],
                                                start=True, stop=True),
                       [("sv", t_bzc)], ("pe", 1))

        def f_comb(v, hh_=hh_, sig_=sig_):
            v.tensor_tensor(cT1, PS[4][:, 0:512], bz_sb[:], ALU.mult)
            v.tensor_tensor(cT2, mem_sb, PS[6][:, 0:512], ALU.mult)
            v.tensor_tensor(cT2, cT2, cT1, ALU.subtract)
            return v.scalar_tensor_tensor(
                COMB[:, hh_ * T + sig_ * 512: hh_ * T + (sig_ + 1) * 512],
                cT2, gsig[:, hh_:hh_ + 1], cT1, ALU.mult, ALU.add)
        comb_toks.append(S_.step("sv", f_comb, [("pe", t_bd), ("ss", t_gsig)], ("sv", 1)))

    for ip in range(8):
        iA, iB = 2 * ip, 2 * ip + 1
        wA = [qld[iA]] + ([("sv", qevac[iA - 2])] if iA >= 2 else [("sv", ffn_bias_last)])
        if sidx[0] >= 2:
            wA.append(("ss", ex_hist[sidx[0] - 2]))
        tA = fm_proj_chunk(S_, WB[iA % 2], XN, 0, PS[0], PS[1], wA)
        wB = [qld[iB]] + ([("sv", qevac[iB - 2])] if iB >= 2 else [("sv", ffn_bias_last)])
        if sidx[0] >= 1:
            wB.append(("ss", ex_hist[sidx[0] - 1]))
        tB = fm_proj_chunk(S_, WB[iB % 2], XN, 0, PS[2], PS[3], wB)
        if ip == 7:
            tq_last = (tA, tB)
        for nx in (iA + 2, iB + 2):
            if nx < 16 and nx not in qld:
                sm = wsem[nx % 2]
                qld[nx] = (sm, S_.dma("gp", WB[nx % 2][:], wq3[korder[nx]],
                                      [("pe", tA if nx % 2 == 0 else tB)], sem=sm))
        sl = ip % 2
        w2 = [("sv", ropeq[ip - 2])] if ip >= 2 else [("sv", rope_done[6 + ip])]
        S_.dma("sy", cosb[:, sl * T:(sl + 1) * T], io["cos0"][ip * P:(ip + 1) * P, :],
               w2, sem=f"cs{sl}")
        qcs[ip] = (f"cs{sl}", S_.dma("sy", sinb[:, sl * T:(sl + 1) * T],
                   io["sin0"][ip * P:(ip + 1) * P, :], sem=f"cs{sl}"))

        def f_qev(v, tB=tB):
            v.tensor_copy(scrA[:, 0:512], PS[0][:, 0:512])
            v.tensor_copy(scrA[:, 512:1024], PS[1][:, 0:512])
            v.wait_ge(S_.sems["pe"], tB)
            v.tensor_copy(scrB[:, 0:512], PS[2][:, 0:512])
            return v.tensor_copy(scrB[:, 512:1024], PS[3][:, 0:512])
        qevac[iA] = qevac[iB] = S_.step("sv", f_qev, [("pe", tA)], ("sv", 1))

        tvthA = transpose16(S_, VFH[iA % 2], 0,
                            lambda r, iA=iA: VTH[iA % 2][:, r * 8 * P:(r + 1) * 8 * P],
                            [vfld[iA]] + ([("pe", zav_hist[(iA - 2, 1)])] if iA >= 2 else []),
                            ident)

        def f_qrope(v, sl=sl):
            v.tensor_tensor(scrC[:], scrA[:], cosb[:, sl * T:(sl + 1) * T], ALU.mult)
            v.tensor_tensor(QR[:, T:2 * T], scrB[:], cosb[:, sl * T:(sl + 1) * T], ALU.mult)
            v.tensor_tensor(scrB[:], scrB[:], sinb[:, sl * T:(sl + 1) * T], ALU.mult)
            v.tensor_tensor(scrA[:], scrA[:], sinb[:, sl * T:(sl + 1) * T], ALU.mult)
            v.tensor_tensor(QR[:, 0:T], scrC[:], scrB[:], ALU.subtract)
            v.tensor_tensor(QR[:, T:2 * T], QR[:, T:2 * T], scrA[:], ALU.add)
            v.tensor_scalar(scrA[:], QR[:, 0:T], 0.0, None, ALU.min)
            return v.tensor_scalar(scrB[:], QR[:, T:2 * T], 0.0, None, ALU.min)
        wr = [qcs[ip]]
        if ip == 0:
            wr.append(("spl", t_spill))
        if comb_toks:
            wr.append(("sv", comb_toks[-1]))
        ropeq[ip] = S_.step("sv", f_qrope, wr, ("sv", 1))

        def f_qexp(s):
            s.activation(scrA[:], scrA[:], AF.Exp)
            return s.activation(scrB[:], scrB[:], AF.Exp)
        t_qe = S_.step("ss", f_qexp, [("sv", ropeq[ip])], ("ss", 1))

        qfo = (ip % 2) * 2 * T

        def f_qf(v, qfo=qfo):
            v.scalar_tensor_tensor(QF[:, qfo:qfo + T], QR[:, 0:T], 0.0, scrA[:], ALU.max, ALU.add)
            return v.scalar_tensor_tensor(QF[:, qfo + T:qfo + 2 * T], QR[:, T:2 * T], 0.0,
                                          scrB[:], ALU.max, ALU.add)
        wqf = [("ss", t_qe)]
        if ip >= 2:
            wqf.append(("pe", zav_hist[(2 * ip - 3, 1)]))
        t_qf = S_.step("sv", f_qf, wqf, ("sv", 1))

        for which in range(2):
            ii = 2 * ip + which
            hh = korder[ii]
            kbuf = KHB[ii % 2]

            if which == 0:
                t_vth = tvthA
            else:
                t_vth = transpose16(S_, VFH[ii % 2], 0,
                                    lambda r, ii=ii: VTH[ii % 2][:, r * 8 * P:(r + 1) * 8 * P],
                                    [vfld[ii]] + ([("pe", zav_hist[(ii - 2, 1)])] if ii >= 2 else []),
                                    ident)

            def f_kmin(v, kbuf=kbuf):
                v.tensor_scalar(scrA[:, 0:512], kbuf[:, 0:512], 0.0, None, ALU.min)
                v.tensor_scalar(scrA[:, 512:1024], kbuf[:, 512:1024], 0.0, None, ALU.min)
                v.tensor_scalar(scrB[:, 0:512], kbuf[:, 1024:1536], 0.0, None, ALU.min)
                return v.tensor_scalar(scrB[:, 512:1024], kbuf[:, 1536:2048], 0.0, None, ALU.min)
            wkm = [khld[ii], ("sv", t_qf)]
            if kt_prev[0] is not None:
                wkm.append(("sv", kt_prev[0]))
            t_km = S_.step("sv", f_kmin, wkm, ("sv", 1))

            def f_kexp(s):
                s.activation(scrA[:], scrA[:], AF.Exp)
                return s.activation(scrB[:], scrB[:], AF.Exp)
            t_ke = S_.step("ss", f_kexp, [("sv", t_km)], ("ss", 1))

            def f_kfm(v, kbuf=kbuf):
                v.scalar_tensor_tensor(KFM[:, 0:T], kbuf[:, 0:T], 0.0, scrA[:], ALU.max, ALU.add)
                v.scalar_tensor_tensor(KFM[:, T:S], kbuf[:, T:S], 0.0, scrB[:], ALU.max, ALU.add)
                return v.tensor_reduce(thr[:], KFM[:], AX.X, ALU.add)
            t_kf0 = S_.step("sv", f_kfm, [("ss", t_ke)], ("sv", 1))
            wkf = [("sv", t_kf0)]
            if ii >= 2:
                wkf.append(("pe", zav_hist[(ii - 2, 1)]))
            t_kf = S_.step("ss", lambda s, ii=ii: s.copy(NTB[:, ii % 2:ii % 2 + 1], thr[:]),
                           wkf, ("ss", 1))

            t_kt = transpose16(S_, KFM, 0,
                               lambda r: KFT[:, r * 8 * P:(r + 1) * 8 * P],
                               [("ss", t_kf)], ident)
            kt_prev[0] = t_kt

            def f_memmm(t, ii=ii):
                mm = None
                for jt in range(C):
                    mm = t.matmul(PS[6][:, 0:P], KFT[:, jt * P:(jt + 1) * P],
                                  VTH[ii % 2][:, jt * P:(jt + 1) * P],
                                  start=(jt == 0), stop=(jt == C - 1))
                return mm
            wmem = [("sv", t_kt), ("sv", t_vth)]
            if comb_toks:
                wmem.append(("sv", comb_toks[-1]))
            t_mem = S_.step("pe", f_memmm, wmem, ("pe", 1))
            wmsb = [("pe", t_mem)]
            if ii >= 2:
                wmsb.append(("pe", zav_hist[(ii - 2, 1)]))
            t_msb = S_.step("sv", lambda v, ii=ii: v.tensor_copy(
                MEMS[:, (ii % 2) * P:(ii % 2 + 1) * P], PS[6][:, 0:P]), wmsb, ("sv", 1))
            last_msb[0] = t_msb

            for sig in range(2):
                par = sidx[0] % 2
                po = par * C * 512
                qr = which * T + sig * 512
                qo = qfo + which * T + sig * 512
                ex_toks = []
                sc_toks = []
                for jt in range(C):
                    wsc = []
                    if jt >= 2:
                        wsc.append(("ss", ex_toks[jt - 2]))
                    elif sidx[0] >= 2:
                        wsc.append(("ss", ex_hist[sidx[0] - 2]))
                    else:
                        wsc.append(("ss", S_.cnt["ss"]))
                    if jt == 0:
                        wsc.append(("sv", t_qf))

                    def f_sc(t, jt=jt, kbuf=kbuf, qr=qr, par=par):
                        return t.matmul(PS[2 * par + jt % 2][:, 0:512],
                                        kbuf[:, jt * P:(jt + 1) * P],
                                        QR[:, qr:qr + 512], start=True, stop=True)
                    tsc = S_.step("pe", f_sc, wsc, ("pe", 1))
                    sc_toks.append(tsc)
                    wex = [("pe", tsc)]
                    if jt == 0:
                        wex.append(("sg", t_shift))
                    ex_toks.append(S_.step("ss", lambda s, jt=jt, po=po, par=par: s.activation(
                        PTB[:, po + jt * 512:po + (jt + 1) * 512],
                        PS[2 * par + jt % 2][:, 0:512], AF.Exp,
                        bias=shiftT[:], scale=1.0 / math.sqrt(HD)), wex, ("ss", 1)))
                ex_hist[sidx[0]] = ex_toks[-1]
                if pend:
                    l0_tail(pend.pop())
                pend.append(dict(ii=ii, sig=sig, hh=hh, qo=qo, par=par, ex=ex_toks[-1]))
                sidx[0] += 1
            kfree[ii] = sc_toks[-1]
            nx = ii + 2
            if nx < 16 and nx not in khld:
                hh2 = korder[nx]
                S_.dma("gp", KHB[nx % 2][:, 0:T], gk[hh2 * P:(hh2 + 1) * P, :],
                       [("pe", kfree[ii]), ("sv", t_kt)], sem=f"kh{nx % 2}")
                khld[nx] = (f"kh{nx % 2}", S_.dma("gp", KHB[nx % 2][:, T:S],
                            gk[D + hh2 * P:D + (hh2 + 1) * P, :], sem=f"kh{nx % 2}"))
                S_.dma("gp", VFH[nx % 2][:, 0:T], gv[hh2 * P:(hh2 + 1) * P, :],
                       [("sv", t_vth)], sem=f"vf{nx % 2}")
                vfld[nx] = (f"vf{nx % 2}", S_.dma("gp", VFH[nx % 2][:, T:S],
                            gv[D + hh2 * P:D + (hh2 + 1) * P, :], sem=f"vf{nx % 2}"))
    while pend:
        l0_tail(pend.pop())


    if dbg:
        S_.dma("gp", dbg["d_comb"].rearrange("(c p) t -> p c t", p=P),
               COMB[:].rearrange("p (c t) -> p c t", t=T), [("sv", comb_toks[-1])], sem="dbg")

    scope2a.close()
    XCH = [sb("XCHa", [P, T], F32, scope2), sb("XCHb", [P, T], F32, scope2)]
    old = {0: ("wb0", S_.dma("gp", WB[0][:], wo3[0], [("pe", tq_last[0])], sem="wb0"))}
    old[1] = ("wb1", S_.dma("gp", WB[1][:], wo3[1], [("pe", tq_last[1])], sem="wb1"))
    w_xc = [("spl", t_spill), ("sv", comb_toks[-1])]
    xch = {0: S_.dma("sy", XCH[0][:], io["spill"].ap()[0:P, :], w_xc, sem="xch"),
           1: S_.dma("sy", XCH[1][:], io["spill"].ap()[P:2 * P, :], w_xc, sem="xch")}
    xadd = {}
    x1st = {}
    for j in range(C):
        pa, pb = (PS[0], PS[1]) if j % 2 == 0 else (PS[2], PS[3])
        w = [old[j], ("sv", comb_toks[-1])] + ([("sv", xadd[j - 2])] if j >= 2 else [("ss", S_.cnt["ss"])])
        tj = fm_proj_chunk(S_, WB[j % 2], COMB, 0, pa, pb, w)
        nx = j + 2
        if nx < C:
            sm = wsem[nx % 2]
            old[nx] = (sm, S_.dma("gp", WB[nx % 2][:], wo3[nx], [("pe", tj)], sem=sm))

        def f_xadd(v, j=j, pa=pa, pb=pb):
            xc = XCH[j % 2]
            v.tensor_tensor(xc[:, 0:512], xc[:, 0:512], pa[:, 0:512], ALU.add)
            return v.tensor_tensor(xc[:, 512:1024], xc[:, 512:1024], pb[:, 0:512], ALU.add)
        xadd[j] = S_.step("sv", f_xadd, [("pe", tj), ("xch", xch[j])], ("sv", 1))
        x1st[j] = S_.dma("sy", io["x1_out"][j * P:(j + 1) * P, :], XCH[j % 2][:],
                         [("sv", xadd[j])], sem="x1")
        nx = j + 2
        if nx < C:
            xch[nx] = S_.dma("sy", XCH[nx % 2][:], io["spill"].ap()[nx * P:(nx + 1) * P, :],
                             [("x1", x1st[j])], sem="xch")

    scope2.close()
    allw = [(nm, S_.cnt[nm]) for nm in S_.cnt
            if nm not in ("pe", "ss", "sv", "sg") and S_.cnt[nm] > 0]
    S_.step("sy", lambda sy: sy.nop(), allw)


def build_l0(dbg=False):
    nc = bass.Bass("TRN2", target_bir_lowering=False)
    io = {}
    io["x_in"] = nc.dram_tensor("x_fm", [D, T], F32, kind="ExternalInput").ap()
    io["wq"] = nc.dram_tensor("wq_blk", [C * P, D], F32, kind="ExternalInput").ap()
    io["wk"] = nc.dram_tensor("wk_blk", [C * P, D], F32, kind="ExternalInput").ap()
    io["wv"] = nc.dram_tensor("wv_blk", [C * P, D], F32, kind="ExternalInput").ap()
    io["wo"] = nc.dram_tensor("wo_blk", [C * P, D], F32, kind="ExternalInput").ap()
    io["wf"] = nc.dram_tensor("wf_rows", [D, D], F32, kind="ExternalInput").ap()
    io["bias_sw"] = nc.dram_tensor("bias_sw", [P, C], F32, kind="ExternalInput").ap()
    io["gate_in"] = nc.dram_tensor("gate_row", [1, NH], F32, kind="ExternalInput").ap()
    io["cos0"] = nc.dram_tensor("cos0", [8 * P, T], F32, kind="ExternalInput").ap()
    io["sin0"] = nc.dram_tensor("sin0", [8 * P, T], F32, kind="ExternalInput").ap()
    io["ident_in"] = nc.dram_tensor("ident_in", [P, P], F32, kind="ExternalInput").ap()
    io["x1_out"] = nc.dram_tensor("x1_fm", [D, T], F32, kind="ExternalOutput").ap()
    io["spill"] = nc.dram_tensor("spill_x", [D, T], F32, kind="Internal")
    io["ctb_k"] = nc.dram_tensor("ctb_k", [D, T], BF16, kind="Internal")
    io["ctb_v"] = nc.dram_tensor("ctb_v", [D, T], BF16, kind="Internal")
    io["gth_k"] = nc.dram_tensor("gth_k", [2 * D, T], BF16, kind="Internal")
    io["gth_v"] = nc.dram_tensor("gth_v", [2 * D, T], BF16, kind="Internal")
    dbgd = {}
    if dbg:
        for nm in ("d_xn", "d_k", "d_v", "d_ffn", "d_comb"):
            dbgd[nm] = nc.dram_tensor(nm, [D, T], F32, kind="ExternalOutput").ap()
        dbgd["d_ptb"] = nc.dram_tensor("d_ptb", [P, C * 512], F32, kind="ExternalOutput").ap()
        dbgd["d_zrow"] = nc.dram_tensor("d_zrow", [1, 512], F32, kind="ExternalOutput").ap()
        dbgd["d_den"] = nc.dram_tensor("d_den", [1, 512], F32, kind="ExternalOutput").ap()
        dbgd["d_av"] = nc.dram_tensor("d_av", [P, 512], F32, kind="ExternalOutput").ap()
        dbgd["d_mo"] = nc.dram_tensor("d_mo", [P, 512], F32, kind="ExternalOutput").ap()
        dbgd["d_qr"] = nc.dram_tensor("d_qr", [P, T], F32, kind="ExternalOutput").ap()
        dbgd["d_qf"] = nc.dram_tensor("d_qf", [P, T], F32, kind="ExternalOutput").ap()
        dbgd["d_kfm"] = nc.dram_tensor("d_kfm", [P, S], F32, kind="ExternalOutput").ap()
        dbgd["d_ntb"] = nc.dram_tensor("d_ntb", [P, 1], F32, kind="ExternalOutput").ap()
        dbgd["d_mems"] = nc.dram_tensor("d_mems", [P, P], F32, kind="ExternalOutput").ap()
        dbgd["d_vth"] = nc.dram_tensor("d_vth", [P, S], F32, kind="ExternalOutput").ap()
    with ExitStack() as ctx:
        S_ = Sched(nc, ctx)
        emit_l0(S_, io, dbgd)
        blk = ctx.enter_context(nc.Block())
        S_.emit(blk)
    return nc




NT_Q = 8
E64 = 64
G8 = 8
I14 = 1408
IC = I14 // P
KVR = 512
KC4 = KVR // P


def emit_l1(S_, io, dbg, stop_after=None):
    PS = S_.PS
    PT = S_.PT
    sb = S_.sb
    SL = S
    NSL = SL // 512

    XN = sb("XN1", [P, C * T], BF16)
    ident = sb("ident1", [P, P], BF16)
    ones_b = sb("ones1b", [P, 1], BF16)
    ones_f = sb("ones1f", [1, P], F32)
    ones8 = sb("ones8", [8, P], BF16)
    shiftT = sb("shiftT1", [P, 1], F32)
    epsT = sb("epsT1", [1, 1], F32)
    eps2T = sb("eps2T", [1, 1], F32)
    cos1 = sb("cos1t", [P, T], F32)
    sin1 = sb("sin1t", [P, T], F32)
    R128B = sb("R128B", [P, P], BF16)
    KPEF = sb("KPEF", [P, SL], BF16)
    WB = [sb("wb1A", [P, C * P], BF16), sb("wb1B", [P, C * P], BF16)]
    scrA = sb("scrA1", [P, T], F32)
    scrB = sb("scrB1", [P, T], F32)
    scrC = sb("scrC1", [P, T], F32)
    Pmat = sb("Pmat", [64, 64], BF16)
    iota64 = sb("iota64t", [64, 1], F32)
    invf = sb("invf", [1, 64], F32)
    invi = sb("invi", [1, 64], I32)
    seg1 = sb("seg1", [P, 64], F32)
    seg2 = sb("seg2", [P, 64], F32)
    sg8a = sb("sg8a", [P, 8], F32)
    sg8b = sb("sg8b", [P, 8], F32)
    mx8 = sb("mx8", [P, 8], F32)
    rz128 = sb("rz128", [P, 1], F32)
    kvnsw = sb("kvnswt", [P, KC4], F32)

    xd = io["x_in"].rearrange("(c p) t -> p c t", p=P)
    kb3 = io["wkvb_k"].rearrange("(j p) k -> j p k", p=P)
    vb3 = io["wkvb_v"].rearrange("(j p) k -> j p k", p=P)
    wq3 = io["wq1"].rearrange("(j p) k -> j p k", p=P)
    wo3 = io["wo1"].rearrange("(j p) k -> j p k", p=P)
    wkva3 = io["wkva"].rearrange("(j p) k -> j p k", p=P)
    gpw4 = io["gpw"].rearrange("(g j p) k -> g j p k", j=IC, p=P)
    upw4 = io["upw"].rearrange("(g j p) k -> g j p k", j=IC, p=P)
    dnw4 = io["dnw"].rearrange("(g j p) k -> g j p k", j=C, p=P)

    t_on1 = S_.step("gp", lambda g: g.memset(ones_b[:], 1.0), (), ("sg", 1))
    S_.step("gp", lambda g: g.memset(ones_f[:], 1.0), (), ("sg", 1))
    S_.step("gp", lambda g: g.memset(ones8[:], 1.0), (), ("sg", 1))
    S_.step("gp", lambda g: g.memset(shiftT[:], SOFT_SHIFT), (), ("sg", 1))
    S_.step("gp", lambda g: g.memset(epsT[:], 1e-5), (), ("sg", 1))
    t_ms = S_.step("gp", lambda g: g.memset(eps2T[:], 1e-6), (), ("sg", 1))
    S_.dma("gp", ident[:], io["ident_in"], sem="once")
    S_.dma("gp", XN[:].rearrange("p (c t) -> p c t", t=T), xd, sem="once")
    S_.dma("gp", R128B[:], io["r128"], sem="once")
    S_.dma("sy", cos1[:], io["cos1"], sem="once")
    S_.dma("sy", sin1[:], io["sin1"], sem="once")
    S_.dma("sy", iota64[:], io["iota64"], sem="once")
    S_.dma("sy", invi[:], io["inv_row"], sem="once")
    t_once = S_.dma("sy", kvnsw[:], io["kvnsw"], sem="once")

    sAB = ExitStack()
    rowA = sb("rowA1", [1, T], F32, sAB)
    rowB = sb("rowB1", [1, T], F32, sAB)
    bcM = sb("bcM1", [P, T], F32, sAB)
    bcR = sb("bcR1", [P, T], F32, sAB)
    gwpm = sb("gwpm", [P, C * E64], F32, sAB)
    t_gw = S_.dma("sy", gwpm[:], io["gatew"], sem="once")

    sA = ExitStack()
    X = sb("X1", [P, C * T], F32, sA)
    SQ2 = sb("SQ21", [P, 2 * T], BF16, sA)
    XF = [sb("XFa", [P, T], F32, sA), sb("XFb", [P, T], F32, sA)]
    t_x = S_.dma("sy", X[:].rearrange("p (c t) -> p c t", t=T), xd, sem="once")

    sq_tok = {}
    mm_tok = {}
    for c in range(C):
        wsq = [("once", t_x)] if c == 0 else []
        if c >= 2:
            wsq.append(("pe", mm_tok[c - 2]))
        sq_tok[c] = S_.step("ss", lambda s, c=c: s.activation(
            SQ2[:, (c % 2) * T:(c % 2 + 1) * T], X[:, c * T:(c + 1) * T], AF.Square),
            wsq, ("ss", 1))
        def f_mm(t, c=c):
            mm = None
            for s2, pp in ((0, PS[0]), (1, PS[1])):
                t.matmul(pp[0:1, 0:512], ones_b[:],
                         XN[:, c * T + s2 * 512: c * T + (s2 + 1) * 512],
                         start=(c == 0), stop=(c == C - 1))
            for s2, pp in ((0, PS[2]), (1, PS[3])):
                mm = t.matmul(pp[0:1, 0:512], ones_b[:],
                              SQ2[:, (c % 2) * T + s2 * 512: (c % 2) * T + (s2 + 1) * 512],
                              start=(c == 0), stop=(c == C - 1))
            return mm
        wmm = [("ss", sq_tok[c])]
        if c == 0:
            wmm += [("sg", t_ms)]
        mm_tok[c] = S_.step("pe", f_mm, wmm, ("pe", 1))
    t_stats = mm_tok[C - 1]

    def f_rows(v):
        v.tensor_scalar(rowA[0:1, 0:512], PS[0][0:1, 0:512], 1.0 / D, None, ALU.mult)
        v.tensor_scalar(rowA[0:1, 512:1024], PS[1][0:1, 0:512], 1.0 / D, None, ALU.mult)
        v.tensor_scalar(rowB[0:1, 0:512], PS[2][0:1, 0:512], 1.0 / D, None, ALU.mult)
        v.tensor_scalar(rowB[0:1, 512:1024], PS[3][0:1, 0:512], 1.0 / D, None, ALU.mult)
        v.tensor_tensor(scrC[0:1, :], rowA[0:1, :], rowA[0:1, :], ALU.mult)
        v.tensor_tensor(rowB[0:1, :], rowB[0:1, :], scrC[0:1, :], ALU.subtract)
        return v.tensor_scalar(rowA[0:1, :], rowA[0:1, :], -1.0, None, ALU.mult)
    t_var = S_.step("sv", f_rows, [("pe", t_stats)], ("sv", 1))
    t_sd = S_.step("ss", lambda s: s.activation(rowB[0:1, :], rowB[0:1, :], AF.Sqrt, bias=epsT[:]),
                   [("sv", t_var)], ("ss", 1))
    t_rs = S_.step("sv", lambda v: v.reciprocal(rowB[0:1, :], rowB[0:1, :]),
                   [("ss", t_sd)], ("sv", 1))
    t_bc = None
    for s2 in range(2):
        def f_bcs(t, s2=s2):
            t.matmul(PS[4][:, 0:512], ones_f[:], rowA[0:1, s2 * 512:(s2 + 1) * 512],
                     start=True, stop=True)
            return t.matmul(PS[5][:, 0:512], ones_f[:], rowB[0:1, s2 * 512:(s2 + 1) * 512],
                            start=True, stop=True)
        w = [("sv", t_rs)] if s2 == 0 else [("sv", t_bc)]
        tk = S_.step("pe", f_bcs, w, ("pe", 1))
        def f_bce(v, s2=s2):
            v.tensor_copy(bcM[:, s2 * 512:(s2 + 1) * 512], PS[4][:, 0:512])
            return v.tensor_copy(bcR[:, s2 * 512:(s2 + 1) * 512], PS[5][:, 0:512])
        t_bc = S_.step("sv", f_bce, [("pe", tk)], ("sv", 1))

    nrm_tok = {}
    gmm_tok = {}
    for c in range(C):
        def f_nrm(v, c=c):
            v.tensor_tensor(scrC[:], X[:, c * T:(c + 1) * T], bcM[:], ALU.add)
            v.tensor_tensor(XF[c % 2][:], scrC[:], bcR[:], ALU.mult)
            return v.tensor_tensor(XN[:, c * T:(c + 1) * T], scrC[:], bcR[:], ALU.mult)
        wn = [("sv", t_bc)] if c == 0 else []
        if c >= 2:
            wn.append(("pe", gmm_tok[c - 2]))
        nrm_tok[c] = S_.step("sv", f_nrm, wn, ("sv", 1))
        def f_gmm(t, c=c):
            mm = None
            for tt in range(NT_Q):
                mm = t.matmul(PS[6][:, tt * E64:(tt + 1) * E64],
                              XF[c % 2][:, tt * P:(tt + 1) * P],
                              gwpm[:, c * E64:(c + 1) * E64],
                              start=(c == 0 and tt == 0), stop=(c == C - 1))
            return mm
        gmm_tok[c] = S_.step("pe", f_gmm, [("sv", nrm_tok[c]), ("once", t_gw)] if c == 0
                             else [("sv", nrm_tok[c])], ("pe", 1))
    t_xn = nrm_tok[C - 1]
    t_gl = gmm_tok[C - 1]
    if dbg:
        S_.dma("gp", dbg["d_xn1"].rearrange("(c p) t -> p c t", p=P),
               XN[:].rearrange("p (c t) -> p c t", t=T), [("sv", t_xn)], sem="dbg")
    sA.close()

    sB = ExitStack()
    GSC = sb("GSC", [P, 512], F32, sB)
    W_E = sb("W_E", [P, 512], F32, sB)
    FLAT = sb("FLATB", [P, 512], F32, sB)
    FLATBF = sb("FLATBF", [P, 512], BF16, sB)
    SOFTM = sb("SOFTM", [P, 512], BF16, sB)
    SOFTG = [sb(f"SOFTG{g}", [8, T], BF16, sB) for g in range(G8)]
    FLATG = [sb(f"FLATG{g}", [8, T], BF16, sB) for g in range(G8)]
    WET = sb("WET", [64, T], BF16, sB)
    FLT_J = sb("FLT_J", [64, T], BF16, sB)
    CKF = sb("CKF", [P, KC4 * T], F32, sB)
    KPE32 = sb("KPE32", [64, T], F32, sB)
    WPE = sb("WPE", [P, C * 64], BF16, sB)
    SQK = sb("SQK", [P, 2 * T], BF16, sB)
    CKN = sb("CKN", [P, KC4 * T], BF16, sB)
    KPER = sb("KPER", [64, T], BF16, sB)
    CKVF = sb("CKVF", [P, KC4 * SL], BF16, sB)
    KCH = sb("KCH", [P, SL], BF16, sB)

    t_gsc = S_.step("sv", lambda v: v.tensor_copy(GSC[:], PS[6][:, 0:512]),
                    [("pe", t_gl), ("sv", t_xn)], ("sv", 1))
    we_toks = []
    for tt in range(NT_Q):
        sl = GSC[:, tt * E64:(tt + 1) * E64]
        t_mx = S_.step("sv", lambda v, sl=sl: v.reduce_max(rz128[:], sl, AX.X, negate=True),
                       [("sv", t_gsc)] if tt == 0 else [], ("sv", 1))
        t_ex = S_.step("ss", lambda s, sl=sl: s.activation(
            sl, sl, AF.Exp, bias=rz128[:], accum_out=sg8a[:, 0:1]),
            [("sv", t_mx)], ("ss", 1))
        t_rc = S_.step("sv", lambda v: v.reciprocal(rz128[:], sg8a[:, 0:1]),
                       [("ss", t_ex)], ("sv", 1))
        t_sm = S_.step("gp", lambda g, sl=sl: g.tensor_scalar(sl, sl, rz128[:], None, ALU.mult),
                       [("sv", t_rc)], ("sg", 1))
        t_m8 = S_.step("sv", lambda v, sl=sl: v.max(mx8[:], sl), [("sg", t_sm)], ("sv", 1))
        def f_we(g, sl=sl, tt=tt):
            g.tensor_scalar(seg1[:], sl, mx8[:, 7:8], None, ALU.is_ge)
            return g.tensor_tensor(W_E[:, tt * E64:(tt + 1) * E64], seg1[:], sl, ALU.mult)
        we_toks.append(S_.step("gp", f_we, [("sv", t_m8)], ("sg", 1)))

    t_if = S_.step("sv", lambda v: v.tensor_copy(invf[:], invi[:]), [("once", t_once)], ("sv", 1))
    t_ib = S_.step("pe", lambda t: t.matmul(PS[0][0:64, 0:64], ones_f[0:1, 0:64], invf[:],
                                            start=True, stop=True),
                   [("sv", t_if), ("sv", S_.cnt["sv"]), ("ss", S_.cnt["ss"])], ("pe", 1))
    t_pm = S_.step("sv", lambda v: v.tensor_scalar(Pmat[:], PS[0][0:64, 0:64], iota64[:],
                                                   None, ALU.is_equal),
                   [("pe", t_ib)], ("sv", 1))

    t_web = S_.step("ss", lambda s: s.activation(FLATBF[:], W_E[:], AF.Copy),
                    [("sg", we_toks[-1])], ("ss", 1))
    wet_tok = None
    for r in range(2):
        def f_tw(t, r=r):
            mm = None
            for i in range(4):
                tt = r * 4 + i
                mm = t.transpose(PT[0:64, i * P:(i + 1) * P],
                                 FLATBF[:, tt * E64:(tt + 1) * E64], ident[:])
            return mm
        w = [("ss", t_web)] + ([("sv", S_.last_pt_evac)] if S_.last_pt_evac else [])
        if wet_tok:
            w = [("sv", wet_tok)]
        tp = S_.step("pe", f_tw, w, ("pe", 1))
        wet_tok = S_.step("sv", lambda v, r=r: v.tensor_copy(
            WET[0:64, r * 4 * P:(r + 1) * 4 * P], PT[0:64, 0:4 * P]), [("pe", tp)], ("sv", 1))
    S_.last_pt_evac = wet_tok

    def f_flat(t):
        t.matmul(PS[1][0:64, 0:512], Pmat[:], WET[0:64, 0:512], start=True, stop=True)
        return t.matmul(PS[2][0:64, 0:512], Pmat[:], WET[0:64, 512:1024], start=True, stop=True)
    t_ft = S_.step("pe", f_flat, [("sv", wet_tok), ("sv", t_pm)], ("pe", 1))
    t_fj = S_.step("sv", lambda v: (v.tensor_copy(FLT_J[0:64, 0:512], PS[1][0:64, 0:512]),
                                    v.tensor_copy(FLT_J[0:64, 512:1024], PS[2][0:64, 0:512]))[-1],
                   [("pe", t_ft)], ("sv", 1))
    flat_tok = None
    for r in range(2):
        def f_tf(t, r=r):
            mm = None
            for i in range(4):
                tt = r * 4 + i
                mm = t.transpose(PT[:, i * E64:(i + 1) * E64],
                                 FLT_J[0:64, tt * P:(tt + 1) * P], ident[0:64, 0:64])
            return mm
        w = [("sv", t_fj), ("sv", S_.last_pt_evac)]
        if flat_tok:
            w = [("sv", flat_tok)]
        tp = S_.step("pe", f_tf, w, ("pe", 1))
        flat_tok = S_.step("sv", lambda v, r=r: v.tensor_copy(
            FLAT[:, r * 4 * E64:(r + 1) * 4 * E64].rearrange("p (i e) -> p i e", e=E64),
            PT[:, 0:4 * E64].rearrange("p (i e) -> p i e", e=E64)), [("pe", tp)], ("sv", 1))
    S_.last_pt_evac = flat_tok

    soft_toks = []
    for tt in range(NT_Q):
        fl = FLAT[:, tt * E64:(tt + 1) * E64]
        def f_msk(v, fl=fl):
            v.tensor_scalar(seg1[:], fl, 0.0, None, ALU.is_equal)
            v.scalar_tensor_tensor(seg1[:], seg1[:], -1e9, fl, ALU.mult, ALU.add)
            return v.tensor_reduce(sg8a[:], seg1[:].rearrange("p (g s) -> p g s", s=8),
                                   AX.X, ALU.max)
        t_mk = S_.step("sv", f_msk, [("sv", flat_tok)] if tt == 0 else [], ("sv", 1))
        def f_sub(g):
            return g.tensor_tensor(seg2[:].rearrange("p (g s) -> p g s", s=8),
                                   seg1[:].rearrange("p (g s) -> p g s", s=8),
                                   sg8a[:].unsqueeze(2).broadcast_to([P, 8, 8]),
                                   ALU.subtract)
        t_sb2 = S_.step("gp", f_sub, [("sv", t_mk)], ("sg", 1))
        t_ex2 = S_.step("ss", lambda s: s.activation(seg2[:], seg2[:], AF.Exp),
                        [("sg", t_sb2)], ("ss", 1))
        t_rd2 = S_.step("sv", lambda v: v.tensor_reduce(
            sg8a[:], seg2[:].rearrange("p (g s) -> p g s", s=8), AX.X, ALU.add),
            [("ss", t_ex2)], ("sv", 1))
        t_cp2 = S_.step("gp", lambda g: g.tensor_copy(mx8[:], sg8a[:]),
                        [("sv", t_rd2)], ("sg", 1))
        t_ss2 = S_.step("sv", lambda v: v.reciprocal(sg8b[:], mx8[:]),
                        [("sg", t_cp2)], ("sv", 1))
        def f_div(g, tt=tt):
            return g.tensor_tensor(SOFTM[:, tt * E64:(tt + 1) * E64].rearrange(
                "p (g s) -> p g s", s=8),
                seg2[:].rearrange("p (g s) -> p g s", s=8),
                sg8b[:].unsqueeze(2).broadcast_to([P, 8, 8]), ALU.mult)
        soft_toks.append(S_.step("gp", f_div, [("sv", t_ss2)], ("sg", 1)))

    tg_tok = None
    for tt in range(NT_Q):
        def f_tg(t, tt=tt):
            mm = None
            for g in range(G8):
                mm = t.transpose(PT[0:8, g * P:(g + 1) * P],
                                 SOFTM[:, tt * E64 + g * 8: tt * E64 + (g + 1) * 8], ident[:])
            return mm
        w = [("sg", soft_toks[tt]), ("sv", S_.last_pt_evac)]
        if tg_tok:
            w = [("sg", soft_toks[tt]), ("sv", tg_tok)]
        tp = S_.step("pe", f_tg, w, ("pe", 1))
        def f_eg(v, tt=tt):
            last = None
            for g in range(G8):
                last = v.tensor_copy(SOFTG[g][:, tt * P:(tt + 1) * P], PT[0:8, g * P:(g + 1) * P])
            return last
        tg_tok = S_.step("sv", f_eg, [("pe", tp)], ("sv", 1))
        t_fb = S_.step("ss", lambda s, tt=tt: s.activation(
            FLATBF[:, tt * E64:(tt + 1) * E64], FLAT[:, tt * E64:(tt + 1) * E64], AF.Copy),
            [("sv", tg_tok)] if tt == 0 else [], ("ss", 1))
        def f_tg2(t, tt=tt):
            mm = None
            for g in range(G8):
                mm = t.transpose(PT[0:8, g * P:(g + 1) * P],
                                 FLATBF[:, tt * E64 + g * 8: tt * E64 + (g + 1) * 8], ident[:])
            return mm
        tp2 = S_.step("pe", f_tg2, [("ss", t_fb), ("sv", tg_tok)], ("pe", 1))
        def f_eg2(v, tt=tt):
            last = None
            for g in range(G8):
                last = v.tensor_copy(FLATG[g][:, tt * P:(tt + 1) * P], PT[0:8, g * P:(g + 1) * P])
            return last
        tg_tok = S_.step("sv", f_eg2, [("pe", tp2)], ("sv", 1))
    S_.last_pt_evac = tg_tok

    if dbg:
        S_.dma("gp", dbg["d_we"], W_E[:], [("sg", we_toks[-1])], sem="dbg")
        S_.dma("gp", dbg["d_soft"], SOFTM[:], [("sv", tg_tok)], sem="dbg")
        S_.dma("gp", dbg["d_flat"], FLAT[:], [("sv", tg_tok)], sem="dbg")
        S_.step("sv", lambda v: v.engine_nop(), [("dbg", S_.cnt["dbg"])], ("sv", 1))
    sg_st = None
    for g in range(G8):
        S_.dma("sy", io["softg_d"].ap()[g * 8:(g + 1) * 8, :], SOFTG[g][:],
               [("sv", tg_tok)] if g == 0 else [], sem="sgst")
        sg_st = S_.dma("sy", io["flatg_d"].ap()[g * 8:(g + 1) * 8, :], FLATG[g][:], sem="sgst")

    if stop_after == "gate":
        sB.close()
        allw = [(nm, S_.cnt[nm]) for nm in S_.cnt
                if nm not in ("pe", "ss", "sv", "sg") and S_.cnt[nm] > 0]
        S_.step("sy", lambda sy: sy.nop(), allw)
        return
    kva_ld = {}
    for j in range(2):
        kva_ld[j] = S_.dma("gp", WB[j % 2][:], wkva3[j], [("pe", S_.cnt["pe"])] if j == 0 else [],
                           sem=f"wb{j % 2}")
    t_wpe = S_.dma("gp", WPE[:], io["wkva_pe"], sem="once")
    ck_ev = {}
    for j in range(KC4):
        pa, pb = (PS[0], PS[1]) if j % 2 == 0 else (PS[2], PS[3])
        w = [(f"wb{j % 2}", kva_ld[j])] + ([("sv", ck_ev[j - 2])] if j >= 2
                                           else [("sv", S_.cnt["sv"]), ("ss", S_.cnt["ss"])])
        tj = fm_proj_chunk(S_, WB[j % 2], XN, 0, pa, pb, w)
        if j + 2 < KC4:
            kva_ld[j + 2] = S_.dma("gp", WB[j % 2][:], wkva3[j + 2], [("pe", tj)], sem=f"wb{j % 2}")
        def f_ev(v, j=j):
            pa, pb = (PS[0], PS[1]) if j % 2 == 0 else (PS[2], PS[3])
            v.tensor_copy(CKF[:, j * T:j * T + 512], pa[:, 0:512])
            return v.tensor_copy(CKF[:, j * T + 512:(j + 1) * T], pb[:, 0:512])
        ck_ev[j] = S_.step("sv", f_ev, [("pe", tj)], ("sv", 1))
    def f_kpe(t):
        mm = None
        for c in range(C):
            for s2, pp in ((0, PS[2]), (1, PS[3])):
                mm = t.matmul(pp[0:64, 0:512], WPE[:, c * 64:(c + 1) * 64],
                              XN[:, c * T + s2 * 512: c * T + (s2 + 1) * 512],
                              start=(c == 0), stop=(c == C - 1))
        return mm
    t_kpm = S_.step("pe", f_kpe, [("once", t_wpe), ("sv", ck_ev[KC4 - 1])], ("pe", 1))
    t_kpev = S_.step("sv", lambda v: (v.tensor_copy(KPE32[0:64, 0:512], PS[2][0:64, 0:512]),
                                      v.tensor_copy(KPE32[0:64, 512:1024], PS[3][0:64, 0:512]))[-1],
                     [("pe", t_kpm)], ("sv", 1))

    rms_sq = {}
    rms_mm = {}
    for j in range(KC4):
        wsq = [("sv", ck_ev[j])]
        if j >= 2:
            wsq.append(("pe", rms_mm[j - 2]))
        rms_sq[j] = S_.step("ss", lambda s, j=j: s.activation(
            SQK[:, (j % 2) * T:(j % 2 + 1) * T], CKF[:, j * T:(j + 1) * T], AF.Square),
            wsq, ("ss", 1))
        def f_rmm(t, j=j):
            mm = None
            for s2, pp in ((0, PS[4]), (1, PS[5])):
                mm = t.matmul(pp[0:1, 0:512], ones_b[:],
                              SQK[:, (j % 2) * T + s2 * 512:(j % 2) * T + (s2 + 1) * 512],
                              start=(j == 0), stop=(j == KC4 - 1))
            return mm
        rms_mm[j] = S_.step("pe", f_rmm, [("ss", rms_sq[j])], ("pe", 1))
    def f_rvar(v):
        v.tensor_scalar(rowA[0:1, 0:512], PS[4][0:1, 0:512], 1.0 / KVR, None, ALU.mult)
        return v.tensor_scalar(rowA[0:1, 512:1024], PS[5][0:1, 0:512], 1.0 / KVR, None, ALU.mult)
    t_rv = S_.step("sv", f_rvar, [("pe", rms_mm[KC4 - 1])], ("sv", 1))
    t_rsd = S_.step("ss", lambda s: s.activation(rowA[0:1, :], rowA[0:1, :], AF.Sqrt, bias=eps2T[:]),
                    [("sv", t_rv)], ("ss", 1))
    t_rrc = S_.step("sv", lambda v: v.reciprocal(rowA[0:1, :], rowA[0:1, :]),
                    [("ss", t_rsd)], ("sv", 1))
    t_rbc = None
    for s2 in range(2):
        tk = S_.step("pe", lambda t, s2=s2: t.matmul(
            PS[6][:, 0:512], ones_f[:], rowA[0:1, s2 * 512:(s2 + 1) * 512], start=True, stop=True),
            [("sv", t_rrc)] if s2 == 0 else [("sv", t_rbc)], ("pe", 1))
        t_rbc = S_.step("sv", lambda v, s2=s2: v.tensor_copy(
            bcR[:, s2 * 512:(s2 + 1) * 512], PS[6][:, 0:512]), [("pe", tk)], ("sv", 1))
    ckn_tok = None
    for j in range(KC4):
        def f_ckn(v, j=j):
            v.tensor_tensor(scrA[:], CKF[:, j * T:(j + 1) * T], bcR[:], ALU.mult)
            return v.tensor_scalar(CKN[:, j * T:(j + 1) * T], scrA[:], kvnsw[:, j:j + 1],
                                   None, ALU.mult)
        ckn_tok = S_.step("sv", f_ckn, [("sv", t_rbc)] if j == 0 else [], ("sv", 1))
    t_kpb = S_.step("ss", lambda s: s.copy(KPER[0:64, :], KPE32[0:64, :]),
                    [("sv", t_kpev)], ("ss", 1))
    t_rotk = S_.step("pe", lambda t: (t.matmul(PS[2][0:64, 0:512], R128B[0:64, 0:64],
                                               KPER[0:64, 0:512], start=True, stop=True),
                                      t.matmul(PS[3][0:64, 0:512], R128B[0:64, 0:64],
                                               KPER[0:64, 512:1024], start=True, stop=True))[-1],
                     [("ss", t_kpb)], ("pe", 1))
    def f_kper(v):
        v.tensor_tensor(scrA[0:64, :], KPE32[0:64, :], cos1[0:64, :], ALU.mult)
        v.tensor_tensor(scrB[0:64, 0:512], PS[2][0:64, 0:512], sin1[0:64, 0:512], ALU.mult)
        v.tensor_tensor(scrB[0:64, 512:1024], PS[3][0:64, 0:512], sin1[0:64, 512:1024], ALU.mult)
        return v.tensor_tensor(KPER[0:64, :], scrA[0:64, :], scrB[0:64, :], ALU.add)
    t_kper = S_.step("sv", f_kper, [("pe", t_rotk)], ("sv", 1))

    S_.sem("ckl"); S_.sem("kpl"); S_.sem("ckst")
    t_ckst = S_.dma("sy", io["ctb_ckv"].ap().rearrange("(c p) t -> p c t", p=P),
                    CKN[:].rearrange("p (c t) -> p c t", t=T),
                    [("sv", ckn_tok)], sem="ckst")
    t_kpst = S_.dma("sy", io["ctb_kpe"].ap(), KPER[0:64, :], [("sv", t_kper)], sem="ckst")
    t_agc = S_.step("gp", lambda g: g.collective_compute(
        "AllGather", ALU.bypass, replica_groups=RG_PAIRS,
        ins=[io["ctb_ckv"].ap().opt()], outs=[io["gth_ckv"].ap().opt()]),
        [("ckst", t_kpst)], ("cc0", 1))
    t_agp = S_.step("gp", lambda g: g.collective_compute(
        "AllGather", ALU.bypass, replica_groups=RG_PAIRS,
        ins=[io["ctb_kpe"].ap().opt()], outs=[io["gth_kpe"].ap().opt()]),
        (), ("cc1", 1))
    gc = io["gth_ckv"].ap()
    gpe = io["gth_kpe"].ap()
    for c in range(KC4):
        S_.dma("gp", CKVF[:, c * SL:c * SL + T], gc[c * P:(c + 1) * P, :],
               [("cc0", t_agc)] if c == 0 else [], sem="ckl")
        ck_last = S_.dma("gp", CKVF[:, c * SL + T:(c + 1) * SL],
                         gc[KVR + c * P:KVR + (c + 1) * P, :], sem="ckl")
    S_.dma("gp", KPEF[0:64, 0:T], gpe[0:64, :], [("cc1", t_agp)], sem="kpl")
    S_.dma("gp", KPEF[0:64, T:S], gpe[64:128, :], sem="kpl")
    S_.dma("gp", KPEF[64:128, 0:T], gpe[0:64, :], sem="kpl")
    t_kpef = S_.dma("gp", KPEF[64:128, T:S], gpe[64:128, :], sem="kpl")

    def kvb_proj(blocks3, dram_out, name):
        mmrec = {}
        stp = None
        evp = None
        for j in range(C):
            sm = f"wb{j % 2}"
            tld = S_.dma("gp", WB[j % 2][:, 0:KC4 * P], blocks3[j],
                         [("pe", S_.cnt["pe"])] if j < 2 else [("pe", mmrec[j - 2])], sem=sm)
            w = [(sm, tld), ("ckl", ck_last)]
            if evp is not None:
                w.append(("ss", evp))
            else:
                w += [("sv", S_.cnt["sv"]), ("ss", S_.cnt["ss"])]
            def f_k(t, j=j):
                mm = None
                for c in range(KC4):
                    for s4 in range(NSL):
                        pp = PS[s4]
                        mm = t.matmul(pp[:, 0:512], WB[j % 2][:, c * P:(c + 1) * P],
                                      CKVF[:, c * SL + s4 * 512: c * SL + (s4 + 1) * 512],
                                      start=(c == 0), stop=(c == KC4 - 1))
                return mm
            tmm = S_.step("pe", f_k, w, ("pe", 1))
            mmrec[j] = tmm
            def f_ev(s):
                last = None
                for s4 in range(NSL):
                    last = s.activation(KCH[:, s4 * 512:(s4 + 1) * 512], PS[s4][:, 0:512], AF.Copy)
                return last
            wv_ = [("pe", tmm)]
            if stp is not None:
                wv_.append((name, stp))
            evp = S_.step("ss", f_ev, wv_, ("ss", 1))
            stp = S_.dma("sy", dram_out[j * P:(j + 1) * P, :], KCH[:], [("ss", evp)], sem=name)
        return stp

    if stop_after == "kva":
        sB.close()
        allw = [(nm, S_.cnt[nm]) for nm in S_.cnt
                if nm not in ("pe", "ss", "sv", "sg") and S_.cnt[nm] > 0]
        S_.step("sy", lambda sy: sy.nop(), allw)
        return
    t_knst = kvb_proj(kb3, io["kn_dram"].ap(), "knst")
    t_vnst = kvb_proj(vb3, io["vn_dram"].ap(), "vnst")
    sB.close()
    sAB.close()
    if stop_after == "kvb":
        allw = [(nm, S_.cnt[nm]) for nm in S_.cnt
                if nm not in ("pe", "ss", "sv", "sg") and S_.cnt[nm] > 0]
        S_.step("sy", lambda sy: sy.nop(), allw)
        return

    scope2 = ExitStack()
    COMB = sb("COMB1", [P, NH * T], BF16, scope2)
    KHB = [sb("KH10", [P, SL], BF16, scope2), sb("KH11", [P, SL], BF16, scope2)]
    VFH = [sb("VF10", [P, SL], BF16, scope2), sb("VF11", [P, SL], BF16, scope2)]
    VTH = [sb("VT10", [P, SL], BF16, scope2), sb("VT11", [P, SL], BF16, scope2)]
    QNB = [sb("QN0", [P, T], BF16, scope2), sb("QN1", [P, T], BF16, scope2)]
    QPE = sb("QPE", [P, T], BF16, scope2)
    QPEB = sb("QPEB", [P, T], BF16, scope2)
    QPW = sb("QPW", [P, C * P], BF16, scope2)
    PTB = sb("PTB1", [P, 28 * 512], BF16, scope2)
    MSKF = sb("MSKF", [P, 24 * 512], BF16, scope2)
    rz_sb = sb("rz1", [1, 512], F32, scope2)
    bz_sb = sb("bz1", [P, 512], F32, scope2)
    XCH = [sb("XCH1a", [P, T], F32, scope2), sb("XCH1b", [P, T], F32, scope2)]

    gk = io["kn_dram"].ap()
    gv = io["vn_dram"].ap()
    qld = {}
    khld = {}
    vfld = {}
    for pos in range(2):
        qld[pos] = (f"wb{pos}", S_.dma("gp", WB[pos][:], wq3[pos],
                    [("pe", S_.cnt["pe"]), ("knst", t_knst), ("vnst", t_vnst),
                     ("sgst", sg_st)] if pos == 0 else [], sem=f"wb{pos}"))
        khld[pos] = (f"kh{pos}", S_.dma("gp", KHB[pos][:], gk[pos * P:(pos + 1) * P, 0:SL],
                     sem=f"kh{pos}"))
        vfld[pos] = (f"vf{pos}", S_.dma("gp", VFH[pos][:], gv[pos * P:(pos + 1) * P, 0:SL],
                     sem=f"vf{pos}"))
    t_mskf = S_.dma("gp", MSKF[:, 0:12 * 512].rearrange("p (m n) -> p m n", n=512),
                    io["masks"].rearrange("(m p) n -> p m n", p=P)[:, 0:12, :], sem="once")
    t_mskf = S_.dma("gp", MSKF[:, 12 * 512:24 * 512].rearrange("p (m n) -> p m n", n=512),
                    io["masks"].rearrange("(m p) n -> p m n", p=P)[:, 20:32, :], sem="once")
    qpe_ld = {0: S_.dma("gp", QPW[:], wq3[16], sem="qpw")}

    qevac = {}
    kfree = {}
    comb_toks = []
    zav1 = {}
    ex1 = {}
    pend1 = []
    s1 = [0]
    last_qpe = [None]

    def l1_tail(d):
        ii_, sig_, par_, ntl_ = d["ii"], d["sig"], d["par"], d["ntl"]
        po = 0 if sig_ == 0 else 12 * 512

        def f_zav(t, ii_=ii_, po=po, ntl_=ntl_):
            for jt in range(ntl_):
                t.matmul(PS[5][0:1, 0:512], ones_b[:], PTB[:, po + jt * 512:po + (jt + 1) * 512],
                         start=(jt == 0), stop=(jt == ntl_ - 1))
            mm = None
            for jt in range(ntl_):
                mm = t.matmul(PS[4][:, 0:512], VTH[ii_ % 2][:, jt * P:(jt + 1) * P],
                              PTB[:, po + jt * 512:po + (jt + 1) * 512],
                              start=(jt == 0), stop=(jt == ntl_ - 1))
            return mm
        wz = [("sv", d["msk"]), ("sv", d["vth"])]
        if last_qpe[0] is not None:
            wz.append(("sv", last_qpe[0]))
        wz.append(("sv", comb_toks[-1]) if comb_toks else ("sv", S_.cnt["sv"]))
        tz = S_.step("pe", f_zav, wz, ("pe", 1))
        zav1[(ii_, sig_)] = tz
        t_rcp = S_.step("sv", lambda v: v.reciprocal(rz_sb[0:1, :], PS[5][0:1, 0:512]),
                        [("pe", tz)], ("sv", 1))
        t_bz = S_.step("pe", lambda t: t.matmul(PS[6][:, 0:512], ones_f[:], rz_sb[0:1, :],
                                                start=True, stop=True),
                       [("sv", t_rcp)], ("pe", 1))
        t_bzc = S_.step("sv", lambda v: v.tensor_copy(bz_sb[:], PS[6][:, 0:512]),
                        [("pe", t_bz)], ("sv", 1))

        def f_out(v, ii_=ii_, sig_=sig_):
            return v.tensor_tensor(COMB[:, ii_ * T + sig_ * 512: ii_ * T + (sig_ + 1) * 512],
                                   PS[4][:, 0:512], bz_sb[:], ALU.mult)
        comb_toks.append(S_.step("sv", f_out, [("sv", t_bzc)], ("sv", 1)))

    for ip in range(8):
        iA, iB = 2 * ip, 2 * ip + 1
        wA = [qld[iA]] + ([("ss", qevac[iA - 2])] if iA >= 2
                          else [("ss", S_.cnt["ss"]), ("sv", S_.cnt["sv"])])
        if s1[0] >= 2:
            wA.append(("ss", ex1[s1[0] - 2]))
        tA = fm_proj_chunk(S_, WB[iA % 2], XN, 0, PS[0], PS[1], wA)
        wB = [qld[iB]] + ([("ss", qevac[iB - 2])] if iB >= 2 else [])
        if s1[0] >= 1:
            wB.append(("ss", ex1[s1[0] - 1]))
        tB = fm_proj_chunk(S_, WB[iB % 2], XN, 0, PS[2], PS[3], wB)
        if ip == 7:
            tq_last = (tA, tB)
        for nx in (iA + 2, iB + 2):
            if nx < 16 and nx not in qld:
                qld[nx] = (f"wb{nx % 2}", S_.dma("gp", WB[nx % 2][:], wq3[nx],
                           [("pe", tA if nx % 2 == 0 else tB)], sem=f"wb{nx % 2}"))

        def f_qn(s, tB=tB):
            s.activation(QNB[0][:, 0:512], PS[0][:, 0:512], AF.Copy)
            s.activation(QNB[0][:, 512:1024], PS[1][:, 0:512], AF.Copy)
            s.wait_ge(S_.sems["pe"], tB)
            s.activation(QNB[1][:, 0:512], PS[2][:, 0:512], AF.Copy)
            return s.activation(QNB[1][:, 512:1024], PS[3][:, 0:512], AF.Copy)
        wq_ = [("pe", tA)]
        if comb_toks:
            wq_.append(("sv", comb_toks[-1]))
        qevac[iA] = qevac[iB] = S_.step("ss", f_qn, wq_, ("ss", 1))

        tvthA = transpose16(S_, VFH[iA % 2], 0,
                            lambda r, iA=iA: VTH[iA % 2][:, r * 8 * P:(r + 1) * 8 * P],
                            [vfld[iA]] + ([("pe", zav1[(iA - 2, 1)])] if iA >= 2 else []),
                            ident, nt=SL // P)

        wqp = [("qpw", qpe_ld[ip])]
        if comb_toks:
            wqp.append(("sv", comb_toks[-1]))
        tqp = fm_proj_chunk(S_, QPW, XN, 0, PS[4], PS[5], wqp)
        if ip + 1 < 8:
            qpe_ld[ip + 1] = S_.dma("gp", QPW[:], wq3[17 + ip], [("pe", tqp)], sem="qpw")
        t_qev = S_.step("sv", lambda v: (v.tensor_copy(scrA[:, 0:512], PS[4][:, 0:512]),
                                         v.tensor_copy(scrA[:, 512:1024], PS[5][:, 0:512]))[-1],
                        [("pe", tqp)], ("sv", 1))
        t_qpb = S_.step("ss", lambda s: s.copy(QPEB[:], scrA[:]), [("sv", t_qev)], ("ss", 1))
        t_rotq = S_.step("pe", lambda t: (t.matmul(PS[4][:, 0:512], R128B[:], QPEB[:, 0:512],
                                                   start=True, stop=True),
                                          t.matmul(PS[5][:, 0:512], R128B[:], QPEB[:, 512:1024],
                                                   start=True, stop=True))[-1],
                         [("ss", t_qpb)], ("pe", 1))

        def f_qper(v):
            v.tensor_tensor(scrB[:], scrA[:], cos1[:], ALU.mult)
            v.tensor_tensor(scrC[:, 0:512], PS[4][:, 0:512], sin1[:, 0:512], ALU.mult)
            v.tensor_tensor(scrC[:, 512:1024], PS[5][:, 0:512], sin1[:, 512:1024], ALU.mult)
            return v.tensor_tensor(QPE[:], scrB[:], scrC[:], ALU.add)
        t_qpe = S_.step("sv", f_qper, [("pe", t_rotq)], ("sv", 1))
        last_qpe[0] = t_qpe

        for which in range(2):
            ii = 2 * ip + which
            kbuf = KHB[ii % 2]
            if which == 0:
                t_vth = tvthA
            else:
                t_vth = transpose16(S_, VFH[ii % 2], 0,
                                    lambda r, ii=ii: VTH[ii % 2][:, r * 8 * P:(r + 1) * 8 * P],
                                    [vfld[ii]] + ([("pe", zav1[(ii - 2, 1)])] if ii >= 2 else []),
                                    ident, nt=SL // P)
            for sig in range(2):
                par = s1[0] % 2
                ntl = 12 if sig == 0 else C
                po = 0 if sig == 0 else 12 * 512
                mtiles = range(0, 12) if sig == 0 else range(4, 16)
                ex_toks = []
                sc_toks = []
                for jt in range(ntl):
                    wsc = []
                    if len(sc_toks) >= 2:
                        wsc.append(("ss", ex_toks[-2]))
                    elif s1[0] >= 2:
                        wsc.append(("ss", ex1[s1[0] - 2]))
                    else:
                        wsc.append(("ss", S_.cnt["ss"]))
                    if jt == 0:
                        wsc += [("sv", t_qpe), ("kpl", t_kpef), khld[ii]]

                    def f_sc(t, jt=jt, kbuf=kbuf, which=which, sig=sig, par=par):
                        pp = PS[2 * par + jt % 2]
                        t.matmul(pp[:, 0:512], kbuf[:, jt * P:(jt + 1) * P],
                                 QNB[which][:, sig * 512:(sig + 1) * 512],
                                 start=True, stop=False)
                        return t.matmul(pp[:, 0:512],
                                        KPEF[64 * which:64 * which + 64, jt * P:(jt + 1) * P],
                                        QPE[64 * which:64 * which + 64, sig * 512:(sig + 1) * 512],
                                        start=False, stop=True)
                    tsc = S_.step("pe", f_sc, wsc, ("pe", 1))
                    sc_toks.append(tsc)
                    wex = [("pe", tsc)]
                    ex_toks.append(S_.step("ss", lambda s, jt=jt, po=po, par=par: s.activation(
                        PTB[:, po + jt * 512:po + (jt + 1) * 512],
                        PS[2 * par + jt % 2][:, 0:512], AF.Exp,
                        bias=shiftT[:], scale=1.0 / math.sqrt(192.0)), wex, ("ss", 1)))

                def f_msk(v, po=po, sig=sig, mtiles=mtiles):
                    r = None
                    for jt in mtiles:
                        mc = (jt if sig == 0 else 8 + jt) * 512
                        r = v.tensor_tensor(PTB[:, po + jt * 512:po + (jt + 1) * 512],
                                            PTB[:, po + jt * 512:po + (jt + 1) * 512],
                                            MSKF[:, mc:mc + 512], ALU.mult)
                    return r
                t_msk = S_.step("sv", f_msk, [("ss", ex_toks[-1]), ("once", t_mskf)], ("sv", 1))
                ex1[s1[0]] = ex_toks[-1]
                if pend1:
                    l1_tail(pend1.pop())
                pend1.append(dict(ii=ii, sig=sig, par=par, ntl=ntl, msk=t_msk, vth=t_vth))
                s1[0] += 1
            kfree[ii] = sc_toks[-1]
            nx = ii + 2
            if nx < 16 and nx not in khld:
                khld[nx] = (f"kh{nx % 2}", S_.dma("gp", KHB[nx % 2][:],
                            gk[nx * P:(nx + 1) * P, 0:SL],
                            [("pe", kfree[ii])], sem=f"kh{nx % 2}"))
                vfld[nx] = (f"vf{nx % 2}", S_.dma("gp", VFH[nx % 2][:],
                            gv[nx * P:(nx + 1) * P, 0:SL],
                            [("sv", t_vth)], sem=f"vf{nx % 2}"))
    while pend1:
        l1_tail(pend1.pop())

    if dbg:
        S_.dma("gp", dbg["d_comb1"].rearrange("(c p) t -> p c t", p=P),
               COMB[:].rearrange("p (c t) -> p c t", t=T), [("sv", comb_toks[-1])], sem="dbg")

    old = {0: ("wb0", S_.dma("gp", WB[0][:], wo3[0], [("pe", tq_last[0])], sem="wb0"))}
    old[1] = ("wb1", S_.dma("gp", WB[1][:], wo3[1], [("pe", tq_last[1])], sem="wb1"))
    xa = io["xacc"].ap()
    xch = {0: S_.dma("sy", XCH[0][:], io["x_in"][0:P, :], sem="xch"),
           1: S_.dma("sy", XCH[1][:], io["x_in"][P:2 * P, :], sem="xch")}
    xadd = {}
    x1st = {}
    for j in range(C):
        pa, pb = (PS[0], PS[1]) if j % 2 == 0 else (PS[2], PS[3])
        w = [old[j], ("sv", comb_toks[-1])] + ([("sv", xadd[j - 2])] if j >= 2
                                               else [("ss", S_.cnt["ss"])])
        tj = fm_proj_chunk(S_, WB[j % 2], COMB, 0, pa, pb, w)
        nx = j + 2
        if nx < C:
            old[nx] = (f"wb{nx % 2}", S_.dma("gp", WB[nx % 2][:], wo3[nx], [("pe", tj)],
                       sem=f"wb{nx % 2}"))
        def f_xadd(v, j=j, pa=pa, pb=pb):
            xc = XCH[j % 2]
            v.tensor_tensor(xc[:, 0:512], xc[:, 0:512], pa[:, 0:512], ALU.add)
            return v.tensor_tensor(xc[:, 512:1024], xc[:, 512:1024], pb[:, 0:512], ALU.add)
        xadd[j] = S_.step("sv", f_xadd, [("pe", tj), ("xch", xch[j])], ("sv", 1))
        x1st[j] = S_.dma("sy", xa[j * P:(j + 1) * P, :], XCH[j % 2][:], [("sv", xadd[j])],
                         sem="x2st")
        nx = j + 2
        if nx < C:
            xch[nx] = S_.dma("sy", XCH[nx % 2][:], io["x_in"][nx * P:(nx + 1) * P, :],
                             [("x2st", x1st[j])], sem="xch")
    scope2.close()
    if stop_after == "wo":
        allw = [(nm, S_.cnt[nm]) for nm in S_.cnt
                if nm not in ("pe", "ss", "sv", "sg") and S_.cnt[nm] > 0]
        S_.step("sy", lambda sy: sy.nop(), allw)
        return

    scope3 = ExitStack()
    XG = sb("XG", [P, C * T], BF16, scope3)
    ABUF = sb("ABUF", [P, IC * T], BF16, scope3)
    SBC = sb("SBC", [P, T], F32, scope3)
    GW = [sb("GWa", [P, C * P], BF16, scope3), sb("GWb", [P, C * P], BF16, scope3)]
    UW = [sb("UWa", [P, C * P], BF16, scope3), sb("UWb", [P, C * P], BF16, scope3)]
    DW = [sb("DWa", [P, IC * P], BF16, scope3), sb("DWb", [P, IC * P], BF16, scope3)]
    XC2 = [sb("XC2a", [P, T], F32, scope3), sb("XC2b", [P, T], F32, scope3)]
    MSC = sb("MSC", [P, T], F32, scope3)
    MASKG = sb("MASKG", [8, D], BF16, scope3)
    SGB = sb("SGB", [8, T], BF16, scope3)
    FGB = sb("FGB", [8, T], BF16, scope3)

    x2st_total = x1st[C - 1]
    sgd = io["softg_d"].ap()
    fgd = io["flatg_d"].ap()
    mup = io["maskup"]
    gld = {}
    uld = {}
    dld = {}
    rmw_prev = None
    for g in range(G8):
        alias_w = [("x2st", x2st_total), ("sgst", sg_st)] if g == 0 else []
        t_sgl = S_.dma("gp", SGB[:], sgd[g * 8:(g + 1) * 8, :],
                       alias_w + ([("pe", S_.cnt["pe"])] if g > 0 else []), sem="sgl")
        t_fgl = S_.dma("gp", FGB[:], fgd[g * 8:(g + 1) * 8, :], sem="sgl")
        t_mgl = S_.dma("gp", MASKG[:], mup[g * 8:(g + 1) * 8, :], sem="sgl")
        def f_sbc(t, g=g):
            t.matmul(PS[6][:, 0:512], ones8[:], FGB[:, 0:512], start=True, stop=True)
            return t.matmul(PS[5][:, 0:512], ones8[:], FGB[:, 512:1024], start=True, stop=True)
        wsb = [("sgl", t_mgl), ("sv", S_.cnt["sv"]), ("ss", S_.cnt["ss"])]
        t_sbc = S_.step("pe", f_sbc, wsb, ("pe", 1))
        t_sbe = S_.step("sv", lambda v: (v.tensor_copy(SBC[:, 0:512], PS[6][:, 0:512]),
                                         v.tensor_copy(SBC[:, 512:1024], PS[5][:, 0:512]))[-1],
                        [("pe", t_sbc)], ("sv", 1))
        xg_toks = {}
        for j in range(C):
            pda = PS[0] if j % 2 == 0 else PS[2]
            pdb = PS[1] if j % 2 == 0 else PS[3]
            def f_dl(t, g=g, j=j, pda=pda, pdb=pdb):
                mm = None
                for s2, pp in ((0, pda), (1, pdb)):
                    mm = t.matmul(pp[:, 0:512], MASKG[:, j * P:(j + 1) * P],
                                  SGB[:, s2 * 512:(s2 + 1) * 512],
                                  start=True, stop=True)
                return mm
            wdl = [("sv", xg_toks[j - 2])] if j >= 2 else [("sv", t_sbe)]
            tdl = S_.step("pe", f_dl, wdl, ("pe", 1))
            def f_xg(v, j=j, pda=pda, pdb=pdb):
                v.tensor_tensor(XG[:, j * T:j * T + 512], pda[:, 0:512],
                                XN[:, j * T:j * T + 512], ALU.add)
                return v.tensor_tensor(XG[:, j * T + 512:(j + 1) * T], pdb[:, 0:512],
                                       XN[:, j * T + 512:(j + 1) * T], ALU.add)
            xg_toks[j] = S_.step("sv", f_xg, [("pe", tdl)], ("sv", 1))
        t_xg = xg_toks[C - 1]
        if stop_after == "moeD":
            break

        a_toks = {}
        for it in range(IC):
            key = (g, it)
            if key not in gld:
                gld[key] = S_.dma("gp", GW[it % 2][:], gpw4[g, it],
                                  [("pe", S_.cnt["pe"])], sem=f"gw{it % 2}")
                uld[key] = S_.dma("gp", UW[it % 2][:], upw4[g, it], sem=f"uw{it % 2}")
            def f_g(t, g=g, it=it):
                mm = None
                for c in range(C):
                    for s2 in range(2):
                        mm = t.matmul(PS[s2][:, 0:512], GW[it % 2][:, c * P:(c + 1) * P],
                                      XG[:, c * T + s2 * 512: c * T + (s2 + 1) * 512],
                                      start=(c == 0), stop=(c == C - 1))
                return mm
            wg = [(f"gw{it % 2}", gld[key]), ("sv", t_xg)]
            if it >= 1:
                wg.append(("ss", tsl))
            tg = S_.step("pe", f_g, wg, ("pe", 1))
            def f_u(t, g=g, it=it):
                mm = None
                for c in range(C):
                    for s2 in range(2):
                        mm = t.matmul(PS[2 + s2][:, 0:512], UW[it % 2][:, c * P:(c + 1) * P],
                                      XG[:, c * T + s2 * 512: c * T + (s2 + 1) * 512],
                                      start=(c == 0), stop=(c == C - 1))
                return mm
            wu = [(f"uw{it % 2}", uld[key])]
            wu.append(("sv", a_toks[it - 1]) if it >= 1 else ("sv", t_xg))
            tu = S_.step("pe", f_u, wu, ("pe", 1))
            nkey = (g, it + 2) if it + 2 < IC else ((g + 1, it + 2 - IC) if g + 1 < G8 else None)
            if nkey is not None and nkey not in gld:
                gld[nkey] = S_.dma("gp", GW[nkey[1] % 2][:], gpw4[nkey[0], nkey[1]],
                                   [("pe", tu)], sem=f"gw{nkey[1] % 2}")
                uld[nkey] = S_.dma("gp", UW[nkey[1] % 2][:], upw4[nkey[0], nkey[1]],
                                   sem=f"uw{nkey[1] % 2}")
            def f_silu(s):
                s.activation(MSC[:, 0:512], PS[0][:, 0:512], AF.Silu)
                return s.activation(MSC[:, 512:1024], PS[1][:, 0:512], AF.Silu)
            wsl = [("pe", tg)] + ([("sv", a_toks[it - 1])] if it >= 1 else [])
            tsl = S_.step("ss", f_silu, wsl, ("ss", 1))
            def f_amul(v, it=it):
                v.tensor_tensor(scrA[:, 0:512], MSC[:, 0:512], PS[2][:, 0:512], ALU.mult)
                v.tensor_tensor(scrA[:, 512:1024], MSC[:, 512:1024], PS[3][:, 0:512], ALU.mult)
                v.tensor_tensor(ABUF[:, it * T:it * T + 512], scrA[:, 0:512],
                                SBC[:, 0:512], ALU.mult)
                return v.tensor_tensor(ABUF[:, it * T + 512:(it + 1) * T], scrA[:, 512:1024],
                                       SBC[:, 512:1024], ALU.mult)
            a_toks[it] = S_.step("sv", f_amul, [("ss", tsl), ("pe", tu)], ("sv", 1))

        if stop_after == "moeA":
            break
        rmw_ev = {}
        rmw_st = {}
        for j in range(C):
            pda = PS[0] if j % 2 == 0 else PS[2]
            pdb = PS[1] if j % 2 == 0 else PS[3]
            psem = f"x2{'ab'[j % 2]}"
            key = (g, j)
            if key not in dld:
                dld[key] = S_.dma("gp", DW[j % 2][:], dnw4[g, j],
                                  [("pe", S_.cnt["pe"])], sem=f"dw{j % 2}")
            if j >= 2:
                wxl = [(psem, rmw_st[j - 2])]
            elif rmw_prev:
                wxl = [(psem, rmw_prev[C - 2 + j])]
            else:
                wxl = [("x2st", x2st_total)]
            t_xl = S_.dma("sy", XC2[j % 2][:], xa[j * P:(j + 1) * P, :], wxl, sem="xch")
            def f_dn(t, g=g, j=j, pda=pda, pdb=pdb):
                mm = None
                for ic in range(IC):
                    for s2, pp in ((0, pda), (1, pdb)):
                        mm = t.matmul(pp[:, 0:512], DW[j % 2][:, ic * P:(ic + 1) * P],
                                      ABUF[:, ic * T + s2 * 512: ic * T + (s2 + 1) * 512],
                                      start=(ic == 0), stop=(ic == IC - 1))
                return mm
            wdn = [(f"dw{j % 2}", dld[key]), ("sv", a_toks[IC - 1])]
            if j >= 2:
                wdn.append(("sv", rmw_ev[j - 2]))
            elif j == 0:
                wdn.append(("ss", tsl))
            tdn = S_.step("pe", f_dn, wdn, ("pe", 1))
            nkey = (g, j + 2) if j + 2 < C else ((g + 1, j + 2 - C) if g + 1 < G8 else None)
            if nkey is not None and nkey not in dld:
                dld[nkey] = S_.dma("gp", DW[nkey[1] % 2][:], dnw4[nkey[0], nkey[1]],
                                   [("pe", tdn)], sem=f"dw{nkey[1] % 2}")
            def f_rmw(v, j=j, pda=pda, pdb=pdb):
                xc = XC2[j % 2]
                v.tensor_tensor(xc[:, 0:512], xc[:, 0:512], pda[:, 0:512], ALU.add)
                return v.tensor_tensor(xc[:, 512:1024], xc[:, 512:1024], pdb[:, 0:512], ALU.add)
            rmw_ev[j] = S_.step("sv", f_rmw, [("pe", tdn), ("xch", t_xl)], ("sv", 1))
            target = io["x2_out"] if g == G8 - 1 else xa
            rmw_st[j] = S_.dma("sy", target[j * P:(j + 1) * P, :], XC2[j % 2][:],
                               [("sv", rmw_ev[j])], sem=psem)
        rmw_prev = rmw_st
        if stop_after == "moe1":
            break
    scope3.close()
    allw = [(nm, S_.cnt[nm]) for nm in S_.cnt
            if nm not in ("pe", "ss", "sv", "sg") and S_.cnt[nm] > 0]
    S_.step("sy", lambda sy: sy.nop(), allw)


def build_l1(dbg=False, stop_after=None):
    nc = bass.Bass("TRN2", target_bir_lowering=False)
    io = {}
    io["x_in"] = nc.dram_tensor("x1_fm", [D, T], F32, kind="ExternalInput").ap()
    io["wq1"] = nc.dram_tensor("wq1_blk", [24 * P, D], F32, kind="ExternalInput").ap()
    io["wkva"] = nc.dram_tensor("wkva_blk", [4 * P, D], F32, kind="ExternalInput").ap()
    io["wkva_pe"] = nc.dram_tensor("wkva_pe", [P, C * 64], F32, kind="ExternalInput").ap()
    io["kvnsw"] = nc.dram_tensor("kvnsw", [P, KC4], F32, kind="ExternalInput").ap()
    io["wkvb_k"] = nc.dram_tensor("wkvb_k", [C * P, KVR], F32, kind="ExternalInput").ap()
    io["wkvb_v"] = nc.dram_tensor("wkvb_v", [C * P, KVR], F32, kind="ExternalInput").ap()
    io["wo1"] = nc.dram_tensor("wo1_blk", [C * P, D], F32, kind="ExternalInput").ap()
    io["gatew"] = nc.dram_tensor("gatew_pm", [P, C * E64], F32, kind="ExternalInput").ap()
    io["maskup"] = nc.dram_tensor("maskup", [G8 * 8, D], F32, kind="ExternalInput").ap()
    io["gpw"] = nc.dram_tensor("gpw_blk", [G8 * IC * P, D], F32, kind="ExternalInput").ap()
    io["upw"] = nc.dram_tensor("upw_blk", [G8 * IC * P, D], F32, kind="ExternalInput").ap()
    io["dnw"] = nc.dram_tensor("dnw_blk", [G8 * C * P, I14], F32, kind="ExternalInput").ap()
    io["inv_row"] = nc.dram_tensor("inv_row", [1, 64], I32, kind="ExternalInput").ap()
    io["iota64"] = nc.dram_tensor("iota64", [64, 1], F32, kind="ExternalInput").ap()
    io["cos1"] = nc.dram_tensor("cos1", [P, T], F32, kind="ExternalInput").ap()
    io["sin1"] = nc.dram_tensor("sin1", [P, T], F32, kind="ExternalInput").ap()
    io["r128"] = nc.dram_tensor("r128", [P, P], F32, kind="ExternalInput").ap()
    io["masks"] = nc.dram_tensor("masks", [2 * C * P, 512], F32, kind="ExternalInput").ap()
    io["ident_in"] = nc.dram_tensor("ident_in", [P, P], F32, kind="ExternalInput").ap()
    io["x2_out"] = nc.dram_tensor("x2_fm", [D, T], F32, kind="ExternalOutput").ap()
    io["ctb_ckv"] = nc.dram_tensor("ctb_ckv", [KVR, T], BF16, kind="Internal")
    io["ctb_kpe"] = nc.dram_tensor("ctb_kpe", [64, T], BF16, kind="Internal")
    io["gth_ckv"] = nc.dram_tensor("gth_ckv", [2 * KVR, T], BF16, kind="Internal")
    io["gth_kpe"] = nc.dram_tensor("gth_kpe", [128, T], BF16, kind="Internal")
    io["kn_dram"] = nc.dram_tensor("kn_dram", [D, S], BF16, kind="Internal")
    io["vn_dram"] = nc.dram_tensor("vn_dram", [D, S], BF16, kind="Internal")
    io["xacc"] = nc.dram_tensor("xacc", [D, T], F32, kind="Internal")
    io["softg_d"] = nc.dram_tensor("softg_d", [64, T], BF16, kind="Internal")
    io["flatg_d"] = nc.dram_tensor("flatg_d", [64, T], BF16, kind="Internal")
    dbgd = {}
    if dbg:
        for nm in ("d_xn1", "d_comb1", "d_soft", "d_we", "d_flat"):
            shp = [D, T] if nm in ("d_xn1", "d_comb1") else [P, 512]
            dbgd[nm] = nc.dram_tensor(nm, shp, F32, kind="ExternalOutput").ap()
    with ExitStack() as ctx:
        S_ = Sched(nc, ctx)
        emit_l1(S_, io, dbgd, stop_after=stop_after)
        blk = ctx.enter_context(nc.Block())
        S_.emit(blk)
    return nc


def build_fused(dbg=False):
    nc = bass.Bass("TRN2", target_bir_lowering=False)
    io0 = {}
    io0["x_in"] = nc.dram_tensor("x_fm", [D, T], F32, kind="ExternalInput").ap()
    io0["wq"] = nc.dram_tensor("wq_blk", [C * P, D], F32, kind="ExternalInput").ap()
    io0["wk"] = nc.dram_tensor("wk_blk", [C * P, D], F32, kind="ExternalInput").ap()
    io0["wv"] = nc.dram_tensor("wv_blk", [C * P, D], F32, kind="ExternalInput").ap()
    io0["wo"] = nc.dram_tensor("wo_blk", [C * P, D], F32, kind="ExternalInput").ap()
    io0["wf"] = nc.dram_tensor("wf_rows", [D, D], F32, kind="ExternalInput").ap()
    io0["bias_sw"] = nc.dram_tensor("bias_sw", [P, C], F32, kind="ExternalInput").ap()
    io0["gate_in"] = nc.dram_tensor("gate_row", [1, NH], F32, kind="ExternalInput").ap()
    io0["cos0"] = nc.dram_tensor("cos0", [8 * P, T], F32, kind="ExternalInput").ap()
    io0["sin0"] = nc.dram_tensor("sin0", [8 * P, T], F32, kind="ExternalInput").ap()
    io0["ident_in"] = nc.dram_tensor("ident_in", [P, P], F32, kind="ExternalInput").ap()
    x1d = nc.dram_tensor("x1_mid", [D, T], F32, kind="Internal")
    io0["x1_out"] = x1d.ap()
    io0["spill"] = nc.dram_tensor("spill_x", [D, T], F32, kind="Internal")
    io0["ctb_k"] = nc.dram_tensor("ctb_k", [D, T], BF16, kind="Internal")
    io0["ctb_v"] = nc.dram_tensor("ctb_v", [D, T], BF16, kind="Internal")
    io0["gth_k"] = nc.dram_tensor("gth_k", [2 * D, T], BF16, kind="Internal")
    io0["gth_v"] = nc.dram_tensor("gth_v", [2 * D, T], BF16, kind="Internal")

    io1 = {}
    io1["x_in"] = x1d.ap()
    io1["wq1"] = nc.dram_tensor("wq1_blk", [24 * P, D], F32, kind="ExternalInput").ap()
    io1["wkva"] = nc.dram_tensor("wkva_blk", [4 * P, D], F32, kind="ExternalInput").ap()
    io1["wkva_pe"] = nc.dram_tensor("wkva_pe", [P, C * 64], F32, kind="ExternalInput").ap()
    io1["kvnsw"] = nc.dram_tensor("kvnsw", [P, KC4], F32, kind="ExternalInput").ap()
    io1["wkvb_k"] = nc.dram_tensor("wkvb_k", [C * P, KVR], F32, kind="ExternalInput").ap()
    io1["wkvb_v"] = nc.dram_tensor("wkvb_v", [C * P, KVR], F32, kind="ExternalInput").ap()
    io1["wo1"] = nc.dram_tensor("wo1_blk", [C * P, D], F32, kind="ExternalInput").ap()
    io1["gatew"] = nc.dram_tensor("gatew_pm", [P, C * E64], F32, kind="ExternalInput").ap()
    io1["maskup"] = nc.dram_tensor("maskup", [G8 * 8, D], F32, kind="ExternalInput").ap()
    io1["gpw"] = nc.dram_tensor("gpw_blk", [G8 * IC * P, D], F32, kind="ExternalInput").ap()
    io1["upw"] = nc.dram_tensor("upw_blk", [G8 * IC * P, D], F32, kind="ExternalInput").ap()
    io1["dnw"] = nc.dram_tensor("dnw_blk", [G8 * C * P, I14], F32, kind="ExternalInput").ap()
    io1["inv_row"] = nc.dram_tensor("inv_row", [1, 64], I32, kind="ExternalInput").ap()
    io1["iota64"] = nc.dram_tensor("iota64", [64, 1], F32, kind="ExternalInput").ap()
    io1["cos1"] = nc.dram_tensor("cos1", [P, T], F32, kind="ExternalInput").ap()
    io1["sin1"] = nc.dram_tensor("sin1", [P, T], F32, kind="ExternalInput").ap()
    io1["r128"] = nc.dram_tensor("r128", [P, P], F32, kind="ExternalInput").ap()
    io1["masks"] = nc.dram_tensor("masks", [2 * C * P, 512], F32, kind="ExternalInput").ap()
    io1["ident_in"] = io0["ident_in"]
    io1["x2_out"] = nc.dram_tensor("x2_fm", [D, T], F32, kind="ExternalOutput").ap()
    io1["ctb_ckv"] = nc.dram_tensor("ctb_ckv", [KVR, T], BF16, kind="Internal")
    io1["ctb_kpe"] = nc.dram_tensor("ctb_kpe", [64, T], BF16, kind="Internal")
    io1["gth_ckv"] = nc.dram_tensor("gth_ckv", [2 * KVR, T], BF16, kind="Internal")
    io1["gth_kpe"] = nc.dram_tensor("gth_kpe", [128, T], BF16, kind="Internal")
    io1["kn_dram"] = nc.dram_tensor("kn_dram", [D, S], BF16, kind="Internal")
    io1["vn_dram"] = nc.dram_tensor("vn_dram", [D, S], BF16, kind="Internal")
    io1["xacc"] = nc.dram_tensor("xacc", [D, T], F32, kind="Internal")
    io1["softg_d"] = nc.dram_tensor("softg_d", [64, T], BF16, kind="Internal")
    io1["flatg_d"] = nc.dram_tensor("flatg_d", [64, T], BF16, kind="Internal")

    with ExitStack() as ctx:
        S_ = Sched(nc, ctx)
        l0scope = ExitStack()
        S_.default_scope = l0scope
        emit_l0(S_, io0, {})
        l0scope.close()
        S_.default_scope = None
        S_.barrier()
        emit_l1(S_, io1, {})
        blk = ctx.enter_context(nc.Block())
        S_.emit(blk)
    return nc


def kernel(**inputs):
    inputs = {k: np.asarray(v) for k, v in inputs.items()}
    x0 = np.asarray(inputs["L0_attn_hidden_states"])
    com0 = _prep_l0(inputs)
    com1 = _prep_l1(inputs)
    t0 = [_l0_tables(0), _l0_tables(1)]
    t1 = _l1_tables()
    masks = [_full_masks(0), _full_masks(1)]
    nc = build_fused()
    in_maps = []
    for core in range(8):
        b, h = core // 2, core % 2
        m = dict(com0)
        m.update(com1)
        m["x_fm"] = np.ascontiguousarray(x0[b, h * T:(h + 1) * T, :].T)
        m["cos0"], m["sin0"] = t0[h]
        m["cos1"], m["sin1"] = t1[h]
        m["masks"] = masks[h]
        in_maps.append(m)
    r = run_bass_kernel_spmd(nc, in_maps, core_ids=list(range(8)))
    out = np.zeros((4, S, D), np.float32)
    for core in range(8):
        b, h = core // 2, core % 2
        out[b, h * T:(h + 1) * T, :] = r.results[core]["x2_fm"].T
    return out


def kernel_unfused(**inputs):
    inputs = {k: np.asarray(v) for k, v in inputs.items()}
    res0 = run_l0(inputs)
    x1 = [res0.results[c]["x1_fm"] for c in range(8)]
    res1 = run_l1(inputs, x1)
    out = np.zeros((4, S, D), np.float32)
    for core in range(8):
        b, h = core // 2, core % 2
        out[b, h * T:(h + 1) * T, :] = res1[core]["x2_fm"].T
    return out


def _prep_l0(ins):
    out = {}
    out["wq_blk"] = _blk(np.ascontiguousarray(ins["L0_attn_Wq"].T))
    out["wk_blk"] = _blk(np.ascontiguousarray(ins["L0_attn_Wk"].T))
    out["wv_blk"] = _blk(np.ascontiguousarray(ins["L0_attn_Wv"].T))
    out["wo_blk"] = _blk(np.ascontiguousarray(ins["L0_attn_Wo"].T))
    out["wf_rows"] = np.ascontiguousarray(ins["L0_ffn_weight"])
    out["bias_sw"] = np.ascontiguousarray(ins["L0_ffn_bias"].reshape(C, P).T)
    out["gate_row"] = np.ascontiguousarray(ins["L0_attn_gate"].reshape(1, NH))
    out["ident_in"] = np.eye(P, dtype=np.float32)
    return out


def _l1_tables():
    tabs = []
    for h in range(2):
        pos = np.arange(h * T, (h + 1) * T, dtype=np.float64)
        fr = np.arange(0, 32, dtype=np.float64) / 32.0
        inv = 1.0 / (10000.0 ** fr)
        ang = pos[None, :] * inv[:, None]
        cos = np.cos(ang)
        sin = np.sin(ang)
        c64 = np.concatenate([cos, cos], 0)
        s64 = np.concatenate([sin, sin], 0)
        tabs.append((np.concatenate([c64, c64], 0).astype(np.float32),
                     np.concatenate([s64, s64], 0).astype(np.float32)))
    return tabs


def _rot_mat64():
    R = np.zeros((P, P), dtype=np.float32)
    for p in range(P):
        blk = (p // 64) * 64
        off = p - blk
        if off < 32:
            R[blk + off + 32, p] = -1.0
        else:
            R[blk + off - 32, p] = 1.0
    return R


def _full_masks(h):
    out = []
    for sig in range(2):
        qbase = h * T + sig * 512
        for jt in range(C):
            pp = 128 * jt + np.arange(P)[:, None]
            jj = qbase + np.arange(512)[None, :]
            out.append((pp <= jj).astype(np.float32))
    return np.concatenate(out, axis=0)


def _prep_l1(ins):
    out = {}
    wq = np.asarray(ins["L1_attn_wq"])
    qc = wq.reshape(D, NH, 192)
    nope = qc[:, :, 0:128].reshape(D, NH * 128)
    pe = qc[:, :, 128:192].reshape(D, NH * 64)
    out["wq1_blk"] = _blk(np.concatenate([nope, pe], axis=1))
    wkva = np.asarray(ins["L1_attn_w_kv_a"])
    out["wkva_blk"] = _blk(np.ascontiguousarray(wkva[:, 0:512]))
    out["wkva_pe"] = _pm(np.ascontiguousarray(wkva[:, 512:576]))
    out["kvnsw"] = np.ascontiguousarray(ins["L1_attn_kv_norm_scale"].reshape(KC4, P).T)
    wkvb = np.asarray(ins["L1_attn_w_kv_b"])
    kvc = wkvb.reshape(KVR, NH, 256)
    kn = np.ascontiguousarray(kvc[:, :, 0:128].reshape(KVR, NH * 128))
    vv = np.ascontiguousarray(kvc[:, :, 128:256].reshape(KVR, NH * 128))
    out["wkvb_k"] = _blk(kn)
    out["wkvb_v"] = _blk(vv)
    out["wo1_blk"] = _blk(np.asarray(ins["L1_attn_wo"]))
    out["gatew_pm"] = _pm(np.ascontiguousarray(ins["L1_ffn_gate_w"].T))
    out["maskup"] = np.ascontiguousarray(
        np.asarray(ins["L1_ffn_mask_up_w"]).transpose(0, 2, 1).reshape(G8 * 8, D))
    gpw = np.asarray(ins["L1_ffn_gate_proj_w"])
    upw = np.asarray(ins["L1_ffn_up_proj_w"])
    dnw = np.asarray(ins["L1_ffn_down_proj_w"])

    def blk_gu(w):
        wt = w.transpose(0, 2, 1)
        b = wt.reshape(G8, C, P, IC, P).transpose(0, 3, 2, 1, 4)
        return np.ascontiguousarray(b.reshape(G8 * IC * P, D))

    def blk_dn(w):
        wt = w.transpose(0, 2, 1)
        b = wt.reshape(G8, IC, P, C, P).transpose(0, 3, 2, 1, 4)
        return np.ascontiguousarray(b.reshape(G8 * C * P, I14))

    out["gpw_blk"] = blk_gu(gpw)
    out["upw_blk"] = blk_gu(upw)
    out["dnw_blk"] = blk_dn(dnw)
    out["inv_row"] = np.ascontiguousarray(
        np.asarray(ins["L1_ffn_inv_map"]).astype(np.int32).reshape(1, 64))
    out["iota64"] = np.arange(64, dtype=np.float32).reshape(64, 1)
    out["r128"] = _rot_mat64()
    out["ident_in"] = np.eye(P, dtype=np.float32)
    return out


def run_l0(inputs, dbg=False):
    x0 = np.asarray(inputs["L0_attn_hidden_states"])
    com = _prep_l0(inputs)
    tabs = [_l0_tables(0), _l0_tables(1)]
    nc0 = build_l0(dbg=dbg)
    in_maps = []
    for core in range(8):
        b, h = core // 2, core % 2
        m = dict(com)
        m["x_fm"] = np.ascontiguousarray(x0[b, h * T:(h + 1) * T, :].T)
        m["cos0"], m["sin0"] = tabs[h]
        in_maps.append(m)
    return run_bass_kernel_spmd(nc0, in_maps, core_ids=list(range(8)))


def run_l1(inputs, x1_percore, dbg=False):
    com = _prep_l1(inputs)
    tabs = _l1_tables()
    masks = [_full_masks(0), _full_masks(1)]
    nc1 = build_l1(dbg=dbg)
    in_maps = []
    for core in range(8):
        b, h = core // 2, core % 2
        m = dict(com)
        m["x1_fm"] = np.ascontiguousarray(x1_percore[core])
        m["cos1"], m["sin1"] = tabs[h]
        m["masks"] = masks[h]
        in_maps.append(m)
    r = run_bass_kernel_spmd(nc1, in_maps, core_ids=list(range(8)))
    return list(r.results)


